# version 15
# baseline (speedup 1.0000x reference)
import math
import numpy as np
import ml_dtypes

import concourse.bass as bass
import concourse.mybir as mybir
from concourse.bass_utils import run_bass_kernel_spmd

F32 = mybir.dt.float32
BF16 = mybir.dt.bfloat16
I8 = mybir.dt.int8
AF = mybir.ActivationFunctionType
ALU = mybir.AluOpType
AX = mybir.AxisListType

NCORE = 8
T = 1024
D = 2048
NDC = D // 128
H = 16
HD = 128
C = 128
NTC = T // C
FF = 5632
NHC = FF // 128
INTER = 8192
NOC = INTER // 128
SEQ = 8192

MB = 8192
MARKS = None


def mk(b, name):
    if MARKS is not None:
        MARKS.append((name, {e: len(b.rec[e]) for e in ENGS}))


def _f32(x):
    return np.asarray(x, np.float32)


def _host_const(core):
    f32 = np.float32
    t0 = core * T
    half = HD // 2
    theta = np.power(f32(10000.0), (f32(-2.0) * np.arange(half, dtype=f32) / f32(HD))).astype(f32)
    pos = np.arange(t0, t0 + T, dtype=f32)
    ang = (pos[:, None] * theta[None, :]).astype(f32)
    cos = np.cos(ang).astype(f32)
    sin = np.sin(ang).astype(f32)
    cosf = np.zeros((128, T), f32)
    sinf = np.zeros((128, T), f32)
    cosf[:64, :] = cos.T
    cosf[64:, :] = cos.T
    sinf[:64, :] = -sin.T
    sinf[64:, :] = sin.T
    hh = np.arange(H, dtype=f32)
    slopes = np.power(f32(2.0), (f32(-8.0) * (hh + f32(1.0)) / f32(H))).astype(f32)
    llam = (-slopes).astype(f32)
    i = np.arange(C, dtype=f32)
    diff = i[:, None] - i[None, :]
    dt_tab = np.zeros((128, D), f32)
    dq_tab = np.zeros((128, D), f32)
    dk_tab = np.zeros((128, D), f32)
    lamc_tab = np.zeros((128, D), f32)
    for h in range(H):
        Dh = np.where(diff >= 0, np.exp((llam[h] * diff).astype(f32)).astype(f32), f32(0.0))
        dt_tab[:, h * 128:(h + 1) * 128] = Dh.T
        dq_tab[:, h * 128:(h + 1) * 128] = np.exp((llam[h] * (i + 1)).astype(f32)).astype(f32)[None, :]
        dk_tab[:, h * 128:(h + 1) * 128] = np.exp((llam[h] * (C - 1 - i)).astype(f32)).astype(f32)[:, None]
        lamc_tab[:, h * 128:(h + 1) * 128] = np.exp(f32(llam[h] * C))
    coefc = np.zeros((128, 128), f32)
    for j in range(core):
        p = core - 1 - j
        for h in range(H):
            coefc[:, j * 16 + h] = f32(np.exp(np.float64(llam[h]) * C * NTC * p))
    permpm = np.zeros((128, 128), f32)
    for e in range(128):
        permpm[(e + 64) % 128, e] = 1.0
    idbf = np.eye(128).astype(ml_dtypes.bfloat16)
    mcol = np.zeros((128, 8), f32)
    mcol[:, :core] = 1.0
    onem = 1.0 - mcol
    def _fl(a):
        return np.where(np.abs(a) < 1e-30, f32(0.0), a).astype(f32)

    return dict(cosf=cosf, sinf=sinf, dt_tab=_fl(dt_tab), dq_tab=_fl(dq_tab),
                dk_tab=_fl(dk_tab), lamc_tab=_fl(lamc_tab), coefc=_fl(coefc),
                permpm=permpm, idbf=idbf, mcol=mcol, onem=onem)


def _cols(v, n=128):
    v = np.asarray(v)
    k = v.size // n
    return np.ascontiguousarray(v.reshape(k, n).T)


ENGS = ("t", "v", "s", "p", "y")


DMA_K = 12


class B:
    def __init__(self, nc):
        self.nc = nc
        self.Q = {}
        self.cnt = {}
        self.rec = {e: [] for e in ENGS}
        self.writers = {}
        self.readers = {}
        self.waited = {}
        self.dma_rr = {}

    def sem(self, q):
        if q not in self.Q:
            self.Q[q] = self.nc.alloc_semaphore(name="sm_" + q)
            self.cnt[q] = 0
        return self.Q[q]

    def _need(self, e, deps):
        best = {}
        for (q, v) in deps:
            if e == "t" and q == "t":
                continue
            if v > best.get(q, -1):
                best[q] = v
        for q, v in best.items():
            if self.waited.get((e, q), 0) >= v:
                continue
            self.rec[e].append(("w", q, v))
            self.waited[(e, q)] = v

    def _dep_read(self, keys, eng=None):
        deps = []
        for (sp, lo, hi) in keys:
            for (l, h, q, c) in self.writers.get(sp, ()):
                if l < hi and lo < h:
                    deps.append((q, c))
            if sp.startswith("ps"):
                for (l, h, q, c) in self.readers.get(sp, ()):
                    if l < hi and lo < h and q != eng:
                        deps.append((q, c))
        return deps

    def _dep_write(self, keys):
        deps = []
        for (sp, lo, hi) in keys:
            for lst in (self.writers.get(sp, ()), self.readers.get(sp, ())):
                for (l, h, q, c) in lst:
                    if l < hi and lo < h:
                        deps.append((q, c))
        return deps

    def _register(self, q, c, reads, writes):
        for (sp, lo, hi) in reads:
            self.readers.setdefault(sp, []).append((lo, hi, q, c))
        for (sp, lo, hi) in writes:
            w = self.writers.setdefault(sp, [])
            self.writers[sp] = [e for e in w if not (lo <= e[0] and e[1] <= hi)]
            r = self.readers.get(sp)
            if r:
                self.readers[sp] = [e for e in r if not (lo <= e[0] and e[1] <= hi)]
            self.writers[sp].append((lo, hi, q, c))

    def _emit(self, e, q, inc, fn, reads, writes):
        self.sem(q)
        deps = self._dep_read(reads, e) + self._dep_write(writes)
        self._need(e, deps)
        self.cnt[q] += inc
        self.rec[e].append(("o", fn, q, inc))
        self._register(q, self.cnt[q], reads, writes)

    def op(self, e, fn, reads=(), writes=()):
        self._emit(e, e, 1, fn, reads, writes)

    def dma(self, e, fn, reads=(), writes=()):
        qb = e + "d"
        idx = self.dma_rr.get(qb, 0)
        self.dma_rr[qb] = (idx + 1) % DMA_K
        q = "%s%02d" % (qb, idx)
        self.sem(q)
        deps = self._dep_read(reads, e) + self._dep_write(writes)
        if self.cnt[q]:
            deps.append((q, self.cnt[q]))
        self._need(e, deps)
        self.cnt[q] += 16
        self.rec[e].append(("o", fn, q, 16))
        self._register(q, self.cnt[q], reads, writes)

    def finish(self):
        for q, n in self.cnt.items():
            e = q[0]
            assert e in ENGS
            if self.waited.get((e, q), 0) < n:
                self.rec[e].append(("w", q, n))
                self.waited[(e, q)] = n
        for q, n in self.cnt.items():
            if self.waited.get(("y", q), 0) < n:
                self.rec["y"].append(("w", q, n))
                self.waited[("y", q)] = n

    def _batch(self, items):
        out = []
        run = []

        def flush():
            last = {}
            for idx, it in enumerate(run):
                q = it[2]
                if len(q) == 1:
                    last[q] = idx
            pend = {}
            for idx, it in enumerate(run):
                q = it[2]
                if len(q) == 1:
                    pend[q] = pend.get(q, 0) + it[3]
                    if idx == last[q]:
                        out.append(("o", it[1], q, pend[q]))
                        pend[q] = 0
                    else:
                        out.append(("o", it[1], None, 0))
                else:
                    out.append(it)
            run.clear()

        for it in items:
            if it[0] == "w":
                flush()
                out.append(it)
            else:
                run.append(it)
        flush()
        return out

    def build(self):
        with self.nc.Block() as block:
            emitters = {"t": block.tensor, "v": block.vector, "s": block.scalar,
                        "p": block.gpsimd, "y": block.sync}
            for e in ENGS:
                items = self.rec[e]
                if not items:
                    continue
                items = self._batch(items)

                def body(eng, items=items):
                    for it in items:
                        if it[0] == "w":
                            eng.wait_ge(self.Q[it[1]], it[2])
                        else:
                            inst = it[1](eng)
                            if it[2] is not None:
                                inst.then_inc(self.Q[it[2]], it[3])

                emitters[e](body)


class TV:

    def __init__(self, pools, pool, off, dtype, f, p0=0, p1=128):
        self.pool = pool
        self.off = off
        self.dtype = dtype
        self.f = f
        es = mybir.dt.size(dtype)
        self.es = es
        self._ap = pools[pool].ap()[p0:p1, off:off + f * es].bitcast(dtype)

    def ap(self, lo=0, hi=None):
        hi = self.f if hi is None else hi
        return self._ap[:, lo:hi]

    def k(self, lo=0, hi=None):
        hi = self.f if hi is None else hi
        return (self.pool, self.off + lo * self.es, self.off + hi * self.es)


def PK(name, col_lo, col_hi):
    lo = (col_lo * 4 // 2048) * 2048
    hi = -((-(col_hi * 4)) // 2048) * 2048
    return (name, lo, hi)


def DK(name, lo=0, hi=1 << 62):
    return (name, lo, hi)


class Env:
    pass


def make_env(nc, blocks, fused):
    env = Env()
    env.nc = nc
    env.blocks = blocks
    dr = {}
    env.dr = dr

    def din(name, shape, dtype=F32):
        dr[name] = nc.dram_tensor(name, list(shape), dtype, kind="ExternalInput")
        return dr[name]

    def dint(name, shape, dtype=F32):
        dr[name] = nc.dram_tensor(name, list(shape), dtype, kind="Internal")
        return dr[name]

    def dout(name, shape, dtype=F32):
        dr[name] = nc.dram_tensor(name, list(shape), dtype, kind="ExternalOutput")
        return dr[name]

    env.din, env.dint, env.dout = din, dint, dout

    din("permpm", (128, 128))
    din("idbf", (128, 128), BF16)
    din("ones_col", (128, 1))
    din("ones_col_bf", (128, 1), BF16)
    din("ones_row", (1, 128))
    din("nh_row", (1, 128))
    din("nh_row_bf", (1, 128), BF16)
    din("permpm_bf", (128, 128), BF16)
    din("ccr", (1, 8))
    din("eps1p_col", (128, 1))
    if "l0a" in blocks:
        din("cosf", (128, T)); din("sinf", (128, T))
        din("dt_tab", (128, D)); din("dq_tab", (128, D)); din("dk_tab", (128, D))
        din("lamc_tab", (128, D)); din("coefc", (128, 128))
        din("wq", (D, D)); din("wk", (D, D)); din("wv", (D, D)); din("wo", (D, D))
        din("wg1", (D, 128)); din("wg2", (128, D)); din("nw_cols", (128, NDC))
        dint("qrot_d", (D, T), BF16); dint("krot_d", (D, T), BF16)
        dint("v_d", (T, D), BF16); dint("vdk_d", (T, D), BF16)
        dint("ag1_in", (128, D)); dint("ag1_out", (NCORE * 128, D))
    if "l0f" in blocks:
        din("w1t", (D, FF)); din("w2t", (D, FF)); din("w3t", (FF, D))
    if "l1a" in blocks:
        din("wint", (D, D)); din("wgat", (D, D))
        din("bin_cols", (128, NDC)); din("bgat_cols", (128, NDC))
        din("lam_cols", (128, NDC))
        din("mcol", (128, 8)); din("onem", (128, 8))
        dint("ag2_in", (128, 32)); dint("ag2_out", (NCORE * 128, 32))
    if "l1f" in blocks:
        din("wg8", (D, INTER), I8); din("wu8", (D, INTER), I8); din("wd8", (INTER, D), I8)
        din("sg_cols", (128, NOC)); din("su_cols", (128, NOC)); din("sd_cols", (128, NDC))

    env.xbuf = []
    for i in range(len(blocks) + 1):
        if i == 0:
            env.xbuf.append(din("x_fm", (D, T)))
        elif i == len(blocks):
            env.xbuf.append(dout("y_fm", (D, T)))
        else:
            env.xbuf.append(dint(f"xi{i}", (D, T)))
    if "l1a" in blocks:
        env.x3pre = dint("x3pre", (D, T))

    pools = {}
    for name, nb in (("big1", 65536), ("big2", 32768), ("big3", 32768),
                     ("big4", 32768), ("big5", 38912)):
        pools[name] = nc.alloc_sbuf_tensor(name, [128, nb], mybir.dt.uint8)
    env.pools = pools
    env.tv = lambda pool, off, dtype, f, p0=0, p1=128: TV(pools, pool, off, dtype, f, p0, p1)

    env.sm = nc.alloc_sbuf_tensor("smalls", [128, 6144], mybir.dt.uint8)
    env.sm_off = [0]

    env.sm_cache = {}

    def salloc(dtype, f, name):
        if name in env.sm_cache:
            return env.sm_cache[name]
        es = mybir.dt.size(dtype)
        off = (env.sm_off[0] + 31) // 32 * 32
        env.sm_off[0] = off + f * es
        assert env.sm_off[0] <= 6144, "smalls overflow"
        tv = TV({"smalls": env.sm}, "smalls", off, dtype, f)
        env.sm_cache[name] = tv
        return tv

    env.salloc = salloc

    env.rw = nc.alloc_sbuf_tensor("rowsb", [1, 2048], mybir.dt.uint8)
    env.rw_off = [0]
    env.rw_cache = {}

    def row(name, f=T, dtype=F32):
        if name in env.rw_cache:
            return env.rw_cache[name]
        es = mybir.dt.size(dtype)
        off = (env.rw_off[0] + 31) // 32 * 32
        env.rw_off[0] = off + f * es
        assert env.rw_off[0] <= 2048, "rowsb overflow"
        tv = TV({"rowsb": env.rw}, "rowsb", off, dtype, f, 0, 1)
        env.rw_cache[name] = tv
        return tv

    env.row = row

    def prow(pool, off, f=T, dtype=F32):
        return TV(pools, pool, off, dtype, f, 0, 1)

    env.prow = prow

    env.psA = nc.alloc_psum_tensor("psA", [128, 2048], F32)
    env.psB = nc.alloc_psum_tensor("psB", [128, 2048], F32)

    env.c = {}
    return env


def load_small_consts(b, env):
    names = [("permpm", F32, 128), ("permpm_bf", BF16, 128), ("idbf", BF16, 128), ("ones_col", F32, 1),
             ("ones_col_bf", BF16, 1), ("eps1p_col", F32, 1)]
    if "l0a" in env.blocks:
        names += [("nw_cols", F32, NDC), ("coefc", F32, 128)]
    if "l1a" in env.blocks:
        names += [("bin_cols", F32, NDC), ("bgat_cols", F32, NDC),
                  ("lam_cols", F32, NDC), ("mcol", F32, 8), ("onem", F32, 8)]
    if "l1f" in env.blocks:
        names += [("sg_cols", F32, NOC), ("su_cols", F32, NOC), ("sd_cols", F32, NDC)]
    for (nm, dt_, f) in names:
        tv = env.salloc(dt_, f, nm)
        b.dma("y", lambda E, tv=tv, nm=nm: E.dma_start(out=tv.ap(), in_=env.dr[nm].ap()),
              reads=[DK(nm)], writes=[tv.k()])
        env.c[nm] = tv
    for nm, f, dt_ in (("ones_row", 128, F32), ("nh_row", 128, F32),
                       ("nh_row_bf", 128, BF16), ("ccr", 8, F32)):
        tv = env.row(nm, f, dt_)
        b.dma("y", lambda E, tv=tv, nm=nm: E.dma_start(out=tv._ap, in_=env.dr[nm].ap()),
              reads=[DK(nm)], writes=[tv.k()])
        env.c[nm] = tv


def w_load(b, env, dram_name, col_lo, ncols, dest, nrow_chunks, dcol_off=0,
           n_split=4):
    n_split = min(n_split, nrow_chunks)
    per = nrow_chunks // n_split
    for sp in range(n_split):
        a0, a1 = sp * per, (sp + 1) * per if sp < n_split - 1 else nrow_chunks
        src = env.dr[dram_name].ap()[a0 * 128:a1 * 128,
                                     col_lo:col_lo + ncols].rearrange(
            "(a p) c -> p a c", p=128)
        dst = dest.ap(dcol_off + a0 * ncols,
                      dcol_off + a1 * ncols).rearrange(
            "p (a c) -> p a c", a=a1 - a0)
        b.dma("p", lambda E, src=src, dst=dst: E.dma_start(out=dst, in_=src),
              reads=[DK(dram_name)],
              writes=[dest.k(dcol_off + a0 * ncols, dcol_off + a1 * ncols)])


def newton_rsqrt(b, env, u_tv, lo, hi, out_tv, olo, scr1, scr2, iters=3):
    n = hi - lo
    I32 = mybir.dt.int32
    y = out_tv
    b.op("v", lambda E: E.tensor_scalar(out=y.ap(olo, olo + n).bitcast(I32),
                                        in0=u_tv.ap(lo, hi).bitcast(I32),
                                        scalar1=1, scalar2=None,
                                        op0=ALU.logical_shift_right),
         reads=[u_tv.k(lo, hi)], writes=[y.k(olo, olo + n)])
    b.op("v", lambda E: E.tensor_scalar(out=y.ap(olo, olo + n).bitcast(I32),
                                        in0=y.ap(olo, olo + n).bitcast(I32),
                                        scalar1=0x5f3759df, scalar2=-1,
                                        op0=ALU.subtract, op1=ALU.mult),
         reads=[y.k(olo, olo + n)], writes=[y.k(olo, olo + n)])
    for _ in range(iters):
        b.op("v", lambda E: E.tensor_tensor(out=scr1.ap(0, n), in0=y.ap(olo, olo + n),
                                            in1=y.ap(olo, olo + n), op=ALU.mult),
             reads=[y.k(olo, olo + n)], writes=[scr1.k(0, n)])
        b.op("v", lambda E: E.scalar_tensor_tensor(out=scr2.ap(0, n), in0=scr1.ap(0, n),
                                                   scalar=-0.5, in1=u_tv.ap(lo, hi),
                                                   op0=ALU.mult, op1=ALU.mult),
             reads=[scr1.k(0, n), u_tv.k(lo, hi)], writes=[scr2.k(0, n)])
        b.op("v", lambda E: E.tensor_scalar(out=scr2.ap(0, n), in0=scr2.ap(0, n),
                                            scalar1=1.5, scalar2=None, op0=ALU.add),
             reads=[scr2.k(0, n)], writes=[scr2.k(0, n)])
        b.op("v", lambda E: E.tensor_tensor(out=y.ap(olo, olo + n), in0=y.ap(olo, olo + n),
                                            in1=scr2.ap(0, n), op=ALU.mult),
             reads=[y.k(olo, olo + n), scr2.k(0, n)], writes=[y.k(olo, olo + n)])


def emit_ln(b, env, x_dram, xn, keep_rows=False):
    nc = env.nc
    xk = x_dram.name
    xres = env.tv("big1", 0, F32, NDC * T)
    sq = [env.tv("big5", i * (T * 2), BF16, T) for i in range(2)]
    xbb = [env.tv("big5", 28672 + i * (T * 2), BF16, T) for i in range(2)]
    tmp = env.tv("big5", 4096, F32, T)
    psA, psB = env.psA, env.psB

    for dc in range(NDC):
        b.dma("y" if dc % 2 == 0 else "p", lambda E, dc=dc: E.dma_start(
            out=xres.ap(dc * T, (dc + 1) * T),
            in_=x_dram.ap()[dc * 128:(dc + 1) * 128, :]),
            reads=[DK(xk, dc * 128 * T * 4, (dc + 1) * 128 * T * 4)],
            writes=[xres.k(dc * T, (dc + 1) * T)])
    for dc in range(NDC):
        s = sq[dc % 2]
        xb_ = xbb[dc % 2]
        b.op("s", lambda E, dc=dc, s=s: E.activation(
            out=s.ap(), in_=xres.ap(dc * T, (dc + 1) * T), func=AF.Square),
            reads=[xres.k(dc * T, (dc + 1) * T)], writes=[s.k()])
        b.op("v", lambda E, dc=dc, xb_=xb_: E.tensor_copy(
            out=xb_.ap(), in_=xres.ap(dc * T, (dc + 1) * T)),
            reads=[xres.k(dc * T, (dc + 1) * T)], writes=[xb_.k()])
        for tt in range(2):
            b.op("t", lambda E, dc=dc, tt=tt, xb_=xb_: E.matmul(
                out=psA.ap()[0:1, tt * 512:(tt + 1) * 512],
                lhsT=env.c["ones_col_bf"].ap(),
                rhs=xb_.ap(tt * 512, (tt + 1) * 512),
                start=(dc == 0), stop=(dc == NDC - 1)),
                reads=[xb_.k(tt * 512, (tt + 1) * 512),
                       env.c["ones_col_bf"].k()],
                writes=[PK("psA", tt * 512, (tt + 1) * 512)])
            b.op("t", lambda E, dc=dc, tt=tt, s=s: E.matmul(
                out=psA.ap()[0:1, 1024 + tt * 512:1024 + (tt + 1) * 512],
                lhsT=env.c["ones_col_bf"].ap(),
                rhs=s.ap(tt * 512, (tt + 1) * 512),
                start=(dc == 0), stop=(dc == NDC - 1)),
                reads=[s.k(tt * 512, (tt + 1) * 512), env.c["ones_col_bf"].k()],
                writes=[PK("psA", 1024 + tt * 512, 1024 + (tt + 1) * 512)])
    m_row = env.prow("big5", 8192)
    v_row = env.prow("big5", 12288)
    mm_row = env.prow("big5", 16384)
    rstd_row = env.prow("big5", 20480)
    mr_row = env.prow("big5", 24576)
    b.op("v", lambda E: E.tensor_scalar(out=m_row._ap, in0=psA.ap()[0:1, 0:1024],
                                        scalar1=1.0 / D, scalar2=None, op0=ALU.mult),
         reads=[PK("psA", 0, 1024)], writes=[m_row.k()])
    b.op("v", lambda E: E.tensor_scalar(out=v_row._ap, in0=psA.ap()[0:1, 1024:2048],
                                        scalar1=1.0 / D, scalar2=None, op0=ALU.mult),
         reads=[PK("psA", 1024, 2048)], writes=[v_row.k()])
    b.op("v", lambda E: E.tensor_tensor(out=mm_row._ap, in0=m_row._ap, in1=m_row._ap,
                                        op=ALU.mult),
         reads=[m_row.k()], writes=[mm_row.k()])
    b.op("v", lambda E: E.tensor_tensor(out=v_row._ap, in0=v_row._ap, in1=mm_row._ap,
                                        op=ALU.subtract),
         reads=[v_row.k(), mm_row.k()], writes=[v_row.k()])
    b.op("s", lambda E: E.activation(out=v_row._ap, in_=v_row._ap, func=AF.Ln,
                                     bias=env.c["ccr"]._ap[0:1, 0:1], scale=1.0),
         reads=[v_row.k(), env.c["ccr"].k(0, 1)], writes=[v_row.k()])
    b.op("s", lambda E: E.activation(out=rstd_row._ap, in_=v_row._ap, func=AF.Exp,
                                     scale=-0.5),
         reads=[v_row.k()], writes=[rstd_row.k()])
    b.op("v", lambda E: E.tensor_tensor(out=mr_row._ap, in0=m_row._ap, in1=rstd_row._ap,
                                        op=ALU.mult),
         reads=[m_row.k(), rstd_row.k()], writes=[mr_row.k()])
    for tt in range(2):
        b.op("t", lambda E, tt=tt: E.matmul(
            out=psB.ap()[:, tt * 512:(tt + 1) * 512], lhsT=env.c["ones_row"]._ap,
            rhs=rstd_row._ap[:, tt * 512:(tt + 1) * 512], start=True, stop=True),
            reads=[rstd_row.k(tt * 512, (tt + 1) * 512), env.c["ones_row"].k()],
            writes=[PK("psB", tt * 512, (tt + 1) * 512)])
        b.op("t", lambda E, tt=tt: E.matmul(
            out=psB.ap()[:, 1024 + tt * 512:1024 + (tt + 1) * 512],
            lhsT=env.c["ones_row"]._ap,
            rhs=mr_row._ap[:, tt * 512:(tt + 1) * 512], start=True, stop=True),
            reads=[mr_row.k(tt * 512, (tt + 1) * 512), env.c["ones_row"].k()],
            writes=[PK("psB", 1024 + tt * 512, 1024 + (tt + 1) * 512)])
    for dc in range(NDC):
        b.op("v", lambda E, dc=dc: E.tensor_tensor(
            out=tmp.ap(), in0=xres.ap(dc * T, (dc + 1) * T), in1=psB.ap()[:, 0:1024],
            op=ALU.mult),
            reads=[xres.k(dc * T, (dc + 1) * T), PK("psB", 0, 1024)], writes=[tmp.k()])
        b.op("v", lambda E, dc=dc: E.tensor_tensor(
            out=xn.ap(dc * T, (dc + 1) * T), in0=tmp.ap(), in1=psB.ap()[:, 1024:2048],
            op=ALU.subtract),
            reads=[tmp.k(), PK("psB", 1024, 2048)],
            writes=[xn.k(dc * T, (dc + 1) * T)])
    if keep_rows:
        return rstd_row, mr_row
    return None, None


def emit_l0a(b, env, x_in, x_out):
    psA, psB = env.psA, env.psB
    xn = env.tv("big2", 0, BF16, NDC * T)
    emit_ln(b, env, x_in, xn)

    mk(b, "l0a:ln")
    krot = env.tv("big1", 0, BF16, NDC * T)
    vbf = env.tv("big1", 4 * MB, BF16, NTC * D)
    wblk = [env.tv("big3", i * (NDC * 128 * 2), BF16, NDC * 128) for i in range(2)]
    vwb = env.tv("big3", 1 * MB, BF16, NDC * 512)
    cosf = env.tv("big4", 0, F32, T)
    sinf = env.tv("big4", T * 4, F32, T)
    ks = env.tv("big4", 2 * T * 4, F32, T)
    ksb = env.tv("big4", 22528, BF16, T)
    rs1 = env.tv("big4", 3 * T * 4, F32, T)
    rs2 = env.tv("big4", 4 * T * 4, F32, T)
    qstg = env.tv("big4", 5 * T * 4, BF16, T)
    b.dma("y", lambda E: E.dma_start(out=cosf.ap(), in_=env.dr["cosf"].ap()),
          reads=[DK("cosf")], writes=[cosf.k()])
    b.dma("y", lambda E: E.dma_start(out=sinf.ap(), in_=env.dr["sinf"].ap()),
          reads=[DK("sinf")], writes=[sinf.k()])
    dk = env.tv("big5", 0, F32, D)
    Ttv = env.tv("big5", 1 * MB, F32, D)
    ktmA = env.tv("big5", 2 * MB, BF16, D)
    vdkst = env.tv("big5", 2 * MB + D * 2, BF16, D)
    lamcA = env.tv("big5", 3 * MB, F32, D)
    b.dma("y", lambda E: E.dma_start(out=dk.ap(), in_=env.dr["dk_tab"].ap()),
          reads=[DK("dk_tab")], writes=[dk.k()])
    b.dma("y", lambda E: E.dma_start(out=lamcA.ap(), in_=env.dr["lamc_tab"].ap()),
          reads=[DK("lamc_tab")], writes=[lamcA.k()])

    def proj_rot(wname, out_dram):
        for ec in range(H):
            gamma_pump(b, env, 6)
            wb = wblk[ec % 2]
            w_load(b, env, wname, ec * 128, 128, wb, NDC)
            for tt in range(2):
                for dc in range(NDC):
                    b.op("t", lambda E, ec=ec, tt=tt, dc=dc, wb=wb: E.matmul(
                        out=psA.ap()[:, tt * 512:(tt + 1) * 512],
                        lhsT=wb.ap(dc * 128, (dc + 1) * 128),
                        rhs=xn.ap(dc * T + tt * 512, dc * T + (tt + 1) * 512),
                        start=(dc == 0), stop=(dc == NDC - 1)),
                        reads=[wb.k(dc * 128, (dc + 1) * 128),
                               xn.k(dc * T + tt * 512, dc * T + (tt + 1) * 512)],
                        writes=[PK("psA", tt * 512, (tt + 1) * 512)])
                b.op("s", lambda E, tt=tt: E.activation(
                    out=ks.ap(tt * 512, (tt + 1) * 512),
                    in_=psA.ap()[:, tt * 512:(tt + 1) * 512], func=AF.Silu),
                    reads=[PK("psA", tt * 512, (tt + 1) * 512)],
                    writes=[ks.k(tt * 512, (tt + 1) * 512)])
            b.op("s", lambda E: E.activation(out=ksb.ap(), in_=ks.ap(), func=AF.Copy),
                 reads=[ks.k()], writes=[ksb.k()])
            for tt in range(2):
                b.op("t", lambda E, tt=tt: E.matmul(
                    out=psB.ap()[:, tt * 512:(tt + 1) * 512],
                    lhsT=env.c["permpm_bf"].ap(),
                    rhs=ksb.ap(tt * 512, (tt + 1) * 512), start=True, stop=True),
                    reads=[ksb.k(tt * 512, (tt + 1) * 512), env.c["permpm_bf"].k()],
                    writes=[PK("psB", tt * 512, (tt + 1) * 512)])
            b.op("v", lambda E: E.tensor_tensor(out=rs1.ap(), in0=ks.ap(),
                                                in1=cosf.ap(), op=ALU.mult),
                 reads=[ks.k(), cosf.k()], writes=[rs1.k()])
            b.op("v", lambda E: E.tensor_tensor(out=rs2.ap(), in0=psB.ap()[:, 0:1024],
                                                in1=sinf.ap(), op=ALU.mult),
                 reads=[PK("psB", 0, 1024), sinf.k()], writes=[rs2.k()])
            if out_dram is None:
                dst = krot
                b.op("v", lambda E, ec=ec: E.tensor_tensor(
                    out=dst.ap(ec * T, (ec + 1) * T), in0=rs1.ap(), in1=rs2.ap(),
                    op=ALU.add),
                    reads=[rs1.k(), rs2.k()], writes=[dst.k(ec * T, (ec + 1) * T)])
            else:
                b.op("v", lambda E: E.tensor_tensor(
                    out=qstg.ap(), in0=rs1.ap(), in1=rs2.ap(), op=ALU.add),
                    reads=[rs1.k(), rs2.k()], writes=[qstg.k()])
                b.dma("y", lambda E, ec=ec: E.dma_start(
                    out=env.dr[out_dram].ap()[ec * 128:(ec + 1) * 128, :],
                    in_=qstg.ap()),
                    reads=[qstg.k()],
                    writes=[DK(out_dram, ec * 128 * T * 2, (ec + 1) * 128 * T * 2)])

    proj_rot("wk", None)
    mk(b, "l0a:kproj")
    if L0A_STAGE == 11:
        odbg = env.dout("o_krot", (128, NDC * T), mybir.dt.bfloat16)
        b.dma("y", lambda E: E.dma_start(out=odbg.ap(), in_=krot.ap()),
              reads=[krot.k()], writes=[DK("o_krot")])
        return
    for eb in range(4):
        w_load(b, env, "wv", eb * 512, 512, vwb, NDC)
        for tc in range(NTC):
            gamma_pump(b, env, 2)
            pb = (tc % 2) * 512
            for dc in range(NDC):
                b.op("t", lambda E, eb=eb, tc=tc, dc=dc, pb=pb: E.matmul(
                    out=psA.ap()[:, pb:pb + 512],
                    lhsT=xn.ap(dc * T + tc * 128, dc * T + (tc + 1) * 128),
                    rhs=vwb.ap(dc * 512, (dc + 1) * 512),
                    start=(dc == 0), stop=(dc == NDC - 1)),
                    reads=[xn.k(dc * T + tc * 128, dc * T + (tc + 1) * 128),
                           vwb.k(dc * 512, (dc + 1) * 512)],
                    writes=[PK("psA", pb, pb + 512)])
            if L0A_STAGE == 1152:
                b.op("v", lambda E, eb=eb, tc=tc, pb=pb: E.tensor_copy(
                    out=vbf.ap(tc * D + eb * 512, tc * D + (eb + 1) * 512),
                    in_=psA.ap()[:, pb:pb + 512]),
                    reads=[PK("psA", pb, pb + 512)],
                    writes=[vbf.k(tc * D + eb * 512, tc * D + (eb + 1) * 512)])
            else:
                b.op("s", lambda E, eb=eb, tc=tc, pb=pb: E.activation(
                    out=vbf.ap(tc * D + eb * 512, tc * D + (eb + 1) * 512),
                    in_=psA.ap()[:, pb:pb + 512], func=AF.Copy),
                    reads=[PK("psA", pb, pb + 512)],
                    writes=[vbf.k(tc * D + eb * 512, tc * D + (eb + 1) * 512)])
            if L0A_STAGE not in (1151, 1152):
                b.op("v", lambda E, eb=eb, tc=tc, pb=pb: E.tensor_tensor(
                    out=vdkst.ap(eb * 512, (eb + 1) * 512),
                    in0=psA.ap()[:, pb:pb + 512],
                    in1=dk.ap(eb * 512, (eb + 1) * 512), op=ALU.mult),
                    reads=[PK("psA", pb, pb + 512), dk.k(eb * 512, (eb + 1) * 512)],
                    writes=[vdkst.k(eb * 512, (eb + 1) * 512)])
            if L0A_STAGE not in (1151, 1152, 1153):
                b.dma("y", lambda E, eb=eb, tc=tc: E.dma_start(
                    out=env.dr["vdk_d"].ap()[tc * 128:(tc + 1) * 128, eb * 512:(eb + 1) * 512],
                    in_=vdkst.ap(eb * 512, (eb + 1) * 512)),
                    reads=[vdkst.k(eb * 512, (eb + 1) * 512)],
                    writes=[DK("vdk_d", (tc * D + eb * 512) * 2, (tc * D + (eb + 1) * 512) * 2)])
    if L0A_STAGE in (115, 1151, 1152, 1153):
        odbg = env.dout("o_v", (128, NTC * D), mybir.dt.bfloat16)
        b.dma("y", lambda E: E.dma_start(out=odbg.ap(), in_=vbf.ap()),
              reads=[vbf.k()], writes=[DK("o_v")])
        return
    b.op("v", lambda E: E.memset(Ttv.ap(), 0.0), writes=[Ttv.k()])
    vdkc_bufs = [env.tv("big3", 1 * MB + i * D * 2, BF16, D) for i in range(2)]
    for tc in range(NTC):
        vc = vdkc_bufs[tc % 2]
        b.dma("y", lambda E, tc=tc, vc=vc: E.dma_start(
            out=vc.ap(), in_=env.dr["vdk_d"].ap()[tc * 128:(tc + 1) * 128, :]),
            reads=[DK("vdk_d", tc * 128 * D * 2, (tc + 1) * 128 * D * 2)],
            writes=[vc.k()])
        for h in range(H):
            b.op("t", lambda E, tc=tc, h=h: E.matmul(
                out=psB.ap()[:, h * 128:(h + 1) * 128],
                lhsT=krot.ap(h * T + tc * 128, h * T + (tc + 1) * 128),
                rhs=env.c["idbf"].ap(), start=True, stop=True),
                reads=[krot.k(h * T + tc * 128, h * T + (tc + 1) * 128),
                       env.c["idbf"].k()],
                writes=[PK("psB", h * 128, (h + 1) * 128)])
        b.op("v", lambda E: E.tensor_copy(out=ktmA.ap(), in_=psB.ap()),
             reads=[PK("psB", 0, 2048)], writes=[ktmA.k()])
        if L0A_STAGE != 117:
            for h in range(H):
                b.op("t", lambda E, tc=tc, h=h, vc=vc: E.matmul(
                    out=psA.ap()[:, h * 128:(h + 1) * 128],
                    lhsT=ktmA.ap(h * 128, (h + 1) * 128),
                    rhs=vc.ap(h * 128, (h + 1) * 128), start=True, stop=True),
                    reads=[ktmA.k(h * 128, (h + 1) * 128), vc.k(h * 128, (h + 1) * 128)],
                    writes=[PK("psA", h * 128, (h + 1) * 128)])
            b.op("v", lambda E: E.tensor_tensor(out=Ttv.ap(), in0=Ttv.ap(),
                                                in1=lamcA.ap(), op=ALU.mult),
                 reads=[Ttv.k(), lamcA.k()], writes=[Ttv.k()])
            b.op("v", lambda E: E.tensor_tensor(out=Ttv.ap(), in0=Ttv.ap(),
                                                in1=psA.ap(), op=ALU.add),
                 reads=[Ttv.k(), PK("psA", 0, 2048)], writes=[Ttv.k()])
    if L0A_STAGE == 12:
        odbg = env.dout("o_T", (128, D))
        b.dma("y", lambda E: E.dma_start(out=odbg.ap(), in_=Ttv.ap()),
              reads=[Ttv.k()], writes=[DK("o_T")])
        return
    mk(b, "l0a:v+phA")
    b.dma("p", lambda E: E.dma_start(out=env.dr["ag1_in"].ap(), in_=Ttv.ap()),
          reads=[Ttv.k()], writes=[DK("ag1_in")])
    b.op("p", lambda E: E.collective_compute(
        "AllGather", ALU.bypass, replica_groups=[list(range(NCORE))],
        ins=[env.dr["ag1_in"].ap()], outs=[env.dr["ag1_out"].ap()]),
        reads=[DK("ag1_in")], writes=[DK("ag1_out")])
    if L0A_STAGE == 13:
        odbg = env.dout("o_ag", (NCORE * 128, D))
        b.dma("y", lambda E: E.dma_start(out=odbg.ap(), in_=env.dr["ag1_out"].ap()),
              reads=[DK("ag1_out")], writes=[DK("o_ag")])
        return
    proj_rot("wq", "qrot_d")
    wg1b = env.tv("big3", 8192, BF16, NDC * 128)
    w_load(b, env, "wg1", 0, 128, wg1b, NDC)
    g1 = env.tv("big3", 16384, BF16, T)
    wg2b = env.tv("big3", 18432, BF16, D)
    w_load(b, env, "wg2", 0, D, wg2b, 1)
    for tt in range(2):
        for dc in range(NDC):
            b.op("t", lambda E, tt=tt, dc=dc: E.matmul(
                out=psB.ap()[:, 1024 + tt * 512:1024 + (tt + 1) * 512],
                lhsT=wg1b.ap(dc * 128, (dc + 1) * 128),
                rhs=xn.ap(dc * T + tt * 512, dc * T + (tt + 1) * 512),
                start=(dc == 0), stop=(dc == NDC - 1)),
                reads=[wg1b.k(dc * 128, (dc + 1) * 128),
                       xn.k(dc * T + tt * 512, dc * T + (tt + 1) * 512)],
                writes=[PK("psB", 1024 + tt * 512, 1024 + (tt + 1) * 512)])
    b.op("v", lambda E: E.tensor_copy(out=g1.ap(), in_=psB.ap()[:, 1024:2048]),
         reads=[PK("psB", 1024, 2048)], writes=[g1.k()])
    mk(b, "l0a:q+g1")
    dtt = env.tv("big2", 0, F32, D)
    dqt = env.tv("big2", 1 * MB, F32, D)
    lamc = env.tv("big2", 2 * MB, F32, D)
    S = env.tv("big2", 3 * MB, F32, D)
    tjb = [env.tv("big5", i * MB, F32, D) for i in range(2)]
    b.op("v", lambda E: E.memset(S.ap(), 0.0), writes=[S.k()])
    for j in range(NCORE):
        tj = tjb[j % 2]
        b.dma("y", lambda E, j=j, tj=tj: E.dma_start(
            out=tj.ap(), in_=env.dr["ag1_out"].ap()[j * 128:(j + 1) * 128, :]),
            reads=[DK("ag1_out")], writes=[tj.k()])
        for h in range(H):
            b.op("v", lambda E, j=j, h=h, tj=tj: E.scalar_tensor_tensor(
                out=S.ap(h * 128, (h + 1) * 128), in0=tj.ap(h * 128, (h + 1) * 128),
                scalar=env.c["coefc"].ap(j * 16 + h, j * 16 + h + 1),
                in1=S.ap(h * 128, (h + 1) * 128), op0=ALU.mult, op1=ALU.add),
                reads=[tj.k(h * 128, (h + 1) * 128), S.k(h * 128, (h + 1) * 128),
                       env.c["coefc"].k(j * 16 + h, j * 16 + h + 1)],
                writes=[S.k(h * 128, (h + 1) * 128)])
    b.dma("y", lambda E: E.dma_start(out=dtt.ap(), in_=env.dr["dt_tab"].ap()),
          reads=[DK("dt_tab")], writes=[dtt.k()])
    b.dma("y", lambda E: E.dma_start(out=dqt.ap(), in_=env.dr["dq_tab"].ap()),
          reads=[DK("dq_tab")], writes=[dqt.k()])
    b.dma("y", lambda E: E.dma_start(out=lamc.ap(), in_=env.dr["lamc_tab"].ap()),
          reads=[DK("lamc_tab")], writes=[lamc.k()])
    qcb = [env.tv("big3", i * D * 2, BF16, D) for i in range(2)]
    vdc = [env.tv("big3", 1 * MB + i * D * 2, BF16, D) for i in range(2)]

    def sweep_loads(tc):
        qc, vc = qcb[tc % 2], vdc[tc % 2]
        b.dma("y", lambda E, tc=tc, qc=qc: E.dma_start(
            out=qc.ap().rearrange("p (a c) -> p a c", a=H),
            in_=env.dr["qrot_d"].ap()[:, tc * 128:(tc + 1) * 128].rearrange(
                "(a p) c -> p a c", p=128)),
            reads=[DK("qrot_d")], writes=[qc.k()])
        b.dma("y", lambda E, tc=tc, vc=vc: E.dma_start(
            out=vc.ap(), in_=env.dr["vdk_d"].ap()[tc * 128:(tc + 1) * 128, :]),
            reads=[DK("vdk_d", tc * 128 * D * 2, (tc + 1) * 128 * D * 2)],
            writes=[vc.k()])

    sweep_loads(0)
    sweep_loads(1)
    if L0A_STAGE == 1:
        for ec in range(NDC):
            xt_ = env.tv("big5", (ec % 2) * 4096, F32, T)
            b.dma("y", lambda E, ec=ec, xt_=xt_: E.dma_start(
                out=xt_.ap(), in_=x_in.ap()[ec * 128:(ec + 1) * 128, :]),
                reads=[DK(x_in.name, ec * 128 * T * 4, (ec + 1) * 128 * T * 4)],
                writes=[xt_.k()])
            b.op("v", lambda E, ec=ec, xt_=xt_: E.tensor_tensor(
                out=xt_.ap(), in0=xt_.ap(), in1=S.ap(0, 1024) if False else xt_.ap(),
                op=ALU.add), reads=[xt_.k()], writes=[xt_.k()])
            b.dma("y", lambda E, ec=ec, xt_=xt_: E.dma_start(
                out=x_out.ap()[ec * 128:(ec + 1) * 128, :], in_=xt_.ap()),
                reads=[xt_.k()],
                writes=[DK(x_out.name, ec * 128 * T * 4, (ec + 1) * 128 * T * 4)])
        odbg = env.dout("o_S", (128, D))
        b.dma("y", lambda E: E.dma_start(out=odbg.ap(), in_=S.ap()),
              reads=[S.k()], writes=[DK("o_S")])
        return

    mk(b, "l0a:prefix")
    ktm = env.tv("big5", 0, BF16, D)
    sTD = env.tv("big5", D * 2, BF16, D)
    qdq = env.tv("big5", 2 * D * 2, BF16, D)
    Sbf = env.tv("big5", 3 * D * 2, BF16, D)
    sg = env.tv("big5", 2 * MB, F32, D)
    osg = env.tv("big5", 24576, F32, D)
    xgb = env.tv("big5", 32768, BF16, D)
    xgT = env.tv("big4", 0, BF16, NDC * T)
    msv = env.salloc(F32, 16, "l0a_ms")
    rst = env.salloc(F32, 16, "l0a_rstd")
    nsc1 = env.salloc(F32, 16, "l0a_n1")
    nsc2 = env.salloc(F32, 16, "l0a_n2")
    b.op("v", lambda E: E.tensor_copy(out=Sbf.ap(), in_=S.ap()),
         reads=[S.k()], writes=[Sbf.k()])
    def sweep_gate(tc):
        for eb in range(4):
            b.op("t", lambda E, tc=tc, eb=eb: E.matmul(
                out=psB.ap()[:, eb * 512:(eb + 1) * 512],
                lhsT=g1.ap(tc * 128, (tc + 1) * 128),
                rhs=wg2b.ap(eb * 512, (eb + 1) * 512), start=True, stop=True),
                reads=[g1.k(tc * 128, (tc + 1) * 128), wg2b.k(eb * 512, (eb + 1) * 512)],
                writes=[PK("psB", eb * 512, (eb + 1) * 512)])
        b.op("s", lambda E: E.activation(out=sg.ap(), in_=psB.ap(), func=AF.Sigmoid),
             reads=[PK("psB", 0, 2048)], writes=[sg.k()])

    sweep_gate(0)
    for tc in range(NTC):
        qc = qcb[tc % 2]
        vc = vdc[tc % 2]
        for h in range(H):
            b.op("t", lambda E, tc=tc, h=h, qc=qc: E.matmul(
                out=psA.ap()[:, h * 128:(h + 1) * 128],
                lhsT=krot.ap(h * T + tc * 128, h * T + (tc + 1) * 128),
                rhs=qc.ap(h * 128, (h + 1) * 128), start=True, stop=True),
                reads=[krot.k(h * T + tc * 128, h * T + (tc + 1) * 128),
                       qc.k(h * 128, (h + 1) * 128)],
                writes=[PK("psA", h * 128, (h + 1) * 128)])
        for h in range(H):
            b.op("v", lambda E, h=h: E.tensor_tensor(
                out=sTD.ap(h * 128, (h + 1) * 128),
                in0=psA.ap()[:, h * 128:(h + 1) * 128],
                in1=dtt.ap(h * 128, (h + 1) * 128), op=ALU.mult),
                reads=[PK("psA", h * 128, (h + 1) * 128), dtt.k(h * 128, (h + 1) * 128)],
                writes=[sTD.k(h * 128, (h + 1) * 128)])
        for h in range(H):
            b.op("t", lambda E, tc=tc, h=h: E.matmul(
                out=psB.ap()[:, h * 128:(h + 1) * 128],
                lhsT=krot.ap(h * T + tc * 128, h * T + (tc + 1) * 128),
                rhs=env.c["idbf"].ap(), start=True, stop=True),
                reads=[krot.k(h * T + tc * 128, h * T + (tc + 1) * 128),
                       env.c["idbf"].k()],
                writes=[PK("psB", h * 128, (h + 1) * 128)])
        b.op("v", lambda E: E.tensor_copy(out=ktm.ap(), in_=psB.ap()),
             reads=[PK("psB", 0, 2048)], writes=[ktm.k()])
        for h in range(H):
            b.op("t", lambda E, h=h, vc=vc: E.matmul(
                out=psA.ap()[:, h * 128:(h + 1) * 128],
                lhsT=ktm.ap(h * 128, (h + 1) * 128),
                rhs=vc.ap(h * 128, (h + 1) * 128), start=True, stop=True),
                reads=[ktm.k(h * 128, (h + 1) * 128), vc.k(h * 128, (h + 1) * 128)],
                writes=[PK("psA", h * 128, (h + 1) * 128)])
        for h in range(H):
            b.op("v", lambda E, h=h, qc=qc: E.tensor_tensor(
                out=qdq.ap(h * 128, (h + 1) * 128),
                in0=qc.ap(h * 128, (h + 1) * 128),
                in1=dqt.ap(h * 128, (h + 1) * 128), op=ALU.mult),
                reads=[qc.k(h * 128, (h + 1) * 128), dqt.k(h * 128, (h + 1) * 128)],
                writes=[qdq.k(h * 128, (h + 1) * 128)])
        for h in range(H):
            b.op("t", lambda E, tc=tc, h=h: E.matmul(
                out=psB.ap()[:, h * 128:(h + 1) * 128],
                lhsT=sTD.ap(h * 128, (h + 1) * 128),
                rhs=vbf.ap(tc * D + h * 128, tc * D + (h + 1) * 128),
                start=True, stop=False),
                reads=[sTD.k(h * 128, (h + 1) * 128),
                       vbf.k(tc * D + h * 128, tc * D + (h + 1) * 128)],
                writes=[PK("psB", h * 128, (h + 1) * 128)])
            b.op("t", lambda E, h=h: E.matmul(
                out=psB.ap()[:, h * 128:(h + 1) * 128],
                lhsT=qdq.ap(h * 128, (h + 1) * 128),
                rhs=Sbf.ap(h * 128, (h + 1) * 128), start=False, stop=True),
                reads=[qdq.k(h * 128, (h + 1) * 128), Sbf.k(h * 128, (h + 1) * 128)],
                writes=[PK("psB", h * 128, (h + 1) * 128)])
        b.op("v", lambda E: E.tensor_tensor(out=S.ap(), in0=S.ap(), in1=lamc.ap(),
                                            op=ALU.mult),
             reads=[S.k(), lamc.k()], writes=[S.k()])
        b.op("v", lambda E: E.tensor_tensor(out=S.ap(), in0=S.ap(), in1=psA.ap(),
                                            op=ALU.add),
             reads=[S.k(), PK("psA", 0, 2048)], writes=[S.k()])
        b.op("v", lambda E: E.tensor_copy(out=Sbf.ap(), in_=S.ap()),
             reads=[S.k()], writes=[Sbf.k()])
        b.op("v", lambda E: E.tensor_tensor(out=osg.ap(), in0=psB.ap(), in1=sg.ap(),
                                            op=ALU.mult),
             reads=[PK("psB", 0, 2048), sg.k()], writes=[osg.k()])
        sqv = env.tv("big5", 2 * MB, F32, D)
        b.op("s", lambda E: E.activation(out=sqv.ap(), in_=osg.ap(), func=AF.Square),
             reads=[osg.k()], writes=[sqv.k()])
        b.op("v", lambda E: E.tensor_reduce(
            out=msv.ap(), in_=sqv.ap().rearrange("p (a c) -> p a c", a=H),
            axis=AX.X, op=ALU.add),
            reads=[sqv.k()], writes=[msv.k()])
        b.op("v", lambda E: E.tensor_scalar(out=msv.ap(), in0=msv.ap(),
                                            scalar1=1.0 / HD, scalar2=1e-6,
                                            op0=ALU.mult, op1=ALU.add),
             reads=[msv.k()], writes=[msv.k()])
        newton_rsqrt(b, env, msv, 0, 16, rst, 0, nsc1, nsc2)
        for h in range(H):
            b.op("v", lambda E, h=h: E.tensor_scalar(
                out=xgb.ap(h * 128, (h + 1) * 128),
                in0=osg.ap(h * 128, (h + 1) * 128),
                scalar1=rst.ap(h, h + 1), scalar2=None, op0=ALU.mult),
                reads=[osg.k(h * 128, (h + 1) * 128), rst.k(h, h + 1)],
                writes=[xgb.k(h * 128, (h + 1) * 128)])
        if tc + 2 < NTC:
            sweep_loads(tc + 2)
        if tc + 1 < NTC:
            sweep_gate(tc + 1)
        for ec in range(NDC):
            b.op("t", lambda E, ec=ec: E.matmul(
                out=psA.ap()[:, ec * 128:(ec + 1) * 128],
                lhsT=xgb.ap(ec * 128, (ec + 1) * 128),
                rhs=env.c["idbf"].ap(), start=True, stop=True),
                reads=[xgb.k(ec * 128, (ec + 1) * 128), env.c["idbf"].k()],
                writes=[PK("psA", ec * 128, (ec + 1) * 128)])
        for ec in range(NDC):
            b.op("v", lambda E, ec=ec, tc=tc: E.tensor_scalar(
                out=xgT.ap(ec * T + tc * 128, ec * T + (tc + 1) * 128),
                in0=psA.ap()[:, ec * 128:(ec + 1) * 128],
                scalar1=env.c["nw_cols"].ap(ec, ec + 1), scalar2=None, op0=ALU.mult),
                reads=[PK("psA", ec * 128, (ec + 1) * 128), env.c["nw_cols"].k(ec, ec + 1)],
                writes=[xgT.k(ec * T + tc * 128, ec * T + (tc + 1) * 128)])
    mk(b, "l0a:sweep")
    if L0A_STAGE == 2:
        odbg = env.dout("o_xgT", (128, NDC * T), mybir.dt.bfloat16)
        b.dma("y", lambda E: E.dma_start(out=odbg.ap(), in_=xgT.ap()),
              reads=[xgT.k()], writes=[DK("o_xgT")])
        odbg2 = env.dout("o_S", (128, D))
        b.dma("y", lambda E: E.dma_start(out=odbg2.ap(), in_=S.ap()),
              reads=[S.k()], writes=[DK("o_S")])
        for ec in range(NDC):
            xt_ = env.tv("big5", (ec % 2) * 4096, F32, T)
            b.dma("y", lambda E, ec=ec, xt_=xt_: E.dma_start(
                out=xt_.ap(), in_=x_in.ap()[ec * 128:(ec + 1) * 128, :]),
                reads=[DK(x_in.name, ec * 128 * T * 4, (ec + 1) * 128 * T * 4)],
                writes=[xt_.k()])
            b.dma("y", lambda E, ec=ec, xt_=xt_: E.dma_start(
                out=x_out.ap()[ec * 128:(ec + 1) * 128, :], in_=xt_.ap()),
                reads=[xt_.k()],
                writes=[DK(x_out.name, ec * 128 * T * 4, (ec + 1) * 128 * T * 4)])
        return

    xt = [env.tv("big5", i * (T * 4), F32, T) for i in range(2)]
    ot = [env.tv("big5", 2 * MB + i * 512 * 4, F32, 512) for i in range(4)]
    for ecp in range(NDC):
        gamma_pump(b, env, 4)
        wb = wblk[ecp % 2]
        w_load(b, env, "wo", ecp * 128, 128, wb, NDC)
        x_t = xt[ecp % 2]
        b.dma("y", lambda E, ecp=ecp, x_t=x_t: E.dma_start(
            out=x_t.ap(), in_=x_in.ap()[ecp * 128:(ecp + 1) * 128, :]),
            reads=[DK(x_in.name, ecp * 128 * T * 4, (ecp + 1) * 128 * T * 4)],
            writes=[x_t.k()])
        for tt in range(2):
            pcol = ((2 * ecp + tt) % 4) * 512
            for ein in range(NDC):
                b.op("t", lambda E, ecp=ecp, tt=tt, ein=ein, wb=wb, pcol=pcol: E.matmul(
                    out=psB.ap()[:, pcol:pcol + 512],
                    lhsT=wb.ap(ein * 128, (ein + 1) * 128),
                    rhs=xgT.ap(ein * T + tt * 512, ein * T + (tt + 1) * 512),
                    start=(ein == 0), stop=(ein == NDC - 1)),
                    reads=[wb.k(ein * 128, (ein + 1) * 128),
                           xgT.k(ein * T + tt * 512, ein * T + (tt + 1) * 512)],
                    writes=[PK("psB", pcol, pcol + 512)])
            o_t = ot[(2 * ecp + tt) % 4]
            b.op("v", lambda E, tt=tt, x_t=x_t, o_t=o_t, pcol=pcol: E.tensor_tensor(
                out=o_t.ap(), in0=x_t.ap(tt * 512, (tt + 1) * 512),
                in1=psB.ap()[:, pcol:pcol + 512], op=ALU.add),
                reads=[x_t.k(tt * 512, (tt + 1) * 512), PK("psB", pcol, pcol + 512)],
                writes=[o_t.k()])
            b.dma("y", lambda E, ecp=ecp, tt=tt, o_t=o_t: E.dma_start(
                out=x_out.ap()[ecp * 128:(ecp + 1) * 128, tt * 512:(tt + 1) * 512],
                in_=o_t.ap()),
                reads=[o_t.k()],
                writes=[DK(x_out.name, ecp * 128 * T * 4, (ecp + 1) * 128 * T * 4)])


def emit_gamma_stream(b, env, mat, thname):
    R, Cc = env.dr[mat].shape
    gacc = env.salloc(F32, 1, "gacc_" + thname)
    rcol = env.salloc(F32, 1, "grc_" + thname)
    wtl = [env.tv("big3", 24576 + i * 4096, F32, 1024) for i in range(2)]
    b.op("v", lambda E, gacc=gacc: E.memset(gacc.ap(), 0.0), writes=[gacc.k()])
    cbs = [(c, min(c + 1024, Cc)) for c in range(0, Cc, 1024)]
    idx = [0]

    def tile(rc, c0, c1):
        def emitfn():
            wt = wtl[idx[0] % 2]
            idx[0] += 1
            n = c1 - c0
            b.dma("s", lambda E, wt=wt: E.dma_start(
                out=wt.ap(0, n), in_=env.dr[mat].ap()[rc * 128:(rc + 1) * 128, c0:c1]),
                reads=[DK(mat, rc * 128 * Cc * 4, (rc + 1) * 128 * Cc * 4)],
                writes=[wt.k(0, n)])
            b.op("v", lambda E, wt=wt: E.tensor_reduce(
                out=rcol.ap(), in_=wt.ap(0, n), axis=AX.X, op=ALU.add,
                apply_absolute_value=True),
                reads=[wt.k(0, n)], writes=[rcol.k()])
            b.op("v", lambda E: E.tensor_tensor(out=gacc.ap(), in0=gacc.ap(),
                                                in1=rcol.ap(), op=ALU.add),
                 reads=[gacc.k(), rcol.k()], writes=[gacc.k()])
        return emitfn

    for rc in range(R // 128):
        for (c0, c1) in cbs:
            env.gpump.append(tile(rc, c0, c1))
    env.gsize[thname] = (gacc, R * Cc)


def gamma_pump(b, env, n):
    for _ in range(n):
        if not env.gpump:
            return
        env.gpump.popleft()()


def emit_gamma_fin(b, env, thname):
    psA = env.psA
    gacc, N = env.gsize[thname]
    thcol = env.salloc(F32, 1, thname)
    throw = env.row("throw", 8)
    b.op("t", lambda E: E.matmul(out=psA.ap()[0:1, 0:1], lhsT=gacc.ap(),
                                 rhs=env.c["ones_col"].ap(), start=True, stop=True),
         reads=[gacc.k(), env.c["ones_col"].k()], writes=[PK("psA", 0, 1)])
    b.op("s", lambda E, N=N: E.activation(
        out=throw._ap[:, 0:1], in_=psA.ap()[0:1, 0:1], func=AF.Identity,
        scale=0.5 / N, bias=env.c["ccr"]._ap[0:1, 1:2]),
        reads=[PK("psA", 0, 1), env.c["ccr"].k(1, 2)], writes=[throw.k(0, 1)])
    b.op("t", lambda E: E.matmul(out=psA.ap()[:, 0:1], lhsT=env.c["ones_row"]._ap,
                                 rhs=throw._ap[:, 0:1], start=True, stop=True),
         reads=[throw.k(0, 1), env.c["ones_row"].k()], writes=[PK("psA", 0, 1)])
    b.op("v", lambda E: E.tensor_copy(out=thcol.ap(), in_=psA.ap()[:, 0:1]),
         reads=[PK("psA", 0, 1)], writes=[thcol.k()])
    return thcol


def quantize_tile(b, env, win, lo, hi, thcol, Ab, Bb, qw, qlo):
    n = hi - lo
    b.op("v", lambda E: E.tensor_scalar(out=Ab.ap(0, n), in0=win.ap(lo, hi),
                                        scalar1=thcol.ap(), scalar2=None, op0=ALU.is_gt),
         reads=[win.k(lo, hi), thcol.k()], writes=[Ab.k(0, n)])
    b.op("s", lambda E: E.activation(out=Bb.ap(0, n), in_=win.ap(lo, hi), func=AF.Sign,
                                     bias=thcol.ap()),
         reads=[win.k(lo, hi), thcol.k()], writes=[Bb.k(0, n)])
    b.op("v", lambda E: E.scalar_tensor_tensor(out=qw.ap(qlo, qlo + n), in0=Bb.ap(0, n),
                                               scalar=0.5, in1=Ab.ap(0, n),
                                               op0=ALU.mult, op1=ALU.add),
         reads=[Bb.k(0, n), Ab.k(0, n)], writes=[qw.k(qlo, qlo + n)])


def emit_l0f(b, env, x_in, x_out):
    psA, psB = env.psA, env.psB
    xn = env.tv("big2", 0, BF16, NDC * T)
    emit_ln(b, env, x_in, xn)
    rs_row = env.prow("big5", 30720)
    for tt in range(2):
        for dc in range(NDC):
            b.op("t", lambda E, tt=tt, dc=dc: E.matmul(
                out=psA.ap()[0:1, tt * 512:(tt + 1) * 512],
                lhsT=env.c["ones_col_bf"].ap(),
                rhs=xn.ap(dc * T + tt * 512, dc * T + (tt + 1) * 512),
                start=(dc == 0), stop=(dc == NDC - 1)),
                reads=[xn.k(dc * T + tt * 512, dc * T + (tt + 1) * 512),
                       env.c["ones_col_bf"].k()],
                writes=[PK("psA", tt * 512, (tt + 1) * 512)])
    b.op("v", lambda E: E.tensor_copy(out=rs_row._ap, in_=psA.ap()[0:1, 0:1024]),
         reads=[PK("psA", 0, 1024)], writes=[rs_row.k()])
    rs_hi = env.prow("big5", 34816, T, BF16)
    rs_lo = env.prow("big5", 36864, T, BF16)
    b.op("v", lambda E: E.tensor_copy(out=rs_hi._ap, in_=rs_row._ap),
         reads=[rs_row.k()], writes=[rs_hi.k()])
    b.op("v", lambda E: E.scalar_tensor_tensor(out=rs_lo._ap, in0=rs_hi._ap,
                                               scalar=-1.0, in1=rs_row._ap,
                                               op0=ALU.mult, op1=ALU.add),
         reads=[rs_hi.k(), rs_row.k()], writes=[rs_lo.k()])
    mk(b, "l0f:ln+rs")
    while env.gpump:
        gamma_pump(b, env, 8)
    th1 = emit_gamma_fin(b, env, "th1")
    th2 = emit_gamma_fin(b, env, "th2")
    w1b = [env.tv("big3", 0, F32, D), env.tv("big3", 16384, F32, D)]
    w2b = [env.tv("big3", 8192, F32, D), env.tv("big3", 24576, F32, D)]
    Ab = env.tv("big5", 0, BF16, D)
    Bb = env.tv("big5", 4096, BF16, D)
    Ab2 = env.tv("big4", 24576, BF16, D)
    Bb2 = env.tv("big4", 28672, BF16, D)
    qw1b = [env.tv("big5", 8192 + i * 4096, BF16, D) for i in range(2)]
    qw2b = [env.tv("big5", 16384 + i * 4096, BF16, D) for i in range(2)]
    sig = [env.tv("big5", 24576 + i * 2048, F32, 512) for i in range(2)]
    hidt = [env.tv("big1", hc * 2048, BF16, T) if hc < 32
            else env.tv("big4", (hc - 32) * 2048, BF16, T) for hc in range(NHC)]

    def st1_loads(hc):
        wa, wb2 = w1b[hc % 2], w2b[hc % 2]
        for hf in range(2):
            b.dma("s", lambda E, hc=hc, hf=hf, wa=wa: E.dma_start(
                out=wa.ap(hf * 1024, (hf + 1) * 1024).rearrange("p (a c) -> p a c", a=8),
                in_=env.dr["w1t"].ap()[hf * 1024:(hf + 1) * 1024,
                                       hc * 128:(hc + 1) * 128].rearrange(
                    "(a p) c -> p a c", p=128)),
                reads=[DK("w1t")], writes=[wa.k(hf * 1024, (hf + 1) * 1024)])
            b.dma("y", lambda E, hc=hc, hf=hf, wb2=wb2: E.dma_start(
                out=wb2.ap(hf * 1024, (hf + 1) * 1024).rearrange("p (a c) -> p a c", a=8),
                in_=env.dr["w2t"].ap()[hf * 1024:(hf + 1) * 1024,
                                       hc * 128:(hc + 1) * 128].rearrange(
                    "(a p) c -> p a c", p=128)),
                reads=[DK("w2t")], writes=[wb2.k(hf * 1024, (hf + 1) * 1024)])

    st1_loads(0)
    st1_loads(1)
    quantize_tile(b, env, w1b[0], 0, D, th1, Ab, Bb, qw1b[0], 0)
    quantize_tile(b, env, w2b[0], 0, D, th2, Ab2, Bb2, qw2b[0], 0)
    for hc in range(NHC):
        qw1 = qw1b[hc % 2]
        qw2 = qw2b[hc % 2]
        if hc + 1 < NHC:
            quantize_tile(b, env, w1b[(hc + 1) % 2], 0, D, th1, Ab, Bb,
                          qw1b[(hc + 1) % 2], 0)
            quantize_tile(b, env, w2b[(hc + 1) % 2], 0, D, th2, Ab2, Bb2,
                          qw2b[(hc + 1) % 2], 0)
        if hc + 2 < NHC:
            st1_loads(hc + 2)
        for tt in range(2):
            for dc in range(NDC):
                b.op("t", lambda E, tt=tt, dc=dc, qw1=qw1: E.matmul(
                    out=psA.ap()[:, tt * 512:(tt + 1) * 512],
                    lhsT=qw1.ap(dc * 128, (dc + 1) * 128),
                    rhs=xn.ap(dc * T + tt * 512, dc * T + (tt + 1) * 512),
                    start=(dc == 0), stop=False),
                    reads=[qw1.k(dc * 128, (dc + 1) * 128),
                           xn.k(dc * T + tt * 512, dc * T + (tt + 1) * 512)],
                    writes=[PK("psA", tt * 512, (tt + 1) * 512)])
            for rr in (rs_hi, rs_lo):
                b.op("t", lambda E, tt=tt, rr=rr, st=(rr is rs_lo): E.matmul(
                    out=psA.ap()[:, tt * 512:(tt + 1) * 512], lhsT=env.c["nh_row_bf"]._ap,
                    rhs=rr._ap[:, tt * 512:(tt + 1) * 512], start=False, stop=st),
                    reads=[rr.k(tt * 512, (tt + 1) * 512), env.c["nh_row_bf"].k()],
                    writes=[PK("psA", tt * 512, (tt + 1) * 512)])
            b.op("s", lambda E, tt=tt: E.activation(
                out=sig[tt].ap(), in_=psA.ap()[:, tt * 512:(tt + 1) * 512],
                func=AF.Sigmoid),
                reads=[PK("psA", tt * 512, (tt + 1) * 512)], writes=[sig[tt].k()])
            for dc in range(NDC):
                b.op("t", lambda E, tt=tt, dc=dc, qw2=qw2: E.matmul(
                    out=psB.ap()[:, tt * 512:(tt + 1) * 512],
                    lhsT=qw2.ap(dc * 128, (dc + 1) * 128),
                    rhs=xn.ap(dc * T + tt * 512, dc * T + (tt + 1) * 512),
                    start=(dc == 0), stop=False),
                    reads=[qw2.k(dc * 128, (dc + 1) * 128),
                           xn.k(dc * T + tt * 512, dc * T + (tt + 1) * 512)],
                    writes=[PK("psB", tt * 512, (tt + 1) * 512)])
            for rr in (rs_hi, rs_lo):
                b.op("t", lambda E, tt=tt, rr=rr, st=(rr is rs_lo): E.matmul(
                    out=psB.ap()[:, tt * 512:(tt + 1) * 512], lhsT=env.c["nh_row_bf"]._ap,
                    rhs=rr._ap[:, tt * 512:(tt + 1) * 512], start=False, stop=st),
                    reads=[rr.k(tt * 512, (tt + 1) * 512), env.c["nh_row_bf"].k()],
                    writes=[PK("psB", tt * 512, (tt + 1) * 512)])
            b.op("v", lambda E, hc=hc, tt=tt: E.tensor_tensor(
                out=hidt[hc].ap(tt * 512, (tt + 1) * 512),
                in0=psB.ap()[:, tt * 512:(tt + 1) * 512],
                in1=sig[tt].ap(), op=ALU.mult),
                reads=[PK("psB", tt * 512, (tt + 1) * 512), sig[tt].k()],
                writes=[hidt[hc].k(tt * 512, (tt + 1) * 512)])
            b.op("t", lambda E, hc=hc, tt=tt: E.matmul(
                out=psB.ap()[0:1, 1024 + tt * 512:1024 + (tt + 1) * 512],
                lhsT=env.c["ones_col_bf"].ap(),
                rhs=hidt[hc].ap(tt * 512, (tt + 1) * 512),
                start=(hc == 0), stop=(hc == NHC - 1)),
                reads=[hidt[hc].k(tt * 512, (tt + 1) * 512),
                       env.c["ones_col_bf"].k()],
                writes=[PK("psB", 1024 + tt * 512, 1024 + (tt + 1) * 512)])
    rsh_row = env.prow("big5", 24576)
    b.op("v", lambda E: E.tensor_copy(out=rsh_row._ap, in_=psB.ap()[0:1, 1024:2048]),
         reads=[PK("psB", 1024, 2048)], writes=[rsh_row.k()])
    rsh_hi = env.prow("big5", 28672, T, BF16)
    rsh_lo = env.prow("big5", 30720, T, BF16)
    b.op("v", lambda E: E.tensor_copy(out=rsh_hi._ap, in_=rsh_row._ap),
         reads=[rsh_row.k()], writes=[rsh_hi.k()])
    b.op("v", lambda E: E.scalar_tensor_tensor(out=rsh_lo._ap, in0=rsh_hi._ap,
                                               scalar=-1.0, in1=rsh_row._ap,
                                               op0=ALU.mult, op1=ALU.add),
         reads=[rsh_hi.k(), rsh_row.k()], writes=[rsh_lo.k()])
    mk(b, "l0f:stage1")
    mk(b, "l0f:stage2")
    th3 = emit_gamma_fin(b, env, "th3")
    w3i = [env.tv("big3", i * 2048, F32, 512) for i in range(4)]
    x1t = [env.tv("big3", 8192 + i * 2048, F32, 512) for i in range(8)]
    A3b = [env.tv("big5", 12288 + i * 1024, BF16, 512) for i in range(2)]
    B3b = [env.tv("big5", 14336 + i * 1024, BF16, 512) for i in range(2)]
    qw3b = [env.tv("big5", 2048 + i * 1024, BF16, 512) for i in range(2)]
    ostg = [env.tv("big5", 4096 + i * 2048, F32, 512) for i in range(4)]
    NIT = 4 * NHC

    def st2_load(it):
        grp, hc = it // NHC, it % NHC
        wbuf = w3i[it % 4]
        b.dma("s", lambda E, grp=grp, hc=hc, wbuf=wbuf: E.dma_start(
            out=wbuf.ap(),
            in_=env.dr["w3t"].ap()[hc * 128:(hc + 1) * 128, grp * 512:(grp + 1) * 512]),
            reads=[DK("w3t", hc * 128 * D * 4, (hc + 1) * 128 * D * 4)],
            writes=[wbuf.k()])

    def st2_quant(it):
        quantize_tile(b, env, w3i[it % 4], 0, 512, th3, A3b[it % 2], B3b[it % 2],
                      qw3b[it % 2], 0)

    for it in range(3):
        st2_load(it)
    st2_quant(0)
    for grp in range(4):
        for dcg in range(4):
            for tt in range(2):
                dco = grp * 4 + dcg
                xb_ = x1t[2 * dcg + tt]
                b.dma("y", lambda E, dco=dco, tt=tt, xb_=xb_: E.dma_start(
                    out=xb_.ap(),
                    in_=x_in.ap()[dco * 128:(dco + 1) * 128, tt * 512:(tt + 1) * 512]),
                    reads=[DK(x_in.name, dco * 128 * T * 4, (dco + 1) * 128 * T * 4)],
                    writes=[xb_.k()])
        for hc in range(NHC):
            it = grp * NHC + hc
            qw3 = qw3b[it % 2]
            if it + 1 < NIT:
                st2_quant(it + 1)
            if it + 3 < NIT:
                st2_load(it + 3)
            for dcg in range(4):
                ps = psA if dcg < 2 else psB
                pn = "psA" if dcg < 2 else "psB"
                for tt in range(2):
                    pcol = ((dcg % 2) * 2 + tt) * 512
                    b.op("t", lambda E, dcg=dcg, tt=tt, ps=ps, pcol=pcol, hc=hc, qw3=qw3: E.matmul(
                        out=ps.ap()[:, pcol:pcol + 512],
                        lhsT=qw3.ap(dcg * 128, (dcg + 1) * 128),
                        rhs=hidt[hc].ap(tt * 512, (tt + 1) * 512),
                        start=(hc == 0), stop=False),
                        reads=[qw3.k(dcg * 128, (dcg + 1) * 128),
                               hidt[hc].k(tt * 512, (tt + 1) * 512)],
                        writes=[PK(pn, pcol, pcol + 512)])
        for dcg in range(4):
            ps = psA if dcg < 2 else psB
            pn = "psA" if dcg < 2 else "psB"
            for tt in range(2):
                pcol = ((dcg % 2) * 2 + tt) * 512
                for rr in (rsh_hi, rsh_lo):
                    b.op("t", lambda E, dcg=dcg, tt=tt, ps=ps, pcol=pcol, rr=rr,
                         st=(rr is rsh_lo): E.matmul(
                        out=ps.ap()[:, pcol:pcol + 512], lhsT=env.c["nh_row_bf"]._ap,
                        rhs=rr._ap[:, tt * 512:(tt + 1) * 512], start=False, stop=st),
                        reads=[rr.k(tt * 512, (tt + 1) * 512), env.c["nh_row_bf"].k()],
                        writes=[PK(pn, pcol, pcol + 512)])
                dco = grp * 4 + dcg
                xb_ = x1t[2 * dcg + tt]
                ob_ = ostg[(2 * dcg + tt) % 4]
                b.op("v", lambda E, ps=ps, pcol=pcol, xb_=xb_, ob_=ob_: E.tensor_tensor(
                    out=ob_.ap(), in0=xb_.ap(), in1=ps.ap()[:, pcol:pcol + 512],
                    op=ALU.add),
                    reads=[xb_.k(), PK(pn, pcol, pcol + 512)], writes=[ob_.k()])
                b.dma("y", lambda E, dco=dco, tt=tt, ob_=ob_: E.dma_start(
                    out=x_out.ap()[dco * 128:(dco + 1) * 128, tt * 512:(tt + 1) * 512],
                    in_=ob_.ap()),
                    reads=[ob_.k()],
                    writes=[DK(x_out.name, dco * 128 * T * 4, (dco + 1) * 128 * T * 4)])


def emit_l1a(b, env, x_in, x_out):
    mk(b, "l1a:start")
    psA, psB = env.psA, env.psB
    xn = env.tv("big2", 0, BF16, NDC * T)
    rstd_row, mr_row = emit_ln(b, env, x_in, xn, keep_rows=True)
    x3pre = env.x3pre
    e1 = env.salloc(F32, NDC, "l1_e1")
    ccol = env.salloc(F32, NDC, "l1_ccol")
    b.op("s", lambda E: E.activation(out=e1.ap(), in_=env.c["lam_cols"].ap(),
                                     func=AF.Exp),
         reads=[env.c["lam_cols"].k()], writes=[e1.k()])
    b.op("s", lambda E: E.activation(out=e1.ap(), in_=e1.ap(), func=AF.Ln,
                                     bias=env.c["ones_col"].ap()),
         reads=[e1.k(), env.c["ones_col"].k()], writes=[e1.k()])
    b.op("v", lambda E: E.tensor_scalar(out=ccol.ap(), in0=e1.ap(), scalar1=-8.0,
                                        scalar2=None, op0=ALU.mult),
         reads=[e1.k()], writes=[ccol.k()])
    wgb = [env.tv("big3", i * 4096, BF16, D) for i in range(2)]
    wib = [env.tv("big3", 8192 + i * 4096, BF16, D) for i in range(2)]
    xt = [env.tv("big3", 16384 + i * 4096, F32, T) for i in range(2)]
    stg = [env.tv("big5", i * 4096, F32, T) for i in range(2)]
    srg = env.tv("big4", 0, F32, T)
    sigt = env.tv("big4", 4096, F32, T)
    alph = env.tv("big4", 8192, F32, T)
    ub = env.tv("big4", 12288, F32, T)
    betb = env.tv("big4", 16384, F32, T)
    xnf = env.tv("big4", 20480, F32, T)
    bx = env.tv("big4", 24576, F32, T)
    xb2 = env.tv("big4", 0, F32, T)
    hl = env.tv("big4", 12288, F32, T)
    zeros = env.tv("big4", 28672, F32, T)
    Acum = env.tv("big1", 0, BF16, NDC * T)
    A_cols = env.salloc(F32, NDC, "l1_acore")
    H_cols = env.salloc(F32, NDC, "l1_hcore")
    b.op("v", lambda E: E.memset(zeros.ap(), 0.0), writes=[zeros.k()])
    rstdb = env.tv("big5", 8192, F32, T)
    mrb = env.tv("big5", 12288, F32, T)
    for tt in range(2):
        b.op("t", lambda E, tt=tt: E.matmul(
            out=psA.ap()[:, tt * 512:(tt + 1) * 512], lhsT=env.c["ones_row"]._ap,
            rhs=rstd_row._ap[:, tt * 512:(tt + 1) * 512], start=True, stop=True),
            reads=[rstd_row.k(tt * 512, (tt + 1) * 512), env.c["ones_row"].k()],
            writes=[PK("psA", tt * 512, (tt + 1) * 512)])
        b.op("t", lambda E, tt=tt: E.matmul(
            out=psA.ap()[:, 1024 + tt * 512:1024 + (tt + 1) * 512],
            lhsT=env.c["ones_row"]._ap,
            rhs=mr_row._ap[:, tt * 512:(tt + 1) * 512], start=True, stop=True),
            reads=[mr_row.k(tt * 512, (tt + 1) * 512), env.c["ones_row"].k()],
            writes=[PK("psA", 1024 + tt * 512, 1024 + (tt + 1) * 512)])
    b.op("v", lambda E: E.tensor_copy(out=rstdb.ap(), in_=psA.ap()[:, 0:1024]),
         reads=[PK("psA", 0, 1024)], writes=[rstdb.k()])
    b.op("v", lambda E: E.tensor_copy(out=mrb.ap(), in_=psA.ap()[:, 1024:2048]),
         reads=[PK("psA", 1024, 2048)], writes=[mrb.k()])
    for ec in range(NDC):
        wg_, wi_, x_ = wgb[ec % 2], wib[ec % 2], xt[ec % 2]
        w_load(b, env, "wgat", ec * 128, 128, wg_, NDC, n_split=1)
        w_load(b, env, "wint", ec * 128, 128, wi_, NDC, n_split=1)
        b.dma("y", lambda E, ec=ec, x_=x_: E.dma_start(
            out=x_.ap(), in_=x_in.ap()[ec * 128:(ec + 1) * 128, :]),
            reads=[DK(x_in.name, ec * 128 * T * 4, (ec + 1) * 128 * T * 4)],
            writes=[x_.k()])
        for tt in range(2):
            for dc in range(NDC):
                b.op("t", lambda E, tt=tt, dc=dc, wg_=wg_: E.matmul(
                    out=psA.ap()[:, tt * 512:(tt + 1) * 512],
                    lhsT=wg_.ap(dc * 128, (dc + 1) * 128),
                    rhs=xn.ap(dc * T + tt * 512, dc * T + (tt + 1) * 512),
                    start=(dc == 0), stop=(dc == NDC - 1)),
                    reads=[wg_.k(dc * 128, (dc + 1) * 128),
                           xn.k(dc * T + tt * 512, dc * T + (tt + 1) * 512)],
                    writes=[PK("psA", tt * 512, (tt + 1) * 512)])
            for dc in range(NDC):
                b.op("t", lambda E, tt=tt, dc=dc, wi_=wi_: E.matmul(
                    out=psB.ap()[:, tt * 512:(tt + 1) * 512],
                    lhsT=wi_.ap(dc * 128, (dc + 1) * 128),
                    rhs=xn.ap(dc * T + tt * 512, dc * T + (tt + 1) * 512),
                    start=(dc == 0), stop=(dc == NDC - 1)),
                    reads=[wi_.k(dc * 128, (dc + 1) * 128),
                           xn.k(dc * T + tt * 512, dc * T + (tt + 1) * 512)],
                    writes=[PK("psB", tt * 512, (tt + 1) * 512)])
        b.op("s", lambda E, ec=ec: E.activation(
            out=srg.ap(), in_=psA.ap()[:, 0:1024], func=AF.Sigmoid,
            bias=env.c["bgat_cols"].ap(ec, ec + 1)),
            reads=[PK("psA", 0, 1024), env.c["bgat_cols"].k(ec, ec + 1)],
            writes=[srg.k()])
        b.op("s", lambda E, ec=ec: E.activation(
            out=sigt.ap(), in_=psB.ap()[:, 0:1024], func=AF.Sigmoid,
            bias=env.c["bin_cols"].ap(ec, ec + 1)),
            reads=[PK("psB", 0, 1024), env.c["bin_cols"].k(ec, ec + 1)],
            writes=[sigt.k()])
        b.op("s", lambda E, ec=ec: E.activation(
            out=alph.ap(), in_=srg.ap(), func=AF.Exp, scale=ccol.ap(ec, ec + 1)),
            reads=[srg.k(), ccol.k(ec, ec + 1)], writes=[alph.k()])
        b.op("s", lambda E: E.activation(out=srg.ap(), in_=alph.ap(), func=AF.Square),
             reads=[alph.k()], writes=[srg.k()])
        b.op("s", lambda E: E.activation(out=ub.ap(), in_=srg.ap(), func=AF.Ln,
                                         scale=-1.0, bias=env.c["eps1p_col"].ap()),
             reads=[srg.k(), env.c["eps1p_col"].k()], writes=[ub.k()])
        b.op("s", lambda E: E.activation(out=betb.ap(), in_=ub.ap(), func=AF.Exp,
                                         scale=0.5),
             reads=[ub.k()], writes=[betb.k()])
        b.op("v", lambda E, x_=x_: E.tensor_tensor(out=xnf.ap(), in0=x_.ap(),
                                                   in1=rstdb.ap(), op=ALU.mult),
             reads=[x_.k(), rstdb.k()], writes=[xnf.k()])
        b.op("v", lambda E: E.tensor_tensor(out=xnf.ap(), in0=xnf.ap(),
                                            in1=mrb.ap(), op=ALU.subtract),
             reads=[xnf.k(), mrb.k()], writes=[xnf.k()])
        b.op("v", lambda E: E.tensor_tensor(out=bx.ap(), in0=betb.ap(), in1=xnf.ap(),
                                            op=ALU.mult),
             reads=[betb.k(), xnf.k()], writes=[bx.k()])
        b.op("v", lambda E: E.tensor_tensor(out=xb2.ap(), in0=bx.ap(), in1=sigt.ap(),
                                            op=ALU.mult),
             reads=[bx.k(), sigt.k()], writes=[xb2.k()])
        b.op("v", lambda E: E.tensor_tensor_scan(out=hl.ap(), data0=alph.ap(),
                                                 data1=xb2.ap(), initial=0.0,
                                                 op0=ALU.mult, op1=ALU.add),
             reads=[alph.k(), xb2.k()], writes=[hl.k()])
        st_ = stg[ec % 2]
        b.op("v", lambda E, x_=x_, st_=st_: E.tensor_tensor(
            out=st_.ap(), in0=x_.ap(), in1=hl.ap(), op=ALU.add),
            reads=[x_.k(), hl.k()], writes=[st_.k()])
        b.dma("y", lambda E, ec=ec, st_=st_: E.dma_start(
            out=x3pre.ap()[ec * 128:(ec + 1) * 128, :], in_=st_.ap()),
            reads=[st_.k()],
            writes=[DK("x3pre", ec * 128 * T * 4, (ec + 1) * 128 * T * 4)])
        b.op("v", lambda E, ec=ec: E.tensor_tensor_scan(
            out=Acum.ap(ec * T, (ec + 1) * T), data0=alph.ap(), data1=zeros.ap(),
            initial=1.0, op0=ALU.mult, op1=ALU.add),
            reads=[alph.k(), zeros.k()], writes=[Acum.k(ec * T, (ec + 1) * T)])
        b.op("v", lambda E, ec=ec: E.tensor_reduce(
            out=A_cols.ap(ec, ec + 1), in_=alph.ap(), axis=AX.X, op=ALU.mult),
            reads=[alph.k()], writes=[A_cols.k(ec, ec + 1)])
        b.op("v", lambda E, ec=ec: E.tensor_copy(
            out=H_cols.ap(ec, ec + 1), in_=hl.ap(T - 1, T)),
            reads=[hl.k(T - 1, T)], writes=[H_cols.k(ec, ec + 1)])
    b.dma("p", lambda E: E.dma_start(out=env.dr["ag2_in"].ap()[:, 0:16],
                                     in_=A_cols.ap()),
          reads=[A_cols.k()], writes=[DK("ag2_in", 0, 64)])
    b.dma("p", lambda E: E.dma_start(out=env.dr["ag2_in"].ap()[:, 16:32],
                                     in_=H_cols.ap()),
          reads=[H_cols.k()], writes=[DK("ag2_in", 64, 128)])
    b.op("p", lambda E: E.collective_compute(
        "AllGather", ALU.bypass, replica_groups=[list(range(NCORE))],
        ins=[env.dr["ag2_in"].ap()], outs=[env.dr["ag2_out"].ap()]),
        reads=[DK("ag2_in")], writes=[DK("ag2_out")])
    agb = env.salloc(F32, 256, "l1_agb")
    b.dma("y", lambda E: E.dma_start(
        out=agb.ap().rearrange("p (r c) -> p r c", r=NCORE),
        in_=env.dr["ag2_out"].ap().rearrange("(r p) c -> p r c", p=128)),
        reads=[DK("ag2_out")], writes=[agb.k()])
    gcol = env.salloc(F32, NDC, "l1_hinit")
    aeff = env.salloc(F32, NDC, "l1_aeff")
    heff = env.salloc(F32, NDC, "l1_heff")
    b.op("v", lambda E: E.memset(gcol.ap(), 0.0), writes=[gcol.k()])
    for j in range(NCORE):
        b.op("v", lambda E, j=j: E.tensor_scalar(
            out=aeff.ap(), in0=agb.ap(j * 32, j * 32 + 16),
            scalar1=env.c["mcol"].ap(j, j + 1), scalar2=env.c["onem"].ap(j, j + 1),
            op0=ALU.mult, op1=ALU.add),
            reads=[agb.k(j * 32, j * 32 + 16), env.c["mcol"].k(j, j + 1),
                   env.c["onem"].k(j, j + 1)],
            writes=[aeff.k()])
        b.op("v", lambda E, j=j: E.tensor_scalar(
            out=heff.ap(), in0=agb.ap(j * 32 + 16, j * 32 + 32),
            scalar1=env.c["mcol"].ap(j, j + 1), scalar2=None, op0=ALU.mult),
            reads=[agb.k(j * 32 + 16, j * 32 + 32), env.c["mcol"].k(j, j + 1)],
            writes=[heff.k()])
        for ec in range(NDC):
            b.op("v", lambda E, ec=ec: E.scalar_tensor_tensor(
                out=gcol.ap(ec, ec + 1), in0=gcol.ap(ec, ec + 1),
                scalar=aeff.ap(ec, ec + 1), in1=heff.ap(ec, ec + 1),
                op0=ALU.mult, op1=ALU.add),
                reads=[gcol.k(ec, ec + 1), aeff.k(ec, ec + 1), heff.k(ec, ec + 1)],
                writes=[gcol.k(ec, ec + 1)])
    mk(b, "l1a:corr")
    for ec in range(NDC):
        x_ = xt[ec % 2]
        st_ = stg[ec % 2]
        b.dma("y", lambda E, ec=ec, x_=x_: E.dma_start(
            out=x_.ap(), in_=x3pre.ap()[ec * 128:(ec + 1) * 128, :]),
            reads=[DK("x3pre", ec * 128 * T * 4, (ec + 1) * 128 * T * 4)],
            writes=[x_.k()])
        b.op("v", lambda E, ec=ec, x_=x_, st_=st_: E.scalar_tensor_tensor(
            out=st_.ap(), in0=Acum.ap(ec * T, (ec + 1) * T),
            scalar=gcol.ap(ec, ec + 1), in1=x_.ap(), op0=ALU.mult, op1=ALU.add),
            reads=[Acum.k(ec * T, (ec + 1) * T), gcol.k(ec, ec + 1), x_.k()],
            writes=[st_.k()])
        b.dma("y", lambda E, ec=ec, st_=st_: E.dma_start(
            out=x_out.ap()[ec * 128:(ec + 1) * 128, :], in_=st_.ap()),
            reads=[st_.k()],
            writes=[DK(x_out.name, ec * 128 * T * 4, (ec + 1) * 128 * T * 4)])


def emit_l1f(b, env, x_in, x_out):
    mk(b, "l1f:start")
    psA, psB = env.psA, env.psB
    xn = env.tv("big2", 0, BF16, NDC * T)
    emit_ln(b, env, x_in, xn)
    fuse = env.tv("big1", 0, BF16, NOC * 512)
    wgb = [env.tv("big3", i * 4096, BF16, D) for i in range(2)]
    wub = [env.tv("big3", 8192 + i * 4096, BF16, D) for i in range(2)]
    wdb = [env.tv("big3", i * 16384, BF16, NOC * 128) for i in range(2)]
    sgt = [env.tv("big5", i * 2048, F32, 512) for i in range(2)]
    ut = [env.tv("big5", 4096 + i * 2048, F32, 512) for i in range(2)]
    xt = [env.tv("big5", 8192 + i * 2048, F32, 512) for i in range(2)]
    ostg = [env.tv("big5", 12288 + i * 2048, F32, 512) for i in range(2)]
    for th in range(2):
        for oc in range(NOC):
            wg_, wu_ = wgb[oc % 2], wub[oc % 2]
            w_load(b, env, "wg8", oc * 128, 128, wg_, NDC, n_split=1)
            w_load(b, env, "wu8", oc * 128, 128, wu_, NDC, n_split=1)
            for dc in range(NDC):
                b.op("t", lambda E, th=th, dc=dc, oc=oc, wg_=wg_: E.matmul(
                    out=psA.ap()[:, (oc % 2) * 512:(oc % 2) * 512 + 512],
                    lhsT=wg_.ap(dc * 128, (dc + 1) * 128),
                    rhs=xn.ap(dc * T + th * 512, dc * T + (th + 1) * 512),
                    start=(dc == 0), stop=(dc == NDC - 1)),
                    reads=[wg_.k(dc * 128, (dc + 1) * 128),
                           xn.k(dc * T + th * 512, dc * T + (th + 1) * 512)],
                    writes=[PK("psA", (oc % 2) * 512, (oc % 2) * 512 + 512)])
            b.op("s", lambda E, oc=oc: E.activation(
                out=sgt[oc % 2].ap(), in_=psA.ap()[:, (oc % 2) * 512:(oc % 2) * 512 + 512],
                func=AF.Silu, scale=env.c["sg_cols"].ap(oc, oc + 1)),
                reads=[PK("psA", (oc % 2) * 512, (oc % 2) * 512 + 512),
                       env.c["sg_cols"].k(oc, oc + 1)],
                writes=[sgt[oc % 2].k()])
            for dc in range(NDC):
                b.op("t", lambda E, th=th, dc=dc, oc=oc, wu_=wu_: E.matmul(
                    out=psB.ap()[:, (oc % 2) * 512:(oc % 2) * 512 + 512],
                    lhsT=wu_.ap(dc * 128, (dc + 1) * 128),
                    rhs=xn.ap(dc * T + th * 512, dc * T + (th + 1) * 512),
                    start=(dc == 0), stop=(dc == NDC - 1)),
                    reads=[wu_.k(dc * 128, (dc + 1) * 128),
                           xn.k(dc * T + th * 512, dc * T + (th + 1) * 512)],
                    writes=[PK("psB", (oc % 2) * 512, (oc % 2) * 512 + 512)])
            b.op("v", lambda E, oc=oc: E.tensor_scalar(
                out=ut[oc % 2].ap(), in0=psB.ap()[:, (oc % 2) * 512:(oc % 2) * 512 + 512],
                scalar1=env.c["su_cols"].ap(oc, oc + 1), scalar2=None, op0=ALU.mult),
                reads=[PK("psB", (oc % 2) * 512, (oc % 2) * 512 + 512),
                       env.c["su_cols"].k(oc, oc + 1)],
                writes=[ut[oc % 2].k()])
            b.op("v", lambda E, oc=oc: E.tensor_tensor(
                out=fuse.ap(oc * 512, (oc + 1) * 512), in0=sgt[oc % 2].ap(),
                in1=ut[oc % 2].ap(), op=ALU.mult),
                reads=[sgt[oc % 2].k(), ut[oc % 2].k()],
                writes=[fuse.k(oc * 512, (oc + 1) * 512)])
        for dco in range(NDC):
            wd_ = wdb[dco % 2]
            w_load(b, env, "wd8", dco * 128, 128, wd_, NOC, n_split=1)
            for oc in range(NOC):
                b.op("t", lambda E, dco=dco, oc=oc, wd_=wd_: E.matmul(
                    out=psA.ap()[:, (dco % 4) * 512:(dco % 4) * 512 + 512],
                    lhsT=wd_.ap(oc * 128, (oc + 1) * 128),
                    rhs=fuse.ap(oc * 512, (oc + 1) * 512),
                    start=(oc == 0), stop=(oc == NOC - 1)),
                    reads=[wd_.k(oc * 128, (oc + 1) * 128),
                           fuse.k(oc * 512, (oc + 1) * 512)],
                    writes=[PK("psA", (dco % 4) * 512, (dco % 4) * 512 + 512)])
            x_ = xt[dco % 2]
            o_ = ostg[dco % 2]
            b.dma("y", lambda E, dco=dco, th=th, x_=x_: E.dma_start(
                out=x_.ap(),
                in_=x_in.ap()[dco * 128:(dco + 1) * 128, th * 512:(th + 1) * 512]),
                reads=[DK(x_in.name, dco * 128 * T * 4, (dco + 1) * 128 * T * 4)],
                writes=[x_.k()])
            b.op("v", lambda E, dco=dco, x_=x_, o_=o_: E.scalar_tensor_tensor(
                out=o_.ap(), in0=psA.ap()[:, (dco % 4) * 512:(dco % 4) * 512 + 512],
                scalar=env.c["sd_cols"].ap(dco, dco + 1), in1=x_.ap(),
                op0=ALU.mult, op1=ALU.add),
                reads=[PK("psA", (dco % 4) * 512, (dco % 4) * 512 + 512),
                       env.c["sd_cols"].k(dco, dco + 1), x_.k()],
                writes=[o_.k()])
            b.dma("y", lambda E, dco=dco, th=th, o_=o_: E.dma_start(
                out=x_out.ap()[dco * 128:(dco + 1) * 128, th * 512:(th + 1) * 512],
                in_=o_.ap()),
                reads=[o_.k()],
                writes=[DK(x_out.name, dco * 128 * T * 4, (dco + 1) * 128 * T * 4)])


L0A_STAGE = 3

EMIT = {"l0a": emit_l0a, "l0f": emit_l0f, "l1a": emit_l1a, "l1f": emit_l1f}
GROUPS = [["l0a", "l0f", "l1a", "l1f"]]

_prog_cache = {}


def build_program(blocks):
    key = tuple(blocks)
    if key in _prog_cache:
        return _prog_cache[key]
    nc = bass.Bass("TRN2", target_bir_lowering=False)
    env = make_env(nc, blocks, len(blocks) == 4)
    b = B(nc)
    load_small_consts(b, env)
    if "l0f" in blocks:
        from collections import deque
        env.gpump = deque()
        env.gsize = {}
        emit_gamma_stream(b, env, "w1t", "th1")
        emit_gamma_stream(b, env, "w2t", "th2")
        emit_gamma_stream(b, env, "w3t", "th3")
    else:
        from collections import deque
        env.gpump = deque()
    for i, blk in enumerate(blocks):
        EMIT[blk](b, env, env.xbuf[i], env.xbuf[i + 1])
    b.finish()
    b.build()
    _prog_cache[key] = (nc, env)
    return nc, env


NEED = {
    "common": ["permpm", "permpm_bf", "idbf", "ones_col", "ones_col_bf", "ones_row",
               "nh_row", "nh_row_bf", "ccr", "eps1p_col"],
    "l0a": ["cosf", "sinf", "dt_tab", "dq_tab", "dk_tab", "lamc_tab", "coefc",
            "wq", "wk", "wv", "wo", "wg1", "wg2", "nw_cols"],
    "l0f": ["w1t", "w2t", "w3t"],
    "l1a": ["wint", "wgat", "bin_cols", "bgat_cols", "lam_cols", "mcol", "onem"],
    "l1f": ["wg8", "wu8", "wd8", "sg_cols", "su_cols", "sd_cols"],
}
PER_CORE = {"cosf", "sinf", "coefc", "mcol", "onem"}


def _prep(inputs):
    f32 = np.float32
    g = {}
    g["wq"] = np.ascontiguousarray(np.asarray(inputs["L0_attn_Wq"], f32))
    g["wk"] = np.ascontiguousarray(np.asarray(inputs["L0_attn_Wk"], f32))
    g["wv"] = np.ascontiguousarray(np.asarray(inputs["L0_attn_Wv"], f32))
    g["wo"] = np.ascontiguousarray(np.asarray(inputs["L0_attn_Wo"], f32))
    g["wg1"] = np.ascontiguousarray(np.asarray(inputs["L0_attn_Wg1"], f32))
    g["wg2"] = np.ascontiguousarray(np.asarray(inputs["L0_attn_Wg2"], f32))
    g["nw_cols"] = _cols(np.asarray(inputs["L0_attn_norm_weight"], f32))
    g["w1t"] = np.ascontiguousarray(np.asarray(inputs["L0_ffn_w1"], f32).T)
    g["w2t"] = np.ascontiguousarray(np.asarray(inputs["L0_ffn_w2"], f32).T)
    g["w3t"] = np.ascontiguousarray(np.asarray(inputs["L0_ffn_w3"], f32).T)
    g["wint"] = np.ascontiguousarray(np.asarray(inputs["L1_attn_W_in"], f32).T)
    g["wgat"] = np.ascontiguousarray(np.asarray(inputs["L1_attn_W_gate"], f32).T)
    g["bin_cols"] = _cols(np.asarray(inputs["L1_attn_b_in"], f32))
    g["bgat_cols"] = _cols(np.asarray(inputs["L1_attn_b_gate"], f32))
    g["lam_cols"] = _cols(np.asarray(inputs["L1_attn_forget_lambda"], f32))
    g["wg8"] = np.ascontiguousarray(np.asarray(inputs["L1_ffn_w_gate"], np.int8))
    g["wu8"] = np.ascontiguousarray(np.asarray(inputs["L1_ffn_w_up"], np.int8))
    g["wd8"] = np.ascontiguousarray(np.asarray(inputs["L1_ffn_w_down"], np.int8))
    g["sg_cols"] = _cols(np.asarray(inputs["L1_ffn_s_gate"], f32))
    g["su_cols"] = _cols(np.asarray(inputs["L1_ffn_s_up"], f32))
    g["sd_cols"] = _cols(np.asarray(inputs["L1_ffn_s_down"], f32))
    g["ones_col"] = np.ones((128, 1), f32)
    g["ones_col_bf"] = np.ones((128, 1), ml_dtypes.bfloat16)
    g["ones_row"] = np.ones((1, 128), f32)
    g["nh_row"] = np.full((1, 128), -0.5, f32)
    g["nh_row_bf"] = np.full((1, 128), -0.5, ml_dtypes.bfloat16)
    g["permpm_bf"] = None
    ccr = np.zeros((1, 8), f32)
    ccr[0, 0] = 1e-5
    ccr[0, 1] = 0.5e-8
    g["ccr"] = ccr
    g["eps1p_col"] = np.full((128, 1), f32(1.0) + f32(1e-6), f32)
    x = np.asarray(inputs["L0_attn_x"], f32)[0]
    percore = []
    for c in range(NCORE):
        hc = _host_const(c)
        d = {"cosf": hc["cosf"], "sinf": hc["sinf"], "coefc": hc["coefc"],
             "mcol": hc["mcol"], "onem": hc["onem"]}
        d["x_fm"] = np.ascontiguousarray(x[c * T:(c + 1) * T, :].T)
        percore.append(d)
    g["permpm"] = hc["permpm"]
    g["permpm_bf"] = hc["permpm"].astype(ml_dtypes.bfloat16)
    g["idbf"] = hc["idbf"]
    g["dt_tab"] = hc["dt_tab"]
    g["dq_tab"] = hc["dq_tab"]
    g["dk_tab"] = hc["dk_tab"]
    g["lamc_tab"] = hc["lamc_tab"]
    return g, percore


def kernel(**inputs):
    g, percore = _prep(inputs)
    x_cur = [percore[c]["x_fm"] for c in range(NCORE)]
    for blocks in GROUPS:
        nc, env = build_program(blocks)
        names = list(NEED["common"])
        for blk in blocks:
            names += NEED[blk]
        in_maps = []
        for c in range(NCORE):
            m = {}
            for nm in names:
                m[nm] = percore[c][nm] if nm in PER_CORE else g[nm]
            m["x_fm"] = x_cur[c]
            in_maps.append(m)
        res = run_bass_kernel_spmd(nc, in_maps, core_ids=list(range(NCORE)))
        x_cur = [np.asarray(res.results[c]["y_fm"]) for c in range(NCORE)]
    out = np.empty((1, SEQ, D), np.float32)
    for c in range(NCORE):
        out[0, c * T:(c + 1) * T, :] = x_cur[c].T
    return out


# version 16
# speedup vs baseline: 1.0351x; 1.0351x over previous
import math
import numpy as np
import ml_dtypes

import concourse.bass as bass
import concourse.mybir as mybir
from concourse.bass_utils import run_bass_kernel_spmd

F32 = mybir.dt.float32
BF16 = mybir.dt.bfloat16
I8 = mybir.dt.int8
AF = mybir.ActivationFunctionType
ALU = mybir.AluOpType
AX = mybir.AxisListType

NCORE = 8
T = 1024
D = 2048
NDC = D // 128
H = 16
HD = 128
C = 128
NTC = T // C
FF = 5632
NHC = FF // 128
INTER = 8192
NOC = INTER // 128
SEQ = 8192

MB = 8192
MARKS = None


def mk(b, name):
    if MARKS is not None:
        MARKS.append((name, {e: len(b.rec[e]) for e in ENGS}))


def _f32(x):
    return np.asarray(x, np.float32)


def _host_const(core):
    f32 = np.float32
    t0 = core * T
    half = HD // 2
    theta = np.power(f32(10000.0), (f32(-2.0) * np.arange(half, dtype=f32) / f32(HD))).astype(f32)
    pos = np.arange(t0, t0 + T, dtype=f32)
    ang = (pos[:, None] * theta[None, :]).astype(f32)
    cos = np.cos(ang).astype(f32)
    sin = np.sin(ang).astype(f32)
    cosf = np.zeros((128, T), f32)
    sinf = np.zeros((128, T), f32)
    cosf[:64, :] = cos.T
    cosf[64:, :] = cos.T
    sinf[:64, :] = -sin.T
    sinf[64:, :] = sin.T
    hh = np.arange(H, dtype=f32)
    slopes = np.power(f32(2.0), (f32(-8.0) * (hh + f32(1.0)) / f32(H))).astype(f32)
    llam = (-slopes).astype(f32)
    i = np.arange(C, dtype=f32)
    diff = i[:, None] - i[None, :]
    dt_tab = np.zeros((128, D), f32)
    dq_tab = np.zeros((128, D), f32)
    dk_tab = np.zeros((128, D), f32)
    lamc_tab = np.zeros((128, D), f32)
    for h in range(H):
        Dh = np.where(diff >= 0, np.exp((llam[h] * diff).astype(f32)).astype(f32), f32(0.0))
        dt_tab[:, h * 128:(h + 1) * 128] = Dh.T
        dq_tab[:, h * 128:(h + 1) * 128] = np.exp((llam[h] * (i + 1)).astype(f32)).astype(f32)[None, :]
        dk_tab[:, h * 128:(h + 1) * 128] = np.exp((llam[h] * (C - 1 - i)).astype(f32)).astype(f32)[:, None]
        lamc_tab[:, h * 128:(h + 1) * 128] = np.exp(f32(llam[h] * C))
    coefc = np.zeros((128, 128), f32)
    for j in range(core):
        p = core - 1 - j
        for h in range(H):
            coefc[:, j * 16 + h] = f32(np.exp(np.float64(llam[h]) * C * NTC * p))
    permpm = np.zeros((128, 128), f32)
    for e in range(128):
        permpm[(e + 64) % 128, e] = 1.0
    idbf = np.eye(128).astype(ml_dtypes.bfloat16)
    mcol = np.zeros((128, 8), f32)
    mcol[:, :core] = 1.0
    onem = 1.0 - mcol
    def _fl(a):
        return np.where(np.abs(a) < 1e-30, f32(0.0), a).astype(f32)

    return dict(cosf=cosf, sinf=sinf, dt_tab=_fl(dt_tab), dq_tab=_fl(dq_tab),
                dk_tab=_fl(dk_tab), lamc_tab=_fl(lamc_tab), coefc=_fl(coefc),
                permpm=permpm, idbf=idbf, mcol=mcol, onem=onem)


def _cols(v, n=128):
    v = np.asarray(v)
    k = v.size // n
    return np.ascontiguousarray(v.reshape(k, n).T)


ENGS = ("t", "v", "s", "p", "y")


DMA_K = 12


class B:
    def __init__(self, nc):
        self.nc = nc
        self.Q = {}
        self.cnt = {}
        self.rec = {e: [] for e in ENGS}
        self.writers = {}
        self.readers = {}
        self.waited = {}
        self.dma_rr = {}

    def sem(self, q):
        if q not in self.Q:
            self.Q[q] = self.nc.alloc_semaphore(name="sm_" + q)
            self.cnt[q] = 0
        return self.Q[q]

    def _need(self, e, deps):
        best = {}
        for (q, v) in deps:
            if e == "t" and q == "t":
                continue
            if v > best.get(q, -1):
                best[q] = v
        for q, v in best.items():
            if self.waited.get((e, q), 0) >= v:
                continue
            self.rec[e].append(("w", q, v))
            self.waited[(e, q)] = v

    def _dep_read(self, keys, eng=None):
        deps = []
        for (sp, lo, hi) in keys:
            for (l, h, q, c) in self.writers.get(sp, ()):
                if l < hi and lo < h:
                    deps.append((q, c))
            if sp.startswith("ps"):
                for (l, h, q, c) in self.readers.get(sp, ()):
                    if l < hi and lo < h and q != eng:
                        deps.append((q, c))
        return deps

    def _dep_write(self, keys):
        deps = []
        for (sp, lo, hi) in keys:
            for lst in (self.writers.get(sp, ()), self.readers.get(sp, ())):
                for (l, h, q, c) in lst:
                    if l < hi and lo < h:
                        deps.append((q, c))
        return deps

    def _register(self, q, c, reads, writes):
        for (sp, lo, hi) in reads:
            self.readers.setdefault(sp, []).append((lo, hi, q, c))
        for (sp, lo, hi) in writes:
            w = self.writers.setdefault(sp, [])
            self.writers[sp] = [e for e in w if not (lo <= e[0] and e[1] <= hi)]
            r = self.readers.get(sp)
            if r:
                self.readers[sp] = [e for e in r if not (lo <= e[0] and e[1] <= hi)]
            self.writers[sp].append((lo, hi, q, c))

    def _emit(self, e, q, inc, fn, reads, writes):
        self.sem(q)
        deps = self._dep_read(reads, e) + self._dep_write(writes)
        self._need(e, deps)
        self.cnt[q] += inc
        self.rec[e].append(("o", fn, q, inc))
        self._register(q, self.cnt[q], reads, writes)

    def op(self, e, fn, reads=(), writes=()):
        self._emit(e, e, 1, fn, reads, writes)

    def dma(self, e, fn, reads=(), writes=()):
        qb = e + "d"
        idx = self.dma_rr.get(qb, 0)
        self.dma_rr[qb] = (idx + 1) % DMA_K
        q = "%s%02d" % (qb, idx)
        self.sem(q)
        deps = self._dep_read(reads, e) + self._dep_write(writes)
        if self.cnt[q]:
            deps.append((q, self.cnt[q]))
        self._need(e, deps)
        self.cnt[q] += 16
        self.rec[e].append(("o", fn, q, 16))
        self._register(q, self.cnt[q], reads, writes)

    def finish(self):
        for q, n in self.cnt.items():
            e = q[0]
            assert e in ENGS
            if self.waited.get((e, q), 0) < n:
                self.rec[e].append(("w", q, n))
                self.waited[(e, q)] = n
        for q, n in self.cnt.items():
            if self.waited.get(("y", q), 0) < n:
                self.rec["y"].append(("w", q, n))
                self.waited[("y", q)] = n

    def _batch(self, items):
        out = []
        run = []

        def flush():
            last = {}
            for idx, it in enumerate(run):
                q = it[2]
                if len(q) == 1:
                    last[q] = idx
            pend = {}
            for idx, it in enumerate(run):
                q = it[2]
                if len(q) == 1:
                    pend[q] = pend.get(q, 0) + it[3]
                    if idx == last[q]:
                        out.append(("o", it[1], q, pend[q]))
                        pend[q] = 0
                    else:
                        out.append(("o", it[1], None, 0))
                else:
                    out.append(it)
            run.clear()

        for it in items:
            if it[0] == "w":
                flush()
                out.append(it)
            else:
                run.append(it)
        flush()
        return out

    def build(self):
        with self.nc.Block() as block:
            emitters = {"t": block.tensor, "v": block.vector, "s": block.scalar,
                        "p": block.gpsimd, "y": block.sync}
            for e in ENGS:
                items = self.rec[e]
                if not items:
                    continue

                def body(eng, items=items):
                    for it in items:
                        if it[0] == "w":
                            eng.wait_ge(self.Q[it[1]], it[2])
                        else:
                            inst = it[1](eng)
                            if it[2] is not None:
                                inst.then_inc(self.Q[it[2]], it[3])

                emitters[e](body)


class TV:

    def __init__(self, pools, pool, off, dtype, f, p0=0, p1=128):
        self.pool = pool
        self.off = off
        self.dtype = dtype
        self.f = f
        es = mybir.dt.size(dtype)
        self.es = es
        self._ap = pools[pool].ap()[p0:p1, off:off + f * es].bitcast(dtype)

    def ap(self, lo=0, hi=None):
        hi = self.f if hi is None else hi
        return self._ap[:, lo:hi]

    def k(self, lo=0, hi=None):
        hi = self.f if hi is None else hi
        return (self.pool, self.off + lo * self.es, self.off + hi * self.es)


def PK(name, col_lo, col_hi):
    lo = (col_lo * 4 // 2048) * 2048
    hi = -((-(col_hi * 4)) // 2048) * 2048
    return (name, lo, hi)


def DK(name, lo=0, hi=1 << 62):
    return (name, lo, hi)


class Env:
    pass


def make_env(nc, blocks, fused):
    env = Env()
    env.nc = nc
    env.blocks = blocks
    dr = {}
    env.dr = dr

    def din(name, shape, dtype=F32):
        dr[name] = nc.dram_tensor(name, list(shape), dtype, kind="ExternalInput")
        return dr[name]

    def dint(name, shape, dtype=F32):
        dr[name] = nc.dram_tensor(name, list(shape), dtype, kind="Internal")
        return dr[name]

    def dout(name, shape, dtype=F32):
        dr[name] = nc.dram_tensor(name, list(shape), dtype, kind="ExternalOutput")
        return dr[name]

    env.din, env.dint, env.dout = din, dint, dout

    din("permpm", (128, 128))
    din("idbf", (128, 128), BF16)
    din("ones_col", (128, 1))
    din("ones_col_bf", (128, 1), BF16)
    din("ones_row", (1, 128))
    din("nh_row", (1, 128))
    din("nh_row_bf", (1, 128), BF16)
    din("permpm_bf", (128, 128), BF16)
    din("ccr", (1, 8))
    din("eps1p_col", (128, 1))
    if "l0a" in blocks:
        din("cosf", (128, T)); din("sinf", (128, T))
        din("dt_tab", (128, D)); din("dq_tab", (128, D)); din("dk_tab", (128, D))
        din("lamc_tab", (128, D)); din("coefc", (128, 128))
        din("wq", (D, D)); din("wk", (D, D)); din("wv", (D, D)); din("wo", (D, D))
        din("wg1", (D, 128)); din("wg2", (128, D)); din("nw_cols", (128, NDC))
        dint("qrot_d", (D, T), BF16); dint("krot_d", (D, T), BF16)
        dint("v_d", (T, D), BF16); dint("vdk_d", (T, D), BF16)
        dint("ag1_in", (128, D)); dint("ag1_out", (NCORE * 128, D))
    if "l0f" in blocks:
        din("w1t", (D, FF)); din("w2t", (D, FF)); din("w3t", (FF, D))
    if "l1a" in blocks:
        din("wint", (D, D)); din("wgat", (D, D))
        din("bin_cols", (128, NDC)); din("bgat_cols", (128, NDC))
        din("lam_cols", (128, NDC))
        din("mcol", (128, 8)); din("onem", (128, 8))
        dint("ag2_in", (128, 32)); dint("ag2_out", (NCORE * 128, 32))
    if "l1f" in blocks:
        din("wg8", (D, INTER), I8); din("wu8", (D, INTER), I8); din("wd8", (INTER, D), I8)
        din("sg_cols", (128, NOC)); din("su_cols", (128, NOC)); din("sd_cols", (128, NDC))

    env.xbuf = []
    for i in range(len(blocks) + 1):
        if i == 0:
            env.xbuf.append(din("x_fm", (D, T)))
        elif i == len(blocks):
            env.xbuf.append(dout("y_fm", (D, T)))
        else:
            env.xbuf.append(dint(f"xi{i}", (D, T)))
    if "l1a" in blocks:
        env.x3pre = dint("x3pre", (D, T))

    pools = {}
    for name, nb in (("big1", 65536), ("big2", 32768), ("big3", 32768),
                     ("big4", 32768), ("big5", 38912)):
        pools[name] = nc.alloc_sbuf_tensor(name, [128, nb], mybir.dt.uint8)
    env.pools = pools
    env.tv = lambda pool, off, dtype, f, p0=0, p1=128: TV(pools, pool, off, dtype, f, p0, p1)

    env.sm = nc.alloc_sbuf_tensor("smalls", [128, 6144], mybir.dt.uint8)
    env.sm_off = [0]

    env.sm_cache = {}

    def salloc(dtype, f, name):
        if name in env.sm_cache:
            return env.sm_cache[name]
        es = mybir.dt.size(dtype)
        off = (env.sm_off[0] + 31) // 32 * 32
        env.sm_off[0] = off + f * es
        assert env.sm_off[0] <= 6144, "smalls overflow"
        tv = TV({"smalls": env.sm}, "smalls", off, dtype, f)
        env.sm_cache[name] = tv
        return tv

    env.salloc = salloc

    env.rw = nc.alloc_sbuf_tensor("rowsb", [1, 2048], mybir.dt.uint8)
    env.rw_off = [0]
    env.rw_cache = {}

    def row(name, f=T, dtype=F32):
        if name in env.rw_cache:
            return env.rw_cache[name]
        es = mybir.dt.size(dtype)
        off = (env.rw_off[0] + 31) // 32 * 32
        env.rw_off[0] = off + f * es
        assert env.rw_off[0] <= 2048, "rowsb overflow"
        tv = TV({"rowsb": env.rw}, "rowsb", off, dtype, f, 0, 1)
        env.rw_cache[name] = tv
        return tv

    env.row = row

    def prow(pool, off, f=T, dtype=F32):
        return TV(pools, pool, off, dtype, f, 0, 1)

    env.prow = prow

    env.psA = nc.alloc_psum_tensor("psA", [128, 2048], F32)
    env.psB = nc.alloc_psum_tensor("psB", [128, 2048], F32)

    env.c = {}
    return env


def load_small_consts(b, env):
    names = [("permpm", F32, 128), ("permpm_bf", BF16, 128), ("idbf", BF16, 128), ("ones_col", F32, 1),
             ("ones_col_bf", BF16, 1), ("eps1p_col", F32, 1)]
    if "l0a" in env.blocks:
        names += [("nw_cols", F32, NDC), ("coefc", F32, 128)]
    if "l1a" in env.blocks:
        names += [("bin_cols", F32, NDC), ("bgat_cols", F32, NDC),
                  ("lam_cols", F32, NDC), ("mcol", F32, 8), ("onem", F32, 8)]
    if "l1f" in env.blocks:
        names += [("sg_cols", F32, NOC), ("su_cols", F32, NOC), ("sd_cols", F32, NDC)]
    for (nm, dt_, f) in names:
        tv = env.salloc(dt_, f, nm)
        b.dma("y", lambda E, tv=tv, nm=nm: E.dma_start(out=tv.ap(), in_=env.dr[nm].ap()),
              reads=[DK(nm)], writes=[tv.k()])
        env.c[nm] = tv
    for nm, f, dt_ in (("ones_row", 128, F32), ("nh_row", 128, F32),
                       ("nh_row_bf", 128, BF16), ("ccr", 8, F32)):
        tv = env.row(nm, f, dt_)
        b.dma("y", lambda E, tv=tv, nm=nm: E.dma_start(out=tv._ap, in_=env.dr[nm].ap()),
              reads=[DK(nm)], writes=[tv.k()])
        env.c[nm] = tv


def w_load(b, env, dram_name, col_lo, ncols, dest, nrow_chunks, dcol_off=0,
           n_split=4):
    n_split = min(n_split, nrow_chunks)
    per = nrow_chunks // n_split
    for sp in range(n_split):
        a0, a1 = sp * per, (sp + 1) * per if sp < n_split - 1 else nrow_chunks
        src = env.dr[dram_name].ap()[a0 * 128:a1 * 128,
                                     col_lo:col_lo + ncols].rearrange(
            "(a p) c -> p a c", p=128)
        dst = dest.ap(dcol_off + a0 * ncols,
                      dcol_off + a1 * ncols).rearrange(
            "p (a c) -> p a c", a=a1 - a0)
        b.dma("p", lambda E, src=src, dst=dst: E.dma_start(out=dst, in_=src),
              reads=[DK(dram_name)],
              writes=[dest.k(dcol_off + a0 * ncols, dcol_off + a1 * ncols)])


def newton_rsqrt(b, env, u_tv, lo, hi, out_tv, olo, scr1, scr2, iters=3):
    n = hi - lo
    I32 = mybir.dt.int32
    y = out_tv
    b.op("v", lambda E: E.tensor_scalar(out=y.ap(olo, olo + n).bitcast(I32),
                                        in0=u_tv.ap(lo, hi).bitcast(I32),
                                        scalar1=1, scalar2=None,
                                        op0=ALU.logical_shift_right),
         reads=[u_tv.k(lo, hi)], writes=[y.k(olo, olo + n)])
    b.op("v", lambda E: E.tensor_scalar(out=y.ap(olo, olo + n).bitcast(I32),
                                        in0=y.ap(olo, olo + n).bitcast(I32),
                                        scalar1=0x5f3759df, scalar2=-1,
                                        op0=ALU.subtract, op1=ALU.mult),
         reads=[y.k(olo, olo + n)], writes=[y.k(olo, olo + n)])
    for _ in range(iters):
        b.op("v", lambda E: E.tensor_tensor(out=scr1.ap(0, n), in0=y.ap(olo, olo + n),
                                            in1=y.ap(olo, olo + n), op=ALU.mult),
             reads=[y.k(olo, olo + n)], writes=[scr1.k(0, n)])
        b.op("v", lambda E: E.scalar_tensor_tensor(out=scr2.ap(0, n), in0=scr1.ap(0, n),
                                                   scalar=-0.5, in1=u_tv.ap(lo, hi),
                                                   op0=ALU.mult, op1=ALU.mult),
             reads=[scr1.k(0, n), u_tv.k(lo, hi)], writes=[scr2.k(0, n)])
        b.op("v", lambda E: E.tensor_scalar(out=scr2.ap(0, n), in0=scr2.ap(0, n),
                                            scalar1=1.5, scalar2=None, op0=ALU.add),
             reads=[scr2.k(0, n)], writes=[scr2.k(0, n)])
        b.op("v", lambda E: E.tensor_tensor(out=y.ap(olo, olo + n), in0=y.ap(olo, olo + n),
                                            in1=scr2.ap(0, n), op=ALU.mult),
             reads=[y.k(olo, olo + n), scr2.k(0, n)], writes=[y.k(olo, olo + n)])


def emit_ln(b, env, x_dram, xn, keep_rows=False):
    nc = env.nc
    xk = x_dram.name
    xres = env.tv("big1", 0, F32, NDC * T)
    sq = [env.tv("big5", i * (T * 2), BF16, T) for i in range(2)]
    xbb = [env.tv("big5", 28672 + i * (T * 2), BF16, T) for i in range(2)]
    tmp = env.tv("big5", 4096, F32, T)
    psA, psB = env.psA, env.psB

    for dc in range(NDC):
        b.dma("y" if dc % 2 == 0 else "p", lambda E, dc=dc: E.dma_start(
            out=xres.ap(dc * T, (dc + 1) * T),
            in_=x_dram.ap()[dc * 128:(dc + 1) * 128, :]),
            reads=[DK(xk, dc * 128 * T * 4, (dc + 1) * 128 * T * 4)],
            writes=[xres.k(dc * T, (dc + 1) * T)])
    for dc in range(NDC):
        s = sq[dc % 2]
        xb_ = xbb[dc % 2]
        b.op("s", lambda E, dc=dc, s=s: E.activation(
            out=s.ap(), in_=xres.ap(dc * T, (dc + 1) * T), func=AF.Square),
            reads=[xres.k(dc * T, (dc + 1) * T)], writes=[s.k()])
        b.op("v", lambda E, dc=dc, xb_=xb_: E.tensor_copy(
            out=xb_.ap(), in_=xres.ap(dc * T, (dc + 1) * T)),
            reads=[xres.k(dc * T, (dc + 1) * T)], writes=[xb_.k()])
        for tt in range(2):
            b.op("t", lambda E, dc=dc, tt=tt, xb_=xb_: E.matmul(
                out=psA.ap()[0:1, tt * 512:(tt + 1) * 512],
                lhsT=env.c["ones_col_bf"].ap(),
                rhs=xb_.ap(tt * 512, (tt + 1) * 512),
                start=(dc == 0), stop=(dc == NDC - 1)),
                reads=[xb_.k(tt * 512, (tt + 1) * 512),
                       env.c["ones_col_bf"].k()],
                writes=[PK("psA", tt * 512, (tt + 1) * 512)])
            b.op("t", lambda E, dc=dc, tt=tt, s=s: E.matmul(
                out=psA.ap()[0:1, 1024 + tt * 512:1024 + (tt + 1) * 512],
                lhsT=env.c["ones_col_bf"].ap(),
                rhs=s.ap(tt * 512, (tt + 1) * 512),
                start=(dc == 0), stop=(dc == NDC - 1)),
                reads=[s.k(tt * 512, (tt + 1) * 512), env.c["ones_col_bf"].k()],
                writes=[PK("psA", 1024 + tt * 512, 1024 + (tt + 1) * 512)])
    m_row = env.prow("big5", 8192)
    v_row = env.prow("big5", 12288)
    mm_row = env.prow("big5", 16384)
    rstd_row = env.prow("big5", 20480)
    mr_row = env.prow("big5", 24576)
    b.op("v", lambda E: E.tensor_scalar(out=m_row._ap, in0=psA.ap()[0:1, 0:1024],
                                        scalar1=1.0 / D, scalar2=None, op0=ALU.mult),
         reads=[PK("psA", 0, 1024)], writes=[m_row.k()])
    b.op("v", lambda E: E.tensor_scalar(out=v_row._ap, in0=psA.ap()[0:1, 1024:2048],
                                        scalar1=1.0 / D, scalar2=None, op0=ALU.mult),
         reads=[PK("psA", 1024, 2048)], writes=[v_row.k()])
    b.op("v", lambda E: E.tensor_tensor(out=mm_row._ap, in0=m_row._ap, in1=m_row._ap,
                                        op=ALU.mult),
         reads=[m_row.k()], writes=[mm_row.k()])
    b.op("v", lambda E: E.tensor_tensor(out=v_row._ap, in0=v_row._ap, in1=mm_row._ap,
                                        op=ALU.subtract),
         reads=[v_row.k(), mm_row.k()], writes=[v_row.k()])
    b.op("s", lambda E: E.activation(out=v_row._ap, in_=v_row._ap, func=AF.Ln,
                                     bias=env.c["ccr"]._ap[0:1, 0:1], scale=1.0),
         reads=[v_row.k(), env.c["ccr"].k(0, 1)], writes=[v_row.k()])
    b.op("s", lambda E: E.activation(out=rstd_row._ap, in_=v_row._ap, func=AF.Exp,
                                     scale=-0.5),
         reads=[v_row.k()], writes=[rstd_row.k()])
    b.op("v", lambda E: E.tensor_tensor(out=mr_row._ap, in0=m_row._ap, in1=rstd_row._ap,
                                        op=ALU.mult),
         reads=[m_row.k(), rstd_row.k()], writes=[mr_row.k()])
    for tt in range(2):
        b.op("t", lambda E, tt=tt: E.matmul(
            out=psB.ap()[:, tt * 512:(tt + 1) * 512], lhsT=env.c["ones_row"]._ap,
            rhs=rstd_row._ap[:, tt * 512:(tt + 1) * 512], start=True, stop=True),
            reads=[rstd_row.k(tt * 512, (tt + 1) * 512), env.c["ones_row"].k()],
            writes=[PK("psB", tt * 512, (tt + 1) * 512)])
        b.op("t", lambda E, tt=tt: E.matmul(
            out=psB.ap()[:, 1024 + tt * 512:1024 + (tt + 1) * 512],
            lhsT=env.c["ones_row"]._ap,
            rhs=mr_row._ap[:, tt * 512:(tt + 1) * 512], start=True, stop=True),
            reads=[mr_row.k(tt * 512, (tt + 1) * 512), env.c["ones_row"].k()],
            writes=[PK("psB", 1024 + tt * 512, 1024 + (tt + 1) * 512)])
    for dc in range(NDC):
        b.op("v", lambda E, dc=dc: E.tensor_tensor(
            out=tmp.ap(), in0=xres.ap(dc * T, (dc + 1) * T), in1=psB.ap()[:, 0:1024],
            op=ALU.mult),
            reads=[xres.k(dc * T, (dc + 1) * T), PK("psB", 0, 1024)], writes=[tmp.k()])
        b.op("v", lambda E, dc=dc: E.tensor_tensor(
            out=xn.ap(dc * T, (dc + 1) * T), in0=tmp.ap(), in1=psB.ap()[:, 1024:2048],
            op=ALU.subtract),
            reads=[tmp.k(), PK("psB", 1024, 2048)],
            writes=[xn.k(dc * T, (dc + 1) * T)])
    if keep_rows:
        return rstd_row, mr_row
    return None, None


def emit_l0a(b, env, x_in, x_out):
    psA, psB = env.psA, env.psB
    xn = env.tv("big2", 0, BF16, NDC * T)
    emit_ln(b, env, x_in, xn)

    mk(b, "l0a:ln")
    krot = env.tv("big1", 0, BF16, NDC * T)
    vbf = env.tv("big1", 4 * MB, BF16, NTC * D)
    wblk = [env.tv("big3", i * (NDC * 128 * 2), BF16, NDC * 128) for i in range(2)]
    vwb = env.tv("big3", 1 * MB, BF16, NDC * 512)
    cosf = env.tv("big4", 0, F32, T)
    sinf = env.tv("big4", T * 4, F32, T)
    ks = env.tv("big4", 2 * T * 4, F32, T)
    ksb = env.tv("big4", 22528, BF16, T)
    rs1 = env.tv("big4", 3 * T * 4, F32, T)
    rs2 = env.tv("big4", 4 * T * 4, F32, T)
    qstg = env.tv("big4", 5 * T * 4, BF16, T)
    b.dma("y", lambda E: E.dma_start(out=cosf.ap(), in_=env.dr["cosf"].ap()),
          reads=[DK("cosf")], writes=[cosf.k()])
    b.dma("y", lambda E: E.dma_start(out=sinf.ap(), in_=env.dr["sinf"].ap()),
          reads=[DK("sinf")], writes=[sinf.k()])
    dk = env.tv("big5", 0, F32, D)
    Ttv = env.tv("big5", 1 * MB, F32, D)
    ktmA = env.tv("big5", 2 * MB, BF16, D)
    vdkst = env.tv("big5", 2 * MB + D * 2, BF16, D)
    lamcA = env.tv("big5", 3 * MB, F32, D)
    b.dma("y", lambda E: E.dma_start(out=dk.ap(), in_=env.dr["dk_tab"].ap()),
          reads=[DK("dk_tab")], writes=[dk.k()])
    b.dma("y", lambda E: E.dma_start(out=lamcA.ap(), in_=env.dr["lamc_tab"].ap()),
          reads=[DK("lamc_tab")], writes=[lamcA.k()])

    def proj_rot(wname, out_dram):
        for ec in range(H):
            gamma_pump(b, env, 6)
            wb = wblk[ec % 2]
            w_load(b, env, wname, ec * 128, 128, wb, NDC)
            for tt in range(2):
                for dc in range(NDC):
                    b.op("t", lambda E, ec=ec, tt=tt, dc=dc, wb=wb: E.matmul(
                        out=psA.ap()[:, tt * 512:(tt + 1) * 512],
                        lhsT=wb.ap(dc * 128, (dc + 1) * 128),
                        rhs=xn.ap(dc * T + tt * 512, dc * T + (tt + 1) * 512),
                        start=(dc == 0), stop=(dc == NDC - 1)),
                        reads=[wb.k(dc * 128, (dc + 1) * 128),
                               xn.k(dc * T + tt * 512, dc * T + (tt + 1) * 512)],
                        writes=[PK("psA", tt * 512, (tt + 1) * 512)])
                b.op("s", lambda E, tt=tt: E.activation(
                    out=ks.ap(tt * 512, (tt + 1) * 512),
                    in_=psA.ap()[:, tt * 512:(tt + 1) * 512], func=AF.Silu),
                    reads=[PK("psA", tt * 512, (tt + 1) * 512)],
                    writes=[ks.k(tt * 512, (tt + 1) * 512)])
            b.op("s", lambda E: E.activation(out=ksb.ap(), in_=ks.ap(), func=AF.Copy),
                 reads=[ks.k()], writes=[ksb.k()])
            for tt in range(2):
                b.op("t", lambda E, tt=tt: E.matmul(
                    out=psB.ap()[:, tt * 512:(tt + 1) * 512],
                    lhsT=env.c["permpm_bf"].ap(),
                    rhs=ksb.ap(tt * 512, (tt + 1) * 512), start=True, stop=True),
                    reads=[ksb.k(tt * 512, (tt + 1) * 512), env.c["permpm_bf"].k()],
                    writes=[PK("psB", tt * 512, (tt + 1) * 512)])
            b.op("v", lambda E: E.tensor_tensor(out=rs1.ap(), in0=ks.ap(),
                                                in1=cosf.ap(), op=ALU.mult),
                 reads=[ks.k(), cosf.k()], writes=[rs1.k()])
            b.op("v", lambda E: E.tensor_tensor(out=rs2.ap(), in0=psB.ap()[:, 0:1024],
                                                in1=sinf.ap(), op=ALU.mult),
                 reads=[PK("psB", 0, 1024), sinf.k()], writes=[rs2.k()])
            if out_dram is None:
                dst = krot
                b.op("v", lambda E, ec=ec: E.tensor_tensor(
                    out=dst.ap(ec * T, (ec + 1) * T), in0=rs1.ap(), in1=rs2.ap(),
                    op=ALU.add),
                    reads=[rs1.k(), rs2.k()], writes=[dst.k(ec * T, (ec + 1) * T)])
            else:
                b.op("v", lambda E: E.tensor_tensor(
                    out=qstg.ap(), in0=rs1.ap(), in1=rs2.ap(), op=ALU.add),
                    reads=[rs1.k(), rs2.k()], writes=[qstg.k()])
                b.dma("y", lambda E, ec=ec: E.dma_start(
                    out=env.dr[out_dram].ap()[ec * 128:(ec + 1) * 128, :],
                    in_=qstg.ap()),
                    reads=[qstg.k()],
                    writes=[DK(out_dram, ec * 128 * T * 2, (ec + 1) * 128 * T * 2)])

    proj_rot("wk", None)
    mk(b, "l0a:kproj")
    if L0A_STAGE == 11:
        odbg = env.dout("o_krot", (128, NDC * T), mybir.dt.bfloat16)
        b.dma("y", lambda E: E.dma_start(out=odbg.ap(), in_=krot.ap()),
              reads=[krot.k()], writes=[DK("o_krot")])
        return
    for eb in range(4):
        w_load(b, env, "wv", eb * 512, 512, vwb, NDC)
        for tc in range(NTC):
            gamma_pump(b, env, 2)
            pb = (tc % 2) * 512
            for dc in range(NDC):
                b.op("t", lambda E, eb=eb, tc=tc, dc=dc, pb=pb: E.matmul(
                    out=psA.ap()[:, pb:pb + 512],
                    lhsT=xn.ap(dc * T + tc * 128, dc * T + (tc + 1) * 128),
                    rhs=vwb.ap(dc * 512, (dc + 1) * 512),
                    start=(dc == 0), stop=(dc == NDC - 1)),
                    reads=[xn.k(dc * T + tc * 128, dc * T + (tc + 1) * 128),
                           vwb.k(dc * 512, (dc + 1) * 512)],
                    writes=[PK("psA", pb, pb + 512)])
            if L0A_STAGE == 1152:
                b.op("v", lambda E, eb=eb, tc=tc, pb=pb: E.tensor_copy(
                    out=vbf.ap(tc * D + eb * 512, tc * D + (eb + 1) * 512),
                    in_=psA.ap()[:, pb:pb + 512]),
                    reads=[PK("psA", pb, pb + 512)],
                    writes=[vbf.k(tc * D + eb * 512, tc * D + (eb + 1) * 512)])
            else:
                b.op("s", lambda E, eb=eb, tc=tc, pb=pb: E.activation(
                    out=vbf.ap(tc * D + eb * 512, tc * D + (eb + 1) * 512),
                    in_=psA.ap()[:, pb:pb + 512], func=AF.Copy),
                    reads=[PK("psA", pb, pb + 512)],
                    writes=[vbf.k(tc * D + eb * 512, tc * D + (eb + 1) * 512)])
            if L0A_STAGE not in (1151, 1152):
                b.op("v", lambda E, eb=eb, tc=tc, pb=pb: E.tensor_tensor(
                    out=vdkst.ap(eb * 512, (eb + 1) * 512),
                    in0=psA.ap()[:, pb:pb + 512],
                    in1=dk.ap(eb * 512, (eb + 1) * 512), op=ALU.mult),
                    reads=[PK("psA", pb, pb + 512), dk.k(eb * 512, (eb + 1) * 512)],
                    writes=[vdkst.k(eb * 512, (eb + 1) * 512)])
            if L0A_STAGE not in (1151, 1152, 1153):
                b.dma("y", lambda E, eb=eb, tc=tc: E.dma_start(
                    out=env.dr["vdk_d"].ap()[tc * 128:(tc + 1) * 128, eb * 512:(eb + 1) * 512],
                    in_=vdkst.ap(eb * 512, (eb + 1) * 512)),
                    reads=[vdkst.k(eb * 512, (eb + 1) * 512)],
                    writes=[DK("vdk_d", (tc * D + eb * 512) * 2, (tc * D + (eb + 1) * 512) * 2)])
    if L0A_STAGE in (115, 1151, 1152, 1153):
        odbg = env.dout("o_v", (128, NTC * D), mybir.dt.bfloat16)
        b.dma("y", lambda E: E.dma_start(out=odbg.ap(), in_=vbf.ap()),
              reads=[vbf.k()], writes=[DK("o_v")])
        return
    b.op("v", lambda E: E.memset(Ttv.ap(), 0.0), writes=[Ttv.k()])
    vdkc_bufs = [env.tv("big3", 1 * MB + i * D * 2, BF16, D) for i in range(2)]
    for tc in range(NTC):
        vc = vdkc_bufs[tc % 2]
        b.dma("y", lambda E, tc=tc, vc=vc: E.dma_start(
            out=vc.ap(), in_=env.dr["vdk_d"].ap()[tc * 128:(tc + 1) * 128, :]),
            reads=[DK("vdk_d", tc * 128 * D * 2, (tc + 1) * 128 * D * 2)],
            writes=[vc.k()])
        for h in range(H):
            b.op("t", lambda E, tc=tc, h=h: E.matmul(
                out=psB.ap()[:, h * 128:(h + 1) * 128],
                lhsT=krot.ap(h * T + tc * 128, h * T + (tc + 1) * 128),
                rhs=env.c["idbf"].ap(), start=True, stop=True),
                reads=[krot.k(h * T + tc * 128, h * T + (tc + 1) * 128),
                       env.c["idbf"].k()],
                writes=[PK("psB", h * 128, (h + 1) * 128)])
        b.op("v", lambda E: E.tensor_copy(out=ktmA.ap(), in_=psB.ap()),
             reads=[PK("psB", 0, 2048)], writes=[ktmA.k()])
        if L0A_STAGE != 117:
            for h in range(H):
                b.op("t", lambda E, tc=tc, h=h, vc=vc: E.matmul(
                    out=psA.ap()[:, h * 128:(h + 1) * 128],
                    lhsT=ktmA.ap(h * 128, (h + 1) * 128),
                    rhs=vc.ap(h * 128, (h + 1) * 128), start=True, stop=True),
                    reads=[ktmA.k(h * 128, (h + 1) * 128), vc.k(h * 128, (h + 1) * 128)],
                    writes=[PK("psA", h * 128, (h + 1) * 128)])
            b.op("v", lambda E: E.tensor_tensor(out=Ttv.ap(), in0=Ttv.ap(),
                                                in1=lamcA.ap(), op=ALU.mult),
                 reads=[Ttv.k(), lamcA.k()], writes=[Ttv.k()])
            b.op("v", lambda E: E.tensor_tensor(out=Ttv.ap(), in0=Ttv.ap(),
                                                in1=psA.ap(), op=ALU.add),
                 reads=[Ttv.k(), PK("psA", 0, 2048)], writes=[Ttv.k()])
    if L0A_STAGE == 12:
        odbg = env.dout("o_T", (128, D))
        b.dma("y", lambda E: E.dma_start(out=odbg.ap(), in_=Ttv.ap()),
              reads=[Ttv.k()], writes=[DK("o_T")])
        return
    mk(b, "l0a:v+phA")
    b.dma("p", lambda E: E.dma_start(out=env.dr["ag1_in"].ap(), in_=Ttv.ap()),
          reads=[Ttv.k()], writes=[DK("ag1_in")])
    b.op("p", lambda E: E.collective_compute(
        "AllGather", ALU.bypass, replica_groups=[list(range(NCORE))],
        ins=[env.dr["ag1_in"].ap()], outs=[env.dr["ag1_out"].ap()]),
        reads=[DK("ag1_in")], writes=[DK("ag1_out")])
    if L0A_STAGE == 13:
        odbg = env.dout("o_ag", (NCORE * 128, D))
        b.dma("y", lambda E: E.dma_start(out=odbg.ap(), in_=env.dr["ag1_out"].ap()),
              reads=[DK("ag1_out")], writes=[DK("o_ag")])
        return
    proj_rot("wq", "qrot_d")
    wg1b = env.tv("big3", 8192, BF16, NDC * 128)
    w_load(b, env, "wg1", 0, 128, wg1b, NDC)
    g1 = env.tv("big3", 16384, BF16, T)
    wg2b = env.tv("big3", 18432, BF16, D)
    w_load(b, env, "wg2", 0, D, wg2b, 1)
    for tt in range(2):
        for dc in range(NDC):
            b.op("t", lambda E, tt=tt, dc=dc: E.matmul(
                out=psB.ap()[:, 1024 + tt * 512:1024 + (tt + 1) * 512],
                lhsT=wg1b.ap(dc * 128, (dc + 1) * 128),
                rhs=xn.ap(dc * T + tt * 512, dc * T + (tt + 1) * 512),
                start=(dc == 0), stop=(dc == NDC - 1)),
                reads=[wg1b.k(dc * 128, (dc + 1) * 128),
                       xn.k(dc * T + tt * 512, dc * T + (tt + 1) * 512)],
                writes=[PK("psB", 1024 + tt * 512, 1024 + (tt + 1) * 512)])
    b.op("v", lambda E: E.tensor_copy(out=g1.ap(), in_=psB.ap()[:, 1024:2048]),
         reads=[PK("psB", 1024, 2048)], writes=[g1.k()])
    mk(b, "l0a:q+g1")
    dtt = env.tv("big2", 0, F32, D)
    dqt = env.tv("big2", 1 * MB, F32, D)
    lamc = env.tv("big2", 2 * MB, F32, D)
    S = env.tv("big2", 3 * MB, F32, D)
    tjb = [env.tv("big5", i * MB, F32, D) for i in range(2)]
    b.op("v", lambda E: E.memset(S.ap(), 0.0), writes=[S.k()])
    for j in range(NCORE):
        tj = tjb[j % 2]
        b.dma("y", lambda E, j=j, tj=tj: E.dma_start(
            out=tj.ap(), in_=env.dr["ag1_out"].ap()[j * 128:(j + 1) * 128, :]),
            reads=[DK("ag1_out")], writes=[tj.k()])
        for h in range(H):
            b.op("v", lambda E, j=j, h=h, tj=tj: E.scalar_tensor_tensor(
                out=S.ap(h * 128, (h + 1) * 128), in0=tj.ap(h * 128, (h + 1) * 128),
                scalar=env.c["coefc"].ap(j * 16 + h, j * 16 + h + 1),
                in1=S.ap(h * 128, (h + 1) * 128), op0=ALU.mult, op1=ALU.add),
                reads=[tj.k(h * 128, (h + 1) * 128), S.k(h * 128, (h + 1) * 128),
                       env.c["coefc"].k(j * 16 + h, j * 16 + h + 1)],
                writes=[S.k(h * 128, (h + 1) * 128)])
    b.dma("y", lambda E: E.dma_start(out=dtt.ap(), in_=env.dr["dt_tab"].ap()),
          reads=[DK("dt_tab")], writes=[dtt.k()])
    b.dma("y", lambda E: E.dma_start(out=dqt.ap(), in_=env.dr["dq_tab"].ap()),
          reads=[DK("dq_tab")], writes=[dqt.k()])
    b.dma("y", lambda E: E.dma_start(out=lamc.ap(), in_=env.dr["lamc_tab"].ap()),
          reads=[DK("lamc_tab")], writes=[lamc.k()])
    qcb = [env.tv("big3", i * D * 2, BF16, D) for i in range(2)]
    vdc = [env.tv("big3", 1 * MB + i * D * 2, BF16, D) for i in range(2)]

    def sweep_loads(tc):
        qc, vc = qcb[tc % 2], vdc[tc % 2]
        b.dma("y", lambda E, tc=tc, qc=qc: E.dma_start(
            out=qc.ap().rearrange("p (a c) -> p a c", a=H),
            in_=env.dr["qrot_d"].ap()[:, tc * 128:(tc + 1) * 128].rearrange(
                "(a p) c -> p a c", p=128)),
            reads=[DK("qrot_d")], writes=[qc.k()])
        b.dma("y", lambda E, tc=tc, vc=vc: E.dma_start(
            out=vc.ap(), in_=env.dr["vdk_d"].ap()[tc * 128:(tc + 1) * 128, :]),
            reads=[DK("vdk_d", tc * 128 * D * 2, (tc + 1) * 128 * D * 2)],
            writes=[vc.k()])

    sweep_loads(0)
    sweep_loads(1)
    if L0A_STAGE == 1:
        for ec in range(NDC):
            xt_ = env.tv("big5", (ec % 2) * 4096, F32, T)
            b.dma("y", lambda E, ec=ec, xt_=xt_: E.dma_start(
                out=xt_.ap(), in_=x_in.ap()[ec * 128:(ec + 1) * 128, :]),
                reads=[DK(x_in.name, ec * 128 * T * 4, (ec + 1) * 128 * T * 4)],
                writes=[xt_.k()])
            b.op("v", lambda E, ec=ec, xt_=xt_: E.tensor_tensor(
                out=xt_.ap(), in0=xt_.ap(), in1=S.ap(0, 1024) if False else xt_.ap(),
                op=ALU.add), reads=[xt_.k()], writes=[xt_.k()])
            b.dma("y", lambda E, ec=ec, xt_=xt_: E.dma_start(
                out=x_out.ap()[ec * 128:(ec + 1) * 128, :], in_=xt_.ap()),
                reads=[xt_.k()],
                writes=[DK(x_out.name, ec * 128 * T * 4, (ec + 1) * 128 * T * 4)])
        odbg = env.dout("o_S", (128, D))
        b.dma("y", lambda E: E.dma_start(out=odbg.ap(), in_=S.ap()),
              reads=[S.k()], writes=[DK("o_S")])
        return

    mk(b, "l0a:prefix")
    ktm = env.tv("big5", 0, BF16, D)
    sTD = env.tv("big5", D * 2, BF16, D)
    qdq = env.tv("big5", 2 * D * 2, BF16, D)
    Sbf = env.tv("big5", 3 * D * 2, BF16, D)
    sg = env.tv("big5", 2 * MB, F32, D)
    osg = env.tv("big5", 24576, F32, D)
    xgb = env.tv("big5", 32768, BF16, D)
    xgT = env.tv("big4", 0, BF16, NDC * T)
    msv = env.salloc(F32, 16, "l0a_ms")
    rst = env.salloc(F32, 16, "l0a_rstd")
    nsc1 = env.salloc(F32, 16, "l0a_n1")
    nsc2 = env.salloc(F32, 16, "l0a_n2")
    b.op("v", lambda E: E.tensor_copy(out=Sbf.ap(), in_=S.ap()),
         reads=[S.k()], writes=[Sbf.k()])
    def sweep_gate(tc):
        for eb in range(4):
            b.op("t", lambda E, tc=tc, eb=eb: E.matmul(
                out=psB.ap()[:, eb * 512:(eb + 1) * 512],
                lhsT=g1.ap(tc * 128, (tc + 1) * 128),
                rhs=wg2b.ap(eb * 512, (eb + 1) * 512), start=True, stop=True),
                reads=[g1.k(tc * 128, (tc + 1) * 128), wg2b.k(eb * 512, (eb + 1) * 512)],
                writes=[PK("psB", eb * 512, (eb + 1) * 512)])
        b.op("s", lambda E: E.activation(out=sg.ap(), in_=psB.ap(), func=AF.Sigmoid),
             reads=[PK("psB", 0, 2048)], writes=[sg.k()])

    sweep_gate(0)
    for tc in range(NTC):
        qc = qcb[tc % 2]
        vc = vdc[tc % 2]
        for h in range(H):
            b.op("t", lambda E, tc=tc, h=h, qc=qc: E.matmul(
                out=psA.ap()[:, h * 128:(h + 1) * 128],
                lhsT=krot.ap(h * T + tc * 128, h * T + (tc + 1) * 128),
                rhs=qc.ap(h * 128, (h + 1) * 128), start=True, stop=True),
                reads=[krot.k(h * T + tc * 128, h * T + (tc + 1) * 128),
                       qc.k(h * 128, (h + 1) * 128)],
                writes=[PK("psA", h * 128, (h + 1) * 128)])
        for h in range(H):
            b.op("v", lambda E, h=h: E.tensor_tensor(
                out=sTD.ap(h * 128, (h + 1) * 128),
                in0=psA.ap()[:, h * 128:(h + 1) * 128],
                in1=dtt.ap(h * 128, (h + 1) * 128), op=ALU.mult),
                reads=[PK("psA", h * 128, (h + 1) * 128), dtt.k(h * 128, (h + 1) * 128)],
                writes=[sTD.k(h * 128, (h + 1) * 128)])
        for h in range(H):
            b.op("t", lambda E, tc=tc, h=h: E.matmul(
                out=psB.ap()[:, h * 128:(h + 1) * 128],
                lhsT=krot.ap(h * T + tc * 128, h * T + (tc + 1) * 128),
                rhs=env.c["idbf"].ap(), start=True, stop=True),
                reads=[krot.k(h * T + tc * 128, h * T + (tc + 1) * 128),
                       env.c["idbf"].k()],
                writes=[PK("psB", h * 128, (h + 1) * 128)])
        b.op("v", lambda E: E.tensor_copy(out=ktm.ap(), in_=psB.ap()),
             reads=[PK("psB", 0, 2048)], writes=[ktm.k()])
        for h in range(H):
            b.op("t", lambda E, h=h, vc=vc: E.matmul(
                out=psA.ap()[:, h * 128:(h + 1) * 128],
                lhsT=ktm.ap(h * 128, (h + 1) * 128),
                rhs=vc.ap(h * 128, (h + 1) * 128), start=True, stop=True),
                reads=[ktm.k(h * 128, (h + 1) * 128), vc.k(h * 128, (h + 1) * 128)],
                writes=[PK("psA", h * 128, (h + 1) * 128)])
        for h in range(H):
            b.op("v", lambda E, h=h, qc=qc: E.tensor_tensor(
                out=qdq.ap(h * 128, (h + 1) * 128),
                in0=qc.ap(h * 128, (h + 1) * 128),
                in1=dqt.ap(h * 128, (h + 1) * 128), op=ALU.mult),
                reads=[qc.k(h * 128, (h + 1) * 128), dqt.k(h * 128, (h + 1) * 128)],
                writes=[qdq.k(h * 128, (h + 1) * 128)])
        for h in range(H):
            b.op("t", lambda E, tc=tc, h=h: E.matmul(
                out=psB.ap()[:, h * 128:(h + 1) * 128],
                lhsT=sTD.ap(h * 128, (h + 1) * 128),
                rhs=vbf.ap(tc * D + h * 128, tc * D + (h + 1) * 128),
                start=True, stop=False),
                reads=[sTD.k(h * 128, (h + 1) * 128),
                       vbf.k(tc * D + h * 128, tc * D + (h + 1) * 128)],
                writes=[PK("psB", h * 128, (h + 1) * 128)])
            b.op("t", lambda E, h=h: E.matmul(
                out=psB.ap()[:, h * 128:(h + 1) * 128],
                lhsT=qdq.ap(h * 128, (h + 1) * 128),
                rhs=Sbf.ap(h * 128, (h + 1) * 128), start=False, stop=True),
                reads=[qdq.k(h * 128, (h + 1) * 128), Sbf.k(h * 128, (h + 1) * 128)],
                writes=[PK("psB", h * 128, (h + 1) * 128)])
        b.op("v", lambda E: E.tensor_tensor(out=S.ap(), in0=S.ap(), in1=lamc.ap(),
                                            op=ALU.mult),
             reads=[S.k(), lamc.k()], writes=[S.k()])
        b.op("v", lambda E: E.tensor_tensor(out=S.ap(), in0=S.ap(), in1=psA.ap(),
                                            op=ALU.add),
             reads=[S.k(), PK("psA", 0, 2048)], writes=[S.k()])
        b.op("v", lambda E: E.tensor_copy(out=Sbf.ap(), in_=S.ap()),
             reads=[S.k()], writes=[Sbf.k()])
        b.op("v", lambda E: E.tensor_tensor(out=osg.ap(), in0=psB.ap(), in1=sg.ap(),
                                            op=ALU.mult),
             reads=[PK("psB", 0, 2048), sg.k()], writes=[osg.k()])
        sqv = env.tv("big5", 2 * MB, F32, D)
        b.op("s", lambda E: E.activation(out=sqv.ap(), in_=osg.ap(), func=AF.Square),
             reads=[osg.k()], writes=[sqv.k()])
        b.op("v", lambda E: E.tensor_reduce(
            out=msv.ap(), in_=sqv.ap().rearrange("p (a c) -> p a c", a=H),
            axis=AX.X, op=ALU.add),
            reads=[sqv.k()], writes=[msv.k()])
        b.op("v", lambda E: E.tensor_scalar(out=msv.ap(), in0=msv.ap(),
                                            scalar1=1.0 / HD, scalar2=1e-6,
                                            op0=ALU.mult, op1=ALU.add),
             reads=[msv.k()], writes=[msv.k()])
        newton_rsqrt(b, env, msv, 0, 16, rst, 0, nsc1, nsc2)
        for h in range(H):
            b.op("v", lambda E, h=h: E.tensor_scalar(
                out=xgb.ap(h * 128, (h + 1) * 128),
                in0=osg.ap(h * 128, (h + 1) * 128),
                scalar1=rst.ap(h, h + 1), scalar2=None, op0=ALU.mult),
                reads=[osg.k(h * 128, (h + 1) * 128), rst.k(h, h + 1)],
                writes=[xgb.k(h * 128, (h + 1) * 128)])
        if tc + 2 < NTC:
            sweep_loads(tc + 2)
        if tc + 1 < NTC:
            sweep_gate(tc + 1)
        for ec in range(NDC):
            b.op("t", lambda E, ec=ec: E.matmul(
                out=psA.ap()[:, ec * 128:(ec + 1) * 128],
                lhsT=xgb.ap(ec * 128, (ec + 1) * 128),
                rhs=env.c["idbf"].ap(), start=True, stop=True),
                reads=[xgb.k(ec * 128, (ec + 1) * 128), env.c["idbf"].k()],
                writes=[PK("psA", ec * 128, (ec + 1) * 128)])
        for ec in range(NDC):
            b.op("v", lambda E, ec=ec, tc=tc: E.tensor_scalar(
                out=xgT.ap(ec * T + tc * 128, ec * T + (tc + 1) * 128),
                in0=psA.ap()[:, ec * 128:(ec + 1) * 128],
                scalar1=env.c["nw_cols"].ap(ec, ec + 1), scalar2=None, op0=ALU.mult),
                reads=[PK("psA", ec * 128, (ec + 1) * 128), env.c["nw_cols"].k(ec, ec + 1)],
                writes=[xgT.k(ec * T + tc * 128, ec * T + (tc + 1) * 128)])
    mk(b, "l0a:sweep")
    if L0A_STAGE == 2:
        odbg = env.dout("o_xgT", (128, NDC * T), mybir.dt.bfloat16)
        b.dma("y", lambda E: E.dma_start(out=odbg.ap(), in_=xgT.ap()),
              reads=[xgT.k()], writes=[DK("o_xgT")])
        odbg2 = env.dout("o_S", (128, D))
        b.dma("y", lambda E: E.dma_start(out=odbg2.ap(), in_=S.ap()),
              reads=[S.k()], writes=[DK("o_S")])
        for ec in range(NDC):
            xt_ = env.tv("big5", (ec % 2) * 4096, F32, T)
            b.dma("y", lambda E, ec=ec, xt_=xt_: E.dma_start(
                out=xt_.ap(), in_=x_in.ap()[ec * 128:(ec + 1) * 128, :]),
                reads=[DK(x_in.name, ec * 128 * T * 4, (ec + 1) * 128 * T * 4)],
                writes=[xt_.k()])
            b.dma("y", lambda E, ec=ec, xt_=xt_: E.dma_start(
                out=x_out.ap()[ec * 128:(ec + 1) * 128, :], in_=xt_.ap()),
                reads=[xt_.k()],
                writes=[DK(x_out.name, ec * 128 * T * 4, (ec + 1) * 128 * T * 4)])
        return

    xt = [env.tv("big5", i * (T * 4), F32, T) for i in range(2)]
    ot = [env.tv("big5", 2 * MB + i * 512 * 4, F32, 512) for i in range(4)]
    for ecp in range(NDC):
        gamma_pump(b, env, 4)
        wb = wblk[ecp % 2]
        w_load(b, env, "wo", ecp * 128, 128, wb, NDC)
        x_t = xt[ecp % 2]
        b.dma("y", lambda E, ecp=ecp, x_t=x_t: E.dma_start(
            out=x_t.ap(), in_=x_in.ap()[ecp * 128:(ecp + 1) * 128, :]),
            reads=[DK(x_in.name, ecp * 128 * T * 4, (ecp + 1) * 128 * T * 4)],
            writes=[x_t.k()])
        for tt in range(2):
            pcol = ((2 * ecp + tt) % 4) * 512
            for ein in range(NDC):
                b.op("t", lambda E, ecp=ecp, tt=tt, ein=ein, wb=wb, pcol=pcol: E.matmul(
                    out=psB.ap()[:, pcol:pcol + 512],
                    lhsT=wb.ap(ein * 128, (ein + 1) * 128),
                    rhs=xgT.ap(ein * T + tt * 512, ein * T + (tt + 1) * 512),
                    start=(ein == 0), stop=(ein == NDC - 1)),
                    reads=[wb.k(ein * 128, (ein + 1) * 128),
                           xgT.k(ein * T + tt * 512, ein * T + (tt + 1) * 512)],
                    writes=[PK("psB", pcol, pcol + 512)])
            o_t = ot[(2 * ecp + tt) % 4]
            b.op("v", lambda E, tt=tt, x_t=x_t, o_t=o_t, pcol=pcol: E.tensor_tensor(
                out=o_t.ap(), in0=x_t.ap(tt * 512, (tt + 1) * 512),
                in1=psB.ap()[:, pcol:pcol + 512], op=ALU.add),
                reads=[x_t.k(tt * 512, (tt + 1) * 512), PK("psB", pcol, pcol + 512)],
                writes=[o_t.k()])
            b.dma("y", lambda E, ecp=ecp, tt=tt, o_t=o_t: E.dma_start(
                out=x_out.ap()[ecp * 128:(ecp + 1) * 128, tt * 512:(tt + 1) * 512],
                in_=o_t.ap()),
                reads=[o_t.k()],
                writes=[DK(x_out.name, ecp * 128 * T * 4, (ecp + 1) * 128 * T * 4)])


def emit_gamma_stream(b, env, mat, thname):
    R, Cc = env.dr[mat].shape
    gacc = env.salloc(F32, 1, "gacc_" + thname)
    rcol = env.salloc(F32, 1, "grc_" + thname)
    wtl = [env.tv("big3", 24576 + i * 4096, F32, 1024) for i in range(2)]
    b.op("v", lambda E, gacc=gacc: E.memset(gacc.ap(), 0.0), writes=[gacc.k()])
    cbs = [(c, min(c + 1024, Cc)) for c in range(0, Cc, 1024)]
    idx = [0]

    def tile(rc, c0, c1):
        def emitfn():
            wt = wtl[idx[0] % 2]
            idx[0] += 1
            n = c1 - c0
            b.dma("s", lambda E, wt=wt: E.dma_start(
                out=wt.ap(0, n), in_=env.dr[mat].ap()[rc * 128:(rc + 1) * 128, c0:c1]),
                reads=[DK(mat, rc * 128 * Cc * 4, (rc + 1) * 128 * Cc * 4)],
                writes=[wt.k(0, n)])
            b.op("v", lambda E, wt=wt: E.tensor_reduce(
                out=rcol.ap(), in_=wt.ap(0, n), axis=AX.X, op=ALU.add,
                apply_absolute_value=True),
                reads=[wt.k(0, n)], writes=[rcol.k()])
            b.op("v", lambda E: E.tensor_tensor(out=gacc.ap(), in0=gacc.ap(),
                                                in1=rcol.ap(), op=ALU.add),
                 reads=[gacc.k(), rcol.k()], writes=[gacc.k()])
        return emitfn

    for rc in range(R // 128):
        for (c0, c1) in cbs:
            env.gpump.append(tile(rc, c0, c1))
    env.gsize[thname] = (gacc, R * Cc)


def gamma_pump(b, env, n):
    for _ in range(n):
        if not env.gpump:
            return
        env.gpump.popleft()()


def emit_gamma_fin(b, env, thname):
    psA = env.psA
    gacc, N = env.gsize[thname]
    thcol = env.salloc(F32, 1, thname)
    throw = env.row("throw", 8)
    b.op("t", lambda E: E.matmul(out=psA.ap()[0:1, 0:1], lhsT=gacc.ap(),
                                 rhs=env.c["ones_col"].ap(), start=True, stop=True),
         reads=[gacc.k(), env.c["ones_col"].k()], writes=[PK("psA", 0, 1)])
    b.op("s", lambda E, N=N: E.activation(
        out=throw._ap[:, 0:1], in_=psA.ap()[0:1, 0:1], func=AF.Identity,
        scale=0.5 / N, bias=env.c["ccr"]._ap[0:1, 1:2]),
        reads=[PK("psA", 0, 1), env.c["ccr"].k(1, 2)], writes=[throw.k(0, 1)])
    b.op("t", lambda E: E.matmul(out=psA.ap()[:, 0:1], lhsT=env.c["ones_row"]._ap,
                                 rhs=throw._ap[:, 0:1], start=True, stop=True),
         reads=[throw.k(0, 1), env.c["ones_row"].k()], writes=[PK("psA", 0, 1)])
    b.op("v", lambda E: E.tensor_copy(out=thcol.ap(), in_=psA.ap()[:, 0:1]),
         reads=[PK("psA", 0, 1)], writes=[thcol.k()])
    return thcol


def quantize_tile(b, env, win, lo, hi, thcol, Ab, Bb, qw, qlo):
    n = hi - lo
    b.op("v", lambda E: E.tensor_scalar(out=Ab.ap(0, n), in0=win.ap(lo, hi),
                                        scalar1=thcol.ap(), scalar2=None, op0=ALU.is_gt),
         reads=[win.k(lo, hi), thcol.k()], writes=[Ab.k(0, n)])
    b.op("s", lambda E: E.activation(out=Bb.ap(0, n), in_=win.ap(lo, hi), func=AF.Sign,
                                     bias=thcol.ap()),
         reads=[win.k(lo, hi), thcol.k()], writes=[Bb.k(0, n)])
    b.op("v", lambda E: E.scalar_tensor_tensor(out=qw.ap(qlo, qlo + n), in0=Bb.ap(0, n),
                                               scalar=0.5, in1=Ab.ap(0, n),
                                               op0=ALU.mult, op1=ALU.add),
         reads=[Bb.k(0, n), Ab.k(0, n)], writes=[qw.k(qlo, qlo + n)])


def emit_l0f(b, env, x_in, x_out):
    psA, psB = env.psA, env.psB
    xn = env.tv("big2", 0, BF16, NDC * T)
    emit_ln(b, env, x_in, xn)
    rs_row = env.prow("big5", 30720)
    for tt in range(2):
        for dc in range(NDC):
            b.op("t", lambda E, tt=tt, dc=dc: E.matmul(
                out=psA.ap()[0:1, tt * 512:(tt + 1) * 512],
                lhsT=env.c["ones_col_bf"].ap(),
                rhs=xn.ap(dc * T + tt * 512, dc * T + (tt + 1) * 512),
                start=(dc == 0), stop=(dc == NDC - 1)),
                reads=[xn.k(dc * T + tt * 512, dc * T + (tt + 1) * 512),
                       env.c["ones_col_bf"].k()],
                writes=[PK("psA", tt * 512, (tt + 1) * 512)])
    b.op("v", lambda E: E.tensor_copy(out=rs_row._ap, in_=psA.ap()[0:1, 0:1024]),
         reads=[PK("psA", 0, 1024)], writes=[rs_row.k()])
    rs_hi = env.prow("big5", 34816, T, BF16)
    rs_lo = env.prow("big5", 36864, T, BF16)
    b.op("v", lambda E: E.tensor_copy(out=rs_hi._ap, in_=rs_row._ap),
         reads=[rs_row.k()], writes=[rs_hi.k()])
    b.op("v", lambda E: E.scalar_tensor_tensor(out=rs_lo._ap, in0=rs_hi._ap,
                                               scalar=-1.0, in1=rs_row._ap,
                                               op0=ALU.mult, op1=ALU.add),
         reads=[rs_hi.k(), rs_row.k()], writes=[rs_lo.k()])
    mk(b, "l0f:ln+rs")
    while env.gpump:
        gamma_pump(b, env, 8)
    th1 = emit_gamma_fin(b, env, "th1")
    th2 = emit_gamma_fin(b, env, "th2")
    w1b = [env.tv("big3", 0, F32, D), env.tv("big3", 16384, F32, D)]
    w2b = [env.tv("big3", 8192, F32, D), env.tv("big3", 24576, F32, D)]
    Ab = env.tv("big5", 0, BF16, D)
    Bb = env.tv("big5", 4096, BF16, D)
    Ab2 = env.tv("big4", 24576, BF16, D)
    Bb2 = env.tv("big4", 28672, BF16, D)
    qw1b = [env.tv("big5", 8192 + i * 4096, BF16, D) for i in range(2)]
    qw2b = [env.tv("big5", 16384 + i * 4096, BF16, D) for i in range(2)]
    sig = [env.tv("big5", 24576 + i * 2048, F32, 512) for i in range(2)]
    hidt = [env.tv("big1", hc * 2048, BF16, T) if hc < 32
            else env.tv("big4", (hc - 32) * 2048, BF16, T) for hc in range(NHC)]

    def st1_loads(hc):
        wa, wb2 = w1b[hc % 2], w2b[hc % 2]
        for hf in range(2):
            b.dma("s", lambda E, hc=hc, hf=hf, wa=wa: E.dma_start(
                out=wa.ap(hf * 1024, (hf + 1) * 1024).rearrange("p (a c) -> p a c", a=8),
                in_=env.dr["w1t"].ap()[hf * 1024:(hf + 1) * 1024,
                                       hc * 128:(hc + 1) * 128].rearrange(
                    "(a p) c -> p a c", p=128)),
                reads=[DK("w1t")], writes=[wa.k(hf * 1024, (hf + 1) * 1024)])
            b.dma("y", lambda E, hc=hc, hf=hf, wb2=wb2: E.dma_start(
                out=wb2.ap(hf * 1024, (hf + 1) * 1024).rearrange("p (a c) -> p a c", a=8),
                in_=env.dr["w2t"].ap()[hf * 1024:(hf + 1) * 1024,
                                       hc * 128:(hc + 1) * 128].rearrange(
                    "(a p) c -> p a c", p=128)),
                reads=[DK("w2t")], writes=[wb2.k(hf * 1024, (hf + 1) * 1024)])

    st1_loads(0)
    st1_loads(1)
    quantize_tile(b, env, w1b[0], 0, D, th1, Ab, Bb, qw1b[0], 0)
    quantize_tile(b, env, w2b[0], 0, D, th2, Ab2, Bb2, qw2b[0], 0)
    for hc in range(NHC):
        qw1 = qw1b[hc % 2]
        qw2 = qw2b[hc % 2]
        if hc + 1 < NHC:
            quantize_tile(b, env, w1b[(hc + 1) % 2], 0, D, th1, Ab, Bb,
                          qw1b[(hc + 1) % 2], 0)
            quantize_tile(b, env, w2b[(hc + 1) % 2], 0, D, th2, Ab2, Bb2,
                          qw2b[(hc + 1) % 2], 0)
        if hc + 2 < NHC:
            st1_loads(hc + 2)
        for tt in range(2):
            for dc in range(NDC):
                b.op("t", lambda E, tt=tt, dc=dc, qw1=qw1: E.matmul(
                    out=psA.ap()[:, tt * 512:(tt + 1) * 512],
                    lhsT=qw1.ap(dc * 128, (dc + 1) * 128),
                    rhs=xn.ap(dc * T + tt * 512, dc * T + (tt + 1) * 512),
                    start=(dc == 0), stop=False),
                    reads=[qw1.k(dc * 128, (dc + 1) * 128),
                           xn.k(dc * T + tt * 512, dc * T + (tt + 1) * 512)],
                    writes=[PK("psA", tt * 512, (tt + 1) * 512)])
            for rr in (rs_hi, rs_lo):
                b.op("t", lambda E, tt=tt, rr=rr, st=(rr is rs_lo): E.matmul(
                    out=psA.ap()[:, tt * 512:(tt + 1) * 512], lhsT=env.c["nh_row_bf"]._ap,
                    rhs=rr._ap[:, tt * 512:(tt + 1) * 512], start=False, stop=st),
                    reads=[rr.k(tt * 512, (tt + 1) * 512), env.c["nh_row_bf"].k()],
                    writes=[PK("psA", tt * 512, (tt + 1) * 512)])
            b.op("s", lambda E, tt=tt: E.activation(
                out=sig[tt].ap(), in_=psA.ap()[:, tt * 512:(tt + 1) * 512],
                func=AF.Sigmoid),
                reads=[PK("psA", tt * 512, (tt + 1) * 512)], writes=[sig[tt].k()])
            for dc in range(NDC):
                b.op("t", lambda E, tt=tt, dc=dc, qw2=qw2: E.matmul(
                    out=psB.ap()[:, tt * 512:(tt + 1) * 512],
                    lhsT=qw2.ap(dc * 128, (dc + 1) * 128),
                    rhs=xn.ap(dc * T + tt * 512, dc * T + (tt + 1) * 512),
                    start=(dc == 0), stop=False),
                    reads=[qw2.k(dc * 128, (dc + 1) * 128),
                           xn.k(dc * T + tt * 512, dc * T + (tt + 1) * 512)],
                    writes=[PK("psB", tt * 512, (tt + 1) * 512)])
            for rr in (rs_hi, rs_lo):
                b.op("t", lambda E, tt=tt, rr=rr, st=(rr is rs_lo): E.matmul(
                    out=psB.ap()[:, tt * 512:(tt + 1) * 512], lhsT=env.c["nh_row_bf"]._ap,
                    rhs=rr._ap[:, tt * 512:(tt + 1) * 512], start=False, stop=st),
                    reads=[rr.k(tt * 512, (tt + 1) * 512), env.c["nh_row_bf"].k()],
                    writes=[PK("psB", tt * 512, (tt + 1) * 512)])
            b.op("v", lambda E, hc=hc, tt=tt: E.tensor_tensor(
                out=hidt[hc].ap(tt * 512, (tt + 1) * 512),
                in0=psB.ap()[:, tt * 512:(tt + 1) * 512],
                in1=sig[tt].ap(), op=ALU.mult),
                reads=[PK("psB", tt * 512, (tt + 1) * 512), sig[tt].k()],
                writes=[hidt[hc].k(tt * 512, (tt + 1) * 512)])
            b.op("t", lambda E, hc=hc, tt=tt: E.matmul(
                out=psB.ap()[0:1, 1024 + tt * 512:1024 + (tt + 1) * 512],
                lhsT=env.c["ones_col_bf"].ap(),
                rhs=hidt[hc].ap(tt * 512, (tt + 1) * 512),
                start=(hc == 0), stop=(hc == NHC - 1)),
                reads=[hidt[hc].k(tt * 512, (tt + 1) * 512),
                       env.c["ones_col_bf"].k()],
                writes=[PK("psB", 1024 + tt * 512, 1024 + (tt + 1) * 512)])
    rsh_row = env.prow("big5", 24576)
    b.op("v", lambda E: E.tensor_copy(out=rsh_row._ap, in_=psB.ap()[0:1, 1024:2048]),
         reads=[PK("psB", 1024, 2048)], writes=[rsh_row.k()])
    rsh_hi = env.prow("big5", 28672, T, BF16)
    rsh_lo = env.prow("big5", 30720, T, BF16)
    b.op("v", lambda E: E.tensor_copy(out=rsh_hi._ap, in_=rsh_row._ap),
         reads=[rsh_row.k()], writes=[rsh_hi.k()])
    b.op("v", lambda E: E.scalar_tensor_tensor(out=rsh_lo._ap, in0=rsh_hi._ap,
                                               scalar=-1.0, in1=rsh_row._ap,
                                               op0=ALU.mult, op1=ALU.add),
         reads=[rsh_hi.k(), rsh_row.k()], writes=[rsh_lo.k()])
    mk(b, "l0f:stage1")
    mk(b, "l0f:stage2")
    th3 = emit_gamma_fin(b, env, "th3")
    w3i = [env.tv("big3", i * 2048, F32, 512) for i in range(4)]
    x1t = [env.tv("big3", 8192 + i * 2048, F32, 512) for i in range(8)]
    A3b = [env.tv("big5", 12288 + i * 1024, BF16, 512) for i in range(2)]
    B3b = [env.tv("big5", 14336 + i * 1024, BF16, 512) for i in range(2)]
    qw3b = [env.tv("big5", 2048 + i * 1024, BF16, 512) for i in range(2)]
    ostg = [env.tv("big5", 4096 + i * 2048, F32, 512) for i in range(4)]
    NIT = 4 * NHC

    def st2_load(it):
        grp, hc = it // NHC, it % NHC
        wbuf = w3i[it % 4]
        b.dma("s", lambda E, grp=grp, hc=hc, wbuf=wbuf: E.dma_start(
            out=wbuf.ap(),
            in_=env.dr["w3t"].ap()[hc * 128:(hc + 1) * 128, grp * 512:(grp + 1) * 512]),
            reads=[DK("w3t", hc * 128 * D * 4, (hc + 1) * 128 * D * 4)],
            writes=[wbuf.k()])

    def st2_quant(it):
        quantize_tile(b, env, w3i[it % 4], 0, 512, th3, A3b[it % 2], B3b[it % 2],
                      qw3b[it % 2], 0)

    for it in range(3):
        st2_load(it)
    st2_quant(0)
    for grp in range(4):
        for dcg in range(4):
            for tt in range(2):
                dco = grp * 4 + dcg
                xb_ = x1t[2 * dcg + tt]
                b.dma("y", lambda E, dco=dco, tt=tt, xb_=xb_: E.dma_start(
                    out=xb_.ap(),
                    in_=x_in.ap()[dco * 128:(dco + 1) * 128, tt * 512:(tt + 1) * 512]),
                    reads=[DK(x_in.name, dco * 128 * T * 4, (dco + 1) * 128 * T * 4)],
                    writes=[xb_.k()])
        for hc in range(NHC):
            it = grp * NHC + hc
            qw3 = qw3b[it % 2]
            if it + 1 < NIT:
                st2_quant(it + 1)
            if it + 3 < NIT:
                st2_load(it + 3)
            for dcg in range(4):
                ps = psA if dcg < 2 else psB
                pn = "psA" if dcg < 2 else "psB"
                for tt in range(2):
                    pcol = ((dcg % 2) * 2 + tt) * 512
                    b.op("t", lambda E, dcg=dcg, tt=tt, ps=ps, pcol=pcol, hc=hc, qw3=qw3: E.matmul(
                        out=ps.ap()[:, pcol:pcol + 512],
                        lhsT=qw3.ap(dcg * 128, (dcg + 1) * 128),
                        rhs=hidt[hc].ap(tt * 512, (tt + 1) * 512),
                        start=(hc == 0), stop=False),
                        reads=[qw3.k(dcg * 128, (dcg + 1) * 128),
                               hidt[hc].k(tt * 512, (tt + 1) * 512)],
                        writes=[PK(pn, pcol, pcol + 512)])
        for dcg in range(4):
            ps = psA if dcg < 2 else psB
            pn = "psA" if dcg < 2 else "psB"
            for tt in range(2):
                pcol = ((dcg % 2) * 2 + tt) * 512
                for rr in (rsh_hi, rsh_lo):
                    b.op("t", lambda E, dcg=dcg, tt=tt, ps=ps, pcol=pcol, rr=rr,
                         st=(rr is rsh_lo): E.matmul(
                        out=ps.ap()[:, pcol:pcol + 512], lhsT=env.c["nh_row_bf"]._ap,
                        rhs=rr._ap[:, tt * 512:(tt + 1) * 512], start=False, stop=st),
                        reads=[rr.k(tt * 512, (tt + 1) * 512), env.c["nh_row_bf"].k()],
                        writes=[PK(pn, pcol, pcol + 512)])
                dco = grp * 4 + dcg
                xb_ = x1t[2 * dcg + tt]
                ob_ = ostg[(2 * dcg + tt) % 4]
                b.op("v", lambda E, ps=ps, pcol=pcol, xb_=xb_, ob_=ob_: E.tensor_tensor(
                    out=ob_.ap(), in0=xb_.ap(), in1=ps.ap()[:, pcol:pcol + 512],
                    op=ALU.add),
                    reads=[xb_.k(), PK(pn, pcol, pcol + 512)], writes=[ob_.k()])
                b.dma("y", lambda E, dco=dco, tt=tt, ob_=ob_: E.dma_start(
                    out=x_out.ap()[dco * 128:(dco + 1) * 128, tt * 512:(tt + 1) * 512],
                    in_=ob_.ap()),
                    reads=[ob_.k()],
                    writes=[DK(x_out.name, dco * 128 * T * 4, (dco + 1) * 128 * T * 4)])


def emit_l1a(b, env, x_in, x_out):
    mk(b, "l1a:start")
    psA, psB = env.psA, env.psB
    xn = env.tv("big2", 0, BF16, NDC * T)
    rstd_row, mr_row = emit_ln(b, env, x_in, xn, keep_rows=True)
    x3pre = env.x3pre
    e1 = env.salloc(F32, NDC, "l1_e1")
    ccol = env.salloc(F32, NDC, "l1_ccol")
    b.op("s", lambda E: E.activation(out=e1.ap(), in_=env.c["lam_cols"].ap(),
                                     func=AF.Exp),
         reads=[env.c["lam_cols"].k()], writes=[e1.k()])
    b.op("s", lambda E: E.activation(out=e1.ap(), in_=e1.ap(), func=AF.Ln,
                                     bias=env.c["ones_col"].ap()),
         reads=[e1.k(), env.c["ones_col"].k()], writes=[e1.k()])
    b.op("v", lambda E: E.tensor_scalar(out=ccol.ap(), in0=e1.ap(), scalar1=-8.0,
                                        scalar2=None, op0=ALU.mult),
         reads=[e1.k()], writes=[ccol.k()])
    wgb = [env.tv("big3", i * 4096, BF16, D) for i in range(2)]
    wib = [env.tv("big3", 8192 + i * 4096, BF16, D) for i in range(2)]
    xt = [env.tv("big3", 16384 + i * 4096, F32, T) for i in range(2)]
    stg = [env.tv("big5", i * 4096, F32, T) for i in range(2)]
    srg = env.tv("big4", 0, F32, T)
    sigt = env.tv("big4", 4096, F32, T)
    alph = env.tv("big4", 8192, F32, T)
    ub = env.tv("big4", 12288, F32, T)
    betb = env.tv("big4", 16384, F32, T)
    xnf = env.tv("big4", 20480, F32, T)
    bx = env.tv("big4", 24576, F32, T)
    xb2 = env.tv("big4", 0, F32, T)
    hl = env.tv("big4", 12288, F32, T)
    zeros = env.tv("big4", 28672, F32, T)
    Acum = env.tv("big1", 0, BF16, NDC * T)
    A_cols = env.salloc(F32, NDC, "l1_acore")
    H_cols = env.salloc(F32, NDC, "l1_hcore")
    b.op("v", lambda E: E.memset(zeros.ap(), 0.0), writes=[zeros.k()])
    rstdb = env.tv("big5", 8192, F32, T)
    mrb = env.tv("big5", 12288, F32, T)
    for tt in range(2):
        b.op("t", lambda E, tt=tt: E.matmul(
            out=psA.ap()[:, tt * 512:(tt + 1) * 512], lhsT=env.c["ones_row"]._ap,
            rhs=rstd_row._ap[:, tt * 512:(tt + 1) * 512], start=True, stop=True),
            reads=[rstd_row.k(tt * 512, (tt + 1) * 512), env.c["ones_row"].k()],
            writes=[PK("psA", tt * 512, (tt + 1) * 512)])
        b.op("t", lambda E, tt=tt: E.matmul(
            out=psA.ap()[:, 1024 + tt * 512:1024 + (tt + 1) * 512],
            lhsT=env.c["ones_row"]._ap,
            rhs=mr_row._ap[:, tt * 512:(tt + 1) * 512], start=True, stop=True),
            reads=[mr_row.k(tt * 512, (tt + 1) * 512), env.c["ones_row"].k()],
            writes=[PK("psA", 1024 + tt * 512, 1024 + (tt + 1) * 512)])
    b.op("v", lambda E: E.tensor_copy(out=rstdb.ap(), in_=psA.ap()[:, 0:1024]),
         reads=[PK("psA", 0, 1024)], writes=[rstdb.k()])
    b.op("v", lambda E: E.tensor_copy(out=mrb.ap(), in_=psA.ap()[:, 1024:2048]),
         reads=[PK("psA", 1024, 2048)], writes=[mrb.k()])
    for ec in range(NDC):
        wg_, wi_, x_ = wgb[ec % 2], wib[ec % 2], xt[ec % 2]
        w_load(b, env, "wgat", ec * 128, 128, wg_, NDC, n_split=1)
        w_load(b, env, "wint", ec * 128, 128, wi_, NDC, n_split=1)
        b.dma("y", lambda E, ec=ec, x_=x_: E.dma_start(
            out=x_.ap(), in_=x_in.ap()[ec * 128:(ec + 1) * 128, :]),
            reads=[DK(x_in.name, ec * 128 * T * 4, (ec + 1) * 128 * T * 4)],
            writes=[x_.k()])
        for tt in range(2):
            for dc in range(NDC):
                b.op("t", lambda E, tt=tt, dc=dc, wg_=wg_: E.matmul(
                    out=psA.ap()[:, tt * 512:(tt + 1) * 512],
                    lhsT=wg_.ap(dc * 128, (dc + 1) * 128),
                    rhs=xn.ap(dc * T + tt * 512, dc * T + (tt + 1) * 512),
                    start=(dc == 0), stop=(dc == NDC - 1)),
                    reads=[wg_.k(dc * 128, (dc + 1) * 128),
                           xn.k(dc * T + tt * 512, dc * T + (tt + 1) * 512)],
                    writes=[PK("psA", tt * 512, (tt + 1) * 512)])
            for dc in range(NDC):
                b.op("t", lambda E, tt=tt, dc=dc, wi_=wi_: E.matmul(
                    out=psB.ap()[:, tt * 512:(tt + 1) * 512],
                    lhsT=wi_.ap(dc * 128, (dc + 1) * 128),
                    rhs=xn.ap(dc * T + tt * 512, dc * T + (tt + 1) * 512),
                    start=(dc == 0), stop=(dc == NDC - 1)),
                    reads=[wi_.k(dc * 128, (dc + 1) * 128),
                           xn.k(dc * T + tt * 512, dc * T + (tt + 1) * 512)],
                    writes=[PK("psB", tt * 512, (tt + 1) * 512)])
        b.op("s", lambda E, ec=ec: E.activation(
            out=srg.ap(), in_=psA.ap()[:, 0:1024], func=AF.Sigmoid,
            bias=env.c["bgat_cols"].ap(ec, ec + 1)),
            reads=[PK("psA", 0, 1024), env.c["bgat_cols"].k(ec, ec + 1)],
            writes=[srg.k()])
        b.op("s", lambda E, ec=ec: E.activation(
            out=sigt.ap(), in_=psB.ap()[:, 0:1024], func=AF.Sigmoid,
            bias=env.c["bin_cols"].ap(ec, ec + 1)),
            reads=[PK("psB", 0, 1024), env.c["bin_cols"].k(ec, ec + 1)],
            writes=[sigt.k()])
        b.op("s", lambda E, ec=ec: E.activation(
            out=alph.ap(), in_=srg.ap(), func=AF.Exp, scale=ccol.ap(ec, ec + 1)),
            reads=[srg.k(), ccol.k(ec, ec + 1)], writes=[alph.k()])
        b.op("s", lambda E: E.activation(out=srg.ap(), in_=alph.ap(), func=AF.Square),
             reads=[alph.k()], writes=[srg.k()])
        b.op("s", lambda E: E.activation(out=ub.ap(), in_=srg.ap(), func=AF.Ln,
                                         scale=-1.0, bias=env.c["eps1p_col"].ap()),
             reads=[srg.k(), env.c["eps1p_col"].k()], writes=[ub.k()])
        b.op("s", lambda E: E.activation(out=betb.ap(), in_=ub.ap(), func=AF.Exp,
                                         scale=0.5),
             reads=[ub.k()], writes=[betb.k()])
        b.op("v", lambda E, x_=x_: E.tensor_tensor(out=xnf.ap(), in0=x_.ap(),
                                                   in1=rstdb.ap(), op=ALU.mult),
             reads=[x_.k(), rstdb.k()], writes=[xnf.k()])
        b.op("v", lambda E: E.tensor_tensor(out=xnf.ap(), in0=xnf.ap(),
                                            in1=mrb.ap(), op=ALU.subtract),
             reads=[xnf.k(), mrb.k()], writes=[xnf.k()])
        b.op("v", lambda E: E.tensor_tensor(out=bx.ap(), in0=betb.ap(), in1=xnf.ap(),
                                            op=ALU.mult),
             reads=[betb.k(), xnf.k()], writes=[bx.k()])
        b.op("v", lambda E: E.tensor_tensor(out=xb2.ap(), in0=bx.ap(), in1=sigt.ap(),
                                            op=ALU.mult),
             reads=[bx.k(), sigt.k()], writes=[xb2.k()])
        b.op("v", lambda E: E.tensor_tensor_scan(out=hl.ap(), data0=alph.ap(),
                                                 data1=xb2.ap(), initial=0.0,
                                                 op0=ALU.mult, op1=ALU.add),
             reads=[alph.k(), xb2.k()], writes=[hl.k()])
        st_ = stg[ec % 2]
        b.op("v", lambda E, x_=x_, st_=st_: E.tensor_tensor(
            out=st_.ap(), in0=x_.ap(), in1=hl.ap(), op=ALU.add),
            reads=[x_.k(), hl.k()], writes=[st_.k()])
        b.dma("y", lambda E, ec=ec, st_=st_: E.dma_start(
            out=x3pre.ap()[ec * 128:(ec + 1) * 128, :], in_=st_.ap()),
            reads=[st_.k()],
            writes=[DK("x3pre", ec * 128 * T * 4, (ec + 1) * 128 * T * 4)])
        b.op("v", lambda E, ec=ec: E.tensor_tensor_scan(
            out=Acum.ap(ec * T, (ec + 1) * T), data0=alph.ap(), data1=zeros.ap(),
            initial=1.0, op0=ALU.mult, op1=ALU.add),
            reads=[alph.k(), zeros.k()], writes=[Acum.k(ec * T, (ec + 1) * T)])
        b.op("v", lambda E, ec=ec: E.tensor_reduce(
            out=A_cols.ap(ec, ec + 1), in_=alph.ap(), axis=AX.X, op=ALU.mult),
            reads=[alph.k()], writes=[A_cols.k(ec, ec + 1)])
        b.op("v", lambda E, ec=ec: E.tensor_copy(
            out=H_cols.ap(ec, ec + 1), in_=hl.ap(T - 1, T)),
            reads=[hl.k(T - 1, T)], writes=[H_cols.k(ec, ec + 1)])
    b.dma("p", lambda E: E.dma_start(out=env.dr["ag2_in"].ap()[:, 0:16],
                                     in_=A_cols.ap()),
          reads=[A_cols.k()], writes=[DK("ag2_in", 0, 64)])
    b.dma("p", lambda E: E.dma_start(out=env.dr["ag2_in"].ap()[:, 16:32],
                                     in_=H_cols.ap()),
          reads=[H_cols.k()], writes=[DK("ag2_in", 64, 128)])
    b.op("p", lambda E: E.collective_compute(
        "AllGather", ALU.bypass, replica_groups=[list(range(NCORE))],
        ins=[env.dr["ag2_in"].ap()], outs=[env.dr["ag2_out"].ap()]),
        reads=[DK("ag2_in")], writes=[DK("ag2_out")])
    agb = env.salloc(F32, 256, "l1_agb")
    b.dma("y", lambda E: E.dma_start(
        out=agb.ap().rearrange("p (r c) -> p r c", r=NCORE),
        in_=env.dr["ag2_out"].ap().rearrange("(r p) c -> p r c", p=128)),
        reads=[DK("ag2_out")], writes=[agb.k()])
    gcol = env.salloc(F32, NDC, "l1_hinit")
    aeff = env.salloc(F32, NDC, "l1_aeff")
    heff = env.salloc(F32, NDC, "l1_heff")
    b.op("v", lambda E: E.memset(gcol.ap(), 0.0), writes=[gcol.k()])
    for j in range(NCORE):
        b.op("v", lambda E, j=j: E.tensor_scalar(
            out=aeff.ap(), in0=agb.ap(j * 32, j * 32 + 16),
            scalar1=env.c["mcol"].ap(j, j + 1), scalar2=env.c["onem"].ap(j, j + 1),
            op0=ALU.mult, op1=ALU.add),
            reads=[agb.k(j * 32, j * 32 + 16), env.c["mcol"].k(j, j + 1),
                   env.c["onem"].k(j, j + 1)],
            writes=[aeff.k()])
        b.op("v", lambda E, j=j: E.tensor_scalar(
            out=heff.ap(), in0=agb.ap(j * 32 + 16, j * 32 + 32),
            scalar1=env.c["mcol"].ap(j, j + 1), scalar2=None, op0=ALU.mult),
            reads=[agb.k(j * 32 + 16, j * 32 + 32), env.c["mcol"].k(j, j + 1)],
            writes=[heff.k()])
        for ec in range(NDC):
            b.op("v", lambda E, ec=ec: E.scalar_tensor_tensor(
                out=gcol.ap(ec, ec + 1), in0=gcol.ap(ec, ec + 1),
                scalar=aeff.ap(ec, ec + 1), in1=heff.ap(ec, ec + 1),
                op0=ALU.mult, op1=ALU.add),
                reads=[gcol.k(ec, ec + 1), aeff.k(ec, ec + 1), heff.k(ec, ec + 1)],
                writes=[gcol.k(ec, ec + 1)])
    mk(b, "l1a:corr")
    for ec in range(NDC):
        x_ = xt[ec % 2]
        st_ = stg[ec % 2]
        b.dma("y", lambda E, ec=ec, x_=x_: E.dma_start(
            out=x_.ap(), in_=x3pre.ap()[ec * 128:(ec + 1) * 128, :]),
            reads=[DK("x3pre", ec * 128 * T * 4, (ec + 1) * 128 * T * 4)],
            writes=[x_.k()])
        b.op("v", lambda E, ec=ec, x_=x_, st_=st_: E.scalar_tensor_tensor(
            out=st_.ap(), in0=Acum.ap(ec * T, (ec + 1) * T),
            scalar=gcol.ap(ec, ec + 1), in1=x_.ap(), op0=ALU.mult, op1=ALU.add),
            reads=[Acum.k(ec * T, (ec + 1) * T), gcol.k(ec, ec + 1), x_.k()],
            writes=[st_.k()])
        b.dma("y", lambda E, ec=ec, st_=st_: E.dma_start(
            out=x_out.ap()[ec * 128:(ec + 1) * 128, :], in_=st_.ap()),
            reads=[st_.k()],
            writes=[DK(x_out.name, ec * 128 * T * 4, (ec + 1) * 128 * T * 4)])


def emit_l1f(b, env, x_in, x_out):
    mk(b, "l1f:start")
    psA, psB = env.psA, env.psB
    xn = env.tv("big2", 0, BF16, NDC * T)
    emit_ln(b, env, x_in, xn)
    fuse = env.tv("big1", 0, BF16, NOC * 512)
    wgb = [env.tv("big3", i * 4096, BF16, D) for i in range(2)]
    wub = [env.tv("big3", 8192 + i * 4096, BF16, D) for i in range(2)]
    wdb = [env.tv("big3", i * 16384, BF16, NOC * 128) for i in range(2)]
    sgt = [env.tv("big5", i * 2048, F32, 512) for i in range(2)]
    ut = [env.tv("big5", 4096 + i * 2048, F32, 512) for i in range(2)]
    xt = [env.tv("big5", 8192 + i * 2048, F32, 512) for i in range(2)]
    ostg = [env.tv("big5", 12288 + i * 2048, F32, 512) for i in range(2)]
    for th in range(2):
        for oc in range(NOC):
            wg_, wu_ = wgb[oc % 2], wub[oc % 2]
            w_load(b, env, "wg8", oc * 128, 128, wg_, NDC, n_split=1)
            w_load(b, env, "wu8", oc * 128, 128, wu_, NDC, n_split=1)
            for dc in range(NDC):
                b.op("t", lambda E, th=th, dc=dc, oc=oc, wg_=wg_: E.matmul(
                    out=psA.ap()[:, (oc % 2) * 512:(oc % 2) * 512 + 512],
                    lhsT=wg_.ap(dc * 128, (dc + 1) * 128),
                    rhs=xn.ap(dc * T + th * 512, dc * T + (th + 1) * 512),
                    start=(dc == 0), stop=(dc == NDC - 1)),
                    reads=[wg_.k(dc * 128, (dc + 1) * 128),
                           xn.k(dc * T + th * 512, dc * T + (th + 1) * 512)],
                    writes=[PK("psA", (oc % 2) * 512, (oc % 2) * 512 + 512)])
            b.op("s", lambda E, oc=oc: E.activation(
                out=sgt[oc % 2].ap(), in_=psA.ap()[:, (oc % 2) * 512:(oc % 2) * 512 + 512],
                func=AF.Silu, scale=env.c["sg_cols"].ap(oc, oc + 1)),
                reads=[PK("psA", (oc % 2) * 512, (oc % 2) * 512 + 512),
                       env.c["sg_cols"].k(oc, oc + 1)],
                writes=[sgt[oc % 2].k()])
            for dc in range(NDC):
                b.op("t", lambda E, th=th, dc=dc, oc=oc, wu_=wu_: E.matmul(
                    out=psB.ap()[:, (oc % 2) * 512:(oc % 2) * 512 + 512],
                    lhsT=wu_.ap(dc * 128, (dc + 1) * 128),
                    rhs=xn.ap(dc * T + th * 512, dc * T + (th + 1) * 512),
                    start=(dc == 0), stop=(dc == NDC - 1)),
                    reads=[wu_.k(dc * 128, (dc + 1) * 128),
                           xn.k(dc * T + th * 512, dc * T + (th + 1) * 512)],
                    writes=[PK("psB", (oc % 2) * 512, (oc % 2) * 512 + 512)])
            b.op("v", lambda E, oc=oc: E.tensor_scalar(
                out=ut[oc % 2].ap(), in0=psB.ap()[:, (oc % 2) * 512:(oc % 2) * 512 + 512],
                scalar1=env.c["su_cols"].ap(oc, oc + 1), scalar2=None, op0=ALU.mult),
                reads=[PK("psB", (oc % 2) * 512, (oc % 2) * 512 + 512),
                       env.c["su_cols"].k(oc, oc + 1)],
                writes=[ut[oc % 2].k()])
            b.op("v", lambda E, oc=oc: E.tensor_tensor(
                out=fuse.ap(oc * 512, (oc + 1) * 512), in0=sgt[oc % 2].ap(),
                in1=ut[oc % 2].ap(), op=ALU.mult),
                reads=[sgt[oc % 2].k(), ut[oc % 2].k()],
                writes=[fuse.k(oc * 512, (oc + 1) * 512)])
        for dco in range(NDC):
            wd_ = wdb[dco % 2]
            w_load(b, env, "wd8", dco * 128, 128, wd_, NOC, n_split=1)
            for oc in range(NOC):
                b.op("t", lambda E, dco=dco, oc=oc, wd_=wd_: E.matmul(
                    out=psA.ap()[:, (dco % 4) * 512:(dco % 4) * 512 + 512],
                    lhsT=wd_.ap(oc * 128, (oc + 1) * 128),
                    rhs=fuse.ap(oc * 512, (oc + 1) * 512),
                    start=(oc == 0), stop=(oc == NOC - 1)),
                    reads=[wd_.k(oc * 128, (oc + 1) * 128),
                           fuse.k(oc * 512, (oc + 1) * 512)],
                    writes=[PK("psA", (dco % 4) * 512, (dco % 4) * 512 + 512)])
            x_ = xt[dco % 2]
            o_ = ostg[dco % 2]
            b.dma("y", lambda E, dco=dco, th=th, x_=x_: E.dma_start(
                out=x_.ap(),
                in_=x_in.ap()[dco * 128:(dco + 1) * 128, th * 512:(th + 1) * 512]),
                reads=[DK(x_in.name, dco * 128 * T * 4, (dco + 1) * 128 * T * 4)],
                writes=[x_.k()])
            b.op("v", lambda E, dco=dco, x_=x_, o_=o_: E.scalar_tensor_tensor(
                out=o_.ap(), in0=psA.ap()[:, (dco % 4) * 512:(dco % 4) * 512 + 512],
                scalar=env.c["sd_cols"].ap(dco, dco + 1), in1=x_.ap(),
                op0=ALU.mult, op1=ALU.add),
                reads=[PK("psA", (dco % 4) * 512, (dco % 4) * 512 + 512),
                       env.c["sd_cols"].k(dco, dco + 1), x_.k()],
                writes=[o_.k()])
            b.dma("y", lambda E, dco=dco, th=th, o_=o_: E.dma_start(
                out=x_out.ap()[dco * 128:(dco + 1) * 128, th * 512:(th + 1) * 512],
                in_=o_.ap()),
                reads=[o_.k()],
                writes=[DK(x_out.name, dco * 128 * T * 4, (dco + 1) * 128 * T * 4)])


L0A_STAGE = 3

EMIT = {"l0a": emit_l0a, "l0f": emit_l0f, "l1a": emit_l1a, "l1f": emit_l1f}
GROUPS = [["l0a", "l0f", "l1a", "l1f"]]

_prog_cache = {}


def build_program(blocks):
    key = tuple(blocks)
    if key in _prog_cache:
        return _prog_cache[key]
    nc = bass.Bass("TRN2", target_bir_lowering=False)
    env = make_env(nc, blocks, len(blocks) == 4)
    b = B(nc)
    load_small_consts(b, env)
    if "l0f" in blocks:
        from collections import deque
        env.gpump = deque()
        env.gsize = {}
        emit_gamma_stream(b, env, "w1t", "th1")
        emit_gamma_stream(b, env, "w2t", "th2")
        emit_gamma_stream(b, env, "w3t", "th3")
    else:
        from collections import deque
        env.gpump = deque()
    for i, blk in enumerate(blocks):
        EMIT[blk](b, env, env.xbuf[i], env.xbuf[i + 1])
    b.finish()
    b.build()
    _prog_cache[key] = (nc, env)
    return nc, env


NEED = {
    "common": ["permpm", "permpm_bf", "idbf", "ones_col", "ones_col_bf", "ones_row",
               "nh_row", "nh_row_bf", "ccr", "eps1p_col"],
    "l0a": ["cosf", "sinf", "dt_tab", "dq_tab", "dk_tab", "lamc_tab", "coefc",
            "wq", "wk", "wv", "wo", "wg1", "wg2", "nw_cols"],
    "l0f": ["w1t", "w2t", "w3t"],
    "l1a": ["wint", "wgat", "bin_cols", "bgat_cols", "lam_cols", "mcol", "onem"],
    "l1f": ["wg8", "wu8", "wd8", "sg_cols", "su_cols", "sd_cols"],
}
PER_CORE = {"cosf", "sinf", "coefc", "mcol", "onem"}


def _prep(inputs):
    f32 = np.float32
    g = {}
    g["wq"] = np.ascontiguousarray(np.asarray(inputs["L0_attn_Wq"], f32))
    g["wk"] = np.ascontiguousarray(np.asarray(inputs["L0_attn_Wk"], f32))
    g["wv"] = np.ascontiguousarray(np.asarray(inputs["L0_attn_Wv"], f32))
    g["wo"] = np.ascontiguousarray(np.asarray(inputs["L0_attn_Wo"], f32))
    g["wg1"] = np.ascontiguousarray(np.asarray(inputs["L0_attn_Wg1"], f32))
    g["wg2"] = np.ascontiguousarray(np.asarray(inputs["L0_attn_Wg2"], f32))
    g["nw_cols"] = _cols(np.asarray(inputs["L0_attn_norm_weight"], f32))
    g["w1t"] = np.ascontiguousarray(np.asarray(inputs["L0_ffn_w1"], f32).T)
    g["w2t"] = np.ascontiguousarray(np.asarray(inputs["L0_ffn_w2"], f32).T)
    g["w3t"] = np.ascontiguousarray(np.asarray(inputs["L0_ffn_w3"], f32).T)
    g["wint"] = np.ascontiguousarray(np.asarray(inputs["L1_attn_W_in"], f32).T)
    g["wgat"] = np.ascontiguousarray(np.asarray(inputs["L1_attn_W_gate"], f32).T)
    g["bin_cols"] = _cols(np.asarray(inputs["L1_attn_b_in"], f32))
    g["bgat_cols"] = _cols(np.asarray(inputs["L1_attn_b_gate"], f32))
    g["lam_cols"] = _cols(np.asarray(inputs["L1_attn_forget_lambda"], f32))
    g["wg8"] = np.ascontiguousarray(np.asarray(inputs["L1_ffn_w_gate"], np.int8))
    g["wu8"] = np.ascontiguousarray(np.asarray(inputs["L1_ffn_w_up"], np.int8))
    g["wd8"] = np.ascontiguousarray(np.asarray(inputs["L1_ffn_w_down"], np.int8))
    g["sg_cols"] = _cols(np.asarray(inputs["L1_ffn_s_gate"], f32))
    g["su_cols"] = _cols(np.asarray(inputs["L1_ffn_s_up"], f32))
    g["sd_cols"] = _cols(np.asarray(inputs["L1_ffn_s_down"], f32))
    g["ones_col"] = np.ones((128, 1), f32)
    g["ones_col_bf"] = np.ones((128, 1), ml_dtypes.bfloat16)
    g["ones_row"] = np.ones((1, 128), f32)
    g["nh_row"] = np.full((1, 128), -0.5, f32)
    g["nh_row_bf"] = np.full((1, 128), -0.5, ml_dtypes.bfloat16)
    g["permpm_bf"] = None
    ccr = np.zeros((1, 8), f32)
    ccr[0, 0] = 1e-5
    ccr[0, 1] = 0.5e-8
    g["ccr"] = ccr
    g["eps1p_col"] = np.full((128, 1), f32(1.0) + f32(1e-6), f32)
    x = np.asarray(inputs["L0_attn_x"], f32)[0]
    percore = []
    for c in range(NCORE):
        hc = _host_const(c)
        d = {"cosf": hc["cosf"], "sinf": hc["sinf"], "coefc": hc["coefc"],
             "mcol": hc["mcol"], "onem": hc["onem"]}
        d["x_fm"] = np.ascontiguousarray(x[c * T:(c + 1) * T, :].T)
        percore.append(d)
    g["permpm"] = hc["permpm"]
    g["permpm_bf"] = hc["permpm"].astype(ml_dtypes.bfloat16)
    g["idbf"] = hc["idbf"]
    g["dt_tab"] = hc["dt_tab"]
    g["dq_tab"] = hc["dq_tab"]
    g["dk_tab"] = hc["dk_tab"]
    g["lamc_tab"] = hc["lamc_tab"]
    return g, percore


def kernel(**inputs):
    g, percore = _prep(inputs)
    x_cur = [percore[c]["x_fm"] for c in range(NCORE)]
    for blocks in GROUPS:
        nc, env = build_program(blocks)
        names = list(NEED["common"])
        for blk in blocks:
            names += NEED[blk]
        in_maps = []
        for c in range(NCORE):
            m = {}
            for nm in names:
                m[nm] = percore[c][nm] if nm in PER_CORE else g[nm]
            m["x_fm"] = x_cur[c]
            in_maps.append(m)
        res = run_bass_kernel_spmd(nc, in_maps, core_ids=list(range(NCORE)))
        x_cur = [np.asarray(res.results[c]["y_fm"]) for c in range(NCORE)]
    out = np.empty((1, SEQ, D), np.float32)
    for c in range(NCORE):
        out[0, c * T:(c + 1) * T, :] = x_cur[c].T
    return out


# version 17 (final)
# speedup vs baseline: 1.0380x; 1.0028x over previous
import math
import numpy as np
import ml_dtypes

import concourse.bass as bass
import concourse.mybir as mybir
from concourse.bass_utils import run_bass_kernel_spmd

F32 = mybir.dt.float32
BF16 = mybir.dt.bfloat16
I8 = mybir.dt.int8
AF = mybir.ActivationFunctionType
ALU = mybir.AluOpType
AX = mybir.AxisListType

NCORE = 8
T = 1024
D = 2048
NDC = D // 128
H = 16
HD = 128
C = 128
NTC = T // C
FF = 5632
NHC = FF // 128
INTER = 8192
NOC = INTER // 128
SEQ = 8192

MB = 8192
MARKS = None


def mk(b, name):
    if MARKS is not None:
        MARKS.append((name, {e: len(b.rec[e]) for e in ENGS}))


def _f32(x):
    return np.asarray(x, np.float32)


def _host_const(core):
    f32 = np.float32
    t0 = core * T
    half = HD // 2
    theta = np.power(f32(10000.0), (f32(-2.0) * np.arange(half, dtype=f32) / f32(HD))).astype(f32)
    pos = np.arange(t0, t0 + T, dtype=f32)
    ang = (pos[:, None] * theta[None, :]).astype(f32)
    cos = np.cos(ang).astype(f32)
    sin = np.sin(ang).astype(f32)
    cosf = np.zeros((128, T), f32)
    sinf = np.zeros((128, T), f32)
    cosf[:64, :] = cos.T
    cosf[64:, :] = cos.T
    sinf[:64, :] = -sin.T
    sinf[64:, :] = sin.T
    hh = np.arange(H, dtype=f32)
    slopes = np.power(f32(2.0), (f32(-8.0) * (hh + f32(1.0)) / f32(H))).astype(f32)
    llam = (-slopes).astype(f32)
    i = np.arange(C, dtype=f32)
    diff = i[:, None] - i[None, :]
    dt_tab = np.zeros((128, D), f32)
    dq_tab = np.zeros((128, D), f32)
    dk_tab = np.zeros((128, D), f32)
    lamc_tab = np.zeros((128, D), f32)
    for h in range(H):
        Dh = np.where(diff >= 0, np.exp((llam[h] * diff).astype(f32)).astype(f32), f32(0.0))
        dt_tab[:, h * 128:(h + 1) * 128] = Dh.T
        dq_tab[:, h * 128:(h + 1) * 128] = np.exp((llam[h] * (i + 1)).astype(f32)).astype(f32)[None, :]
        dk_tab[:, h * 128:(h + 1) * 128] = np.exp((llam[h] * (C - 1 - i)).astype(f32)).astype(f32)[:, None]
        lamc_tab[:, h * 128:(h + 1) * 128] = np.exp(f32(llam[h] * C))
    coefc = np.zeros((128, 128), f32)
    for j in range(core):
        p = core - 1 - j
        for h in range(H):
            coefc[:, j * 16 + h] = f32(np.exp(np.float64(llam[h]) * C * NTC * p))
    permpm = np.zeros((128, 128), f32)
    for e in range(128):
        permpm[(e + 64) % 128, e] = 1.0
    idbf = np.eye(128).astype(ml_dtypes.bfloat16)
    mcol = np.zeros((128, 8), f32)
    mcol[:, :core] = 1.0
    onem = 1.0 - mcol
    def _fl(a):
        return np.where(np.abs(a) < 1e-30, f32(0.0), a).astype(f32)

    return dict(cosf=cosf, sinf=sinf, dt_tab=_fl(dt_tab), dq_tab=_fl(dq_tab),
                dk_tab=_fl(dk_tab), lamc_tab=_fl(lamc_tab), coefc=_fl(coefc),
                permpm=permpm, idbf=idbf, mcol=mcol, onem=onem)


def _cols(v, n=128):
    v = np.asarray(v)
    k = v.size // n
    return np.ascontiguousarray(v.reshape(k, n).T)


ENGS = ("t", "v", "s", "p", "y")


DMA_K = 12


class B:
    def __init__(self, nc):
        self.nc = nc
        self.Q = {}
        self.cnt = {}
        self.rec = {e: [] for e in ENGS}
        self.writers = {}
        self.readers = {}
        self.waited = {}
        self.dma_rr = {}

    def sem(self, q):
        if q not in self.Q:
            self.Q[q] = self.nc.alloc_semaphore(name="sm_" + q)
            self.cnt[q] = 0
        return self.Q[q]

    def _need(self, e, deps):
        best = {}
        for (q, v) in deps:
            if e == "t" and q == "t":
                continue
            if v > best.get(q, -1):
                best[q] = v
        for q, v in best.items():
            if self.waited.get((e, q), 0) >= v:
                continue
            self.rec[e].append(("w", q, v))
            self.waited[(e, q)] = v

    def _dep_read(self, keys, eng=None):
        deps = []
        for (sp, lo, hi) in keys:
            for (l, h, q, c) in self.writers.get(sp, ()):
                if l < hi and lo < h:
                    deps.append((q, c))
            if sp.startswith("ps"):
                for (l, h, q, c) in self.readers.get(sp, ()):
                    if l < hi and lo < h and q != eng:
                        deps.append((q, c))
        return deps

    def _dep_write(self, keys):
        deps = []
        for (sp, lo, hi) in keys:
            for lst in (self.writers.get(sp, ()), self.readers.get(sp, ())):
                for (l, h, q, c) in lst:
                    if l < hi and lo < h:
                        deps.append((q, c))
        return deps

    def _register(self, q, c, reads, writes):
        for (sp, lo, hi) in reads:
            self.readers.setdefault(sp, []).append((lo, hi, q, c))
        for (sp, lo, hi) in writes:
            w = self.writers.setdefault(sp, [])
            self.writers[sp] = [e for e in w if not (lo <= e[0] and e[1] <= hi)]
            r = self.readers.get(sp)
            if r:
                self.readers[sp] = [e for e in r if not (lo <= e[0] and e[1] <= hi)]
            self.writers[sp].append((lo, hi, q, c))

    def _emit(self, e, q, inc, fn, reads, writes):
        self.sem(q)
        deps = self._dep_read(reads, e) + self._dep_write(writes)
        self._need(e, deps)
        self.cnt[q] += inc
        self.rec[e].append(("o", fn, q, inc))
        self._register(q, self.cnt[q], reads, writes)

    def op(self, e, fn, reads=(), writes=()):
        self._emit(e, e, 1, fn, reads, writes)

    def dma(self, e, fn, reads=(), writes=()):
        qb = e + "d"
        idx = self.dma_rr.get(qb, 0)
        self.dma_rr[qb] = (idx + 1) % DMA_K
        q = "%s%02d" % (qb, idx)
        self.sem(q)
        deps = self._dep_read(reads, e) + self._dep_write(writes)
        if self.cnt[q]:
            deps.append((q, self.cnt[q]))
        self._need(e, deps)
        self.cnt[q] += 16
        self.rec[e].append(("o", fn, q, 16))
        self._register(q, self.cnt[q], reads, writes)

    def finish(self):
        for q, n in self.cnt.items():
            e = q[0]
            assert e in ENGS
            if self.waited.get((e, q), 0) < n:
                self.rec[e].append(("w", q, n))
                self.waited[(e, q)] = n
        for q, n in self.cnt.items():
            if self.waited.get(("y", q), 0) < n:
                self.rec["y"].append(("w", q, n))
                self.waited[("y", q)] = n

    def _batch(self, items):
        out = []
        run = []

        def flush():
            last = {}
            for idx, it in enumerate(run):
                q = it[2]
                if len(q) == 1:
                    last[q] = idx
            pend = {}
            for idx, it in enumerate(run):
                q = it[2]
                if len(q) == 1:
                    pend[q] = pend.get(q, 0) + it[3]
                    if idx == last[q]:
                        out.append(("o", it[1], q, pend[q]))
                        pend[q] = 0
                    else:
                        out.append(("o", it[1], None, 0))
                else:
                    out.append(it)
            run.clear()

        for it in items:
            if it[0] == "w":
                flush()
                out.append(it)
            else:
                run.append(it)
        flush()
        return out

    def build(self):
        with self.nc.Block() as block:
            emitters = {"t": block.tensor, "v": block.vector, "s": block.scalar,
                        "p": block.gpsimd, "y": block.sync}
            for e in ENGS:
                items = self.rec[e]
                if not items:
                    continue

                def body(eng, items=items):
                    for it in items:
                        if it[0] == "w":
                            eng.wait_ge(self.Q[it[1]], it[2])
                        else:
                            inst = it[1](eng)
                            if it[2] is not None:
                                inst.then_inc(self.Q[it[2]], it[3])

                emitters[e](body)


class TV:

    def __init__(self, pools, pool, off, dtype, f, p0=0, p1=128):
        self.pool = pool
        self.off = off
        self.dtype = dtype
        self.f = f
        es = mybir.dt.size(dtype)
        self.es = es
        self._ap = pools[pool].ap()[p0:p1, off:off + f * es].bitcast(dtype)

    def ap(self, lo=0, hi=None):
        hi = self.f if hi is None else hi
        return self._ap[:, lo:hi]

    def k(self, lo=0, hi=None):
        hi = self.f if hi is None else hi
        return (self.pool, self.off + lo * self.es, self.off + hi * self.es)


def PK(name, col_lo, col_hi):
    lo = (col_lo * 4 // 2048) * 2048
    hi = -((-(col_hi * 4)) // 2048) * 2048
    return (name, lo, hi)


def DK(name, lo=0, hi=1 << 62):
    return (name, lo, hi)


class Env:
    pass


def make_env(nc, blocks, fused):
    env = Env()
    env.nc = nc
    env.blocks = blocks
    dr = {}
    env.dr = dr

    def din(name, shape, dtype=F32):
        dr[name] = nc.dram_tensor(name, list(shape), dtype, kind="ExternalInput")
        return dr[name]

    def dint(name, shape, dtype=F32):
        dr[name] = nc.dram_tensor(name, list(shape), dtype, kind="Internal")
        return dr[name]

    def dout(name, shape, dtype=F32):
        dr[name] = nc.dram_tensor(name, list(shape), dtype, kind="ExternalOutput")
        return dr[name]

    env.din, env.dint, env.dout = din, dint, dout

    din("permpm", (128, 128))
    din("idbf", (128, 128), BF16)
    din("ones_col", (128, 1))
    din("ones_col_bf", (128, 1), BF16)
    din("ones_row", (1, 128))
    din("nh_row", (1, 128))
    din("nh_row_bf", (1, 128), BF16)
    din("permpm_bf", (128, 128), BF16)
    din("ccr", (1, 8))
    din("eps1p_col", (128, 1))
    if "l0a" in blocks:
        din("cosf", (128, T)); din("sinf", (128, T))
        din("dt_tab", (128, D)); din("dq_tab", (128, D)); din("dk_tab", (128, D))
        din("lamc_tab", (128, D)); din("coefc", (128, 128))
        din("wq", (D, D)); din("wk", (D, D)); din("wv", (D, D)); din("wo", (D, D))
        din("wg1", (D, 128)); din("wg2", (128, D)); din("nw_cols", (128, NDC))
        dint("qrot_d", (D, T), BF16); dint("krot_d", (D, T), BF16)
        dint("v_d", (T, D), BF16); dint("vdk_d", (T, D), BF16)
        dint("ag1_in", (128, D)); dint("ag1_out", (NCORE * 128, D))
    if "l0f" in blocks:
        din("w1t", (D, FF)); din("w2t", (D, FF)); din("w3t", (FF, D))
    if "l1a" in blocks:
        din("wint", (D, D)); din("wgat", (D, D))
        din("bin_cols", (128, NDC)); din("bgat_cols", (128, NDC))
        din("lam_cols", (128, NDC))
        din("mcol", (128, 8)); din("onem", (128, 8))
        dint("ag2_in", (128, 32)); dint("ag2_out", (NCORE * 128, 32))
    if "l1f" in blocks:
        din("wg8", (D, INTER), I8); din("wu8", (D, INTER), I8); din("wd8", (INTER, D), I8)
        din("sg_cols", (128, NOC)); din("su_cols", (128, NOC)); din("sd_cols", (128, NDC))

    env.xbuf = []
    for i in range(len(blocks) + 1):
        if i == 0:
            env.xbuf.append(din("x_fm", (D, T)))
        elif i == len(blocks):
            env.xbuf.append(dout("y_fm", (D, T)))
        else:
            env.xbuf.append(dint(f"xi{i}", (D, T)))
    if "l1a" in blocks:
        env.x3pre = dint("x3pre", (D, T))

    pools = {}
    for name, nb in (("big1", 65536), ("big2", 32768), ("big3", 32768),
                     ("big4", 32768), ("big5", 38912)):
        pools[name] = nc.alloc_sbuf_tensor(name, [128, nb], mybir.dt.uint8)
    env.pools = pools
    env.tv = lambda pool, off, dtype, f, p0=0, p1=128: TV(pools, pool, off, dtype, f, p0, p1)

    env.sm = nc.alloc_sbuf_tensor("smalls", [128, 6144], mybir.dt.uint8)
    env.sm_off = [0]

    env.sm_cache = {}

    def salloc(dtype, f, name):
        if name in env.sm_cache:
            return env.sm_cache[name]
        es = mybir.dt.size(dtype)
        off = (env.sm_off[0] + 31) // 32 * 32
        env.sm_off[0] = off + f * es
        assert env.sm_off[0] <= 6144, "smalls overflow"
        tv = TV({"smalls": env.sm}, "smalls", off, dtype, f)
        env.sm_cache[name] = tv
        return tv

    env.salloc = salloc

    env.rw = nc.alloc_sbuf_tensor("rowsb", [1, 2048], mybir.dt.uint8)
    env.rw_off = [0]
    env.rw_cache = {}

    def row(name, f=T, dtype=F32):
        if name in env.rw_cache:
            return env.rw_cache[name]
        es = mybir.dt.size(dtype)
        off = (env.rw_off[0] + 31) // 32 * 32
        env.rw_off[0] = off + f * es
        assert env.rw_off[0] <= 2048, "rowsb overflow"
        tv = TV({"rowsb": env.rw}, "rowsb", off, dtype, f, 0, 1)
        env.rw_cache[name] = tv
        return tv

    env.row = row

    def prow(pool, off, f=T, dtype=F32):
        return TV(pools, pool, off, dtype, f, 0, 1)

    env.prow = prow

    env.psA = nc.alloc_psum_tensor("psA", [128, 2048], F32)
    env.psB = nc.alloc_psum_tensor("psB", [128, 2048], F32)

    env.c = {}
    return env


def load_small_consts(b, env):
    names = [("permpm", F32, 128), ("permpm_bf", BF16, 128), ("idbf", BF16, 128), ("ones_col", F32, 1),
             ("ones_col_bf", BF16, 1), ("eps1p_col", F32, 1)]
    if "l0a" in env.blocks:
        names += [("nw_cols", F32, NDC), ("coefc", F32, 128)]
    if "l1a" in env.blocks:
        names += [("bin_cols", F32, NDC), ("bgat_cols", F32, NDC),
                  ("lam_cols", F32, NDC), ("mcol", F32, 8), ("onem", F32, 8)]
    if "l1f" in env.blocks:
        names += [("sg_cols", F32, NOC), ("su_cols", F32, NOC), ("sd_cols", F32, NDC)]
    for (nm, dt_, f) in names:
        tv = env.salloc(dt_, f, nm)
        b.dma("y", lambda E, tv=tv, nm=nm: E.dma_start(out=tv.ap(), in_=env.dr[nm].ap()),
              reads=[DK(nm)], writes=[tv.k()])
        env.c[nm] = tv
    for nm, f, dt_ in (("ones_row", 128, F32), ("nh_row", 128, F32),
                       ("nh_row_bf", 128, BF16), ("ccr", 8, F32)):
        tv = env.row(nm, f, dt_)
        b.dma("y", lambda E, tv=tv, nm=nm: E.dma_start(out=tv._ap, in_=env.dr[nm].ap()),
              reads=[DK(nm)], writes=[tv.k()])
        env.c[nm] = tv


def w_load(b, env, dram_name, col_lo, ncols, dest, nrow_chunks, dcol_off=0,
           n_split=4):
    n_split = min(n_split, nrow_chunks)
    per = nrow_chunks // n_split
    for sp in range(n_split):
        a0, a1 = sp * per, (sp + 1) * per if sp < n_split - 1 else nrow_chunks
        src = env.dr[dram_name].ap()[a0 * 128:a1 * 128,
                                     col_lo:col_lo + ncols].rearrange(
            "(a p) c -> p a c", p=128)
        dst = dest.ap(dcol_off + a0 * ncols,
                      dcol_off + a1 * ncols).rearrange(
            "p (a c) -> p a c", a=a1 - a0)
        b.dma("p", lambda E, src=src, dst=dst: E.dma_start(out=dst, in_=src),
              reads=[DK(dram_name)],
              writes=[dest.k(dcol_off + a0 * ncols, dcol_off + a1 * ncols)])


def newton_rsqrt(b, env, u_tv, lo, hi, out_tv, olo, scr1, scr2, iters=3):
    n = hi - lo
    I32 = mybir.dt.int32
    y = out_tv
    b.op("v", lambda E: E.tensor_scalar(out=y.ap(olo, olo + n).bitcast(I32),
                                        in0=u_tv.ap(lo, hi).bitcast(I32),
                                        scalar1=1, scalar2=None,
                                        op0=ALU.logical_shift_right),
         reads=[u_tv.k(lo, hi)], writes=[y.k(olo, olo + n)])
    b.op("v", lambda E: E.tensor_scalar(out=y.ap(olo, olo + n).bitcast(I32),
                                        in0=y.ap(olo, olo + n).bitcast(I32),
                                        scalar1=0x5f3759df, scalar2=-1,
                                        op0=ALU.subtract, op1=ALU.mult),
         reads=[y.k(olo, olo + n)], writes=[y.k(olo, olo + n)])
    for _ in range(iters):
        b.op("v", lambda E: E.tensor_tensor(out=scr1.ap(0, n), in0=y.ap(olo, olo + n),
                                            in1=y.ap(olo, olo + n), op=ALU.mult),
             reads=[y.k(olo, olo + n)], writes=[scr1.k(0, n)])
        b.op("v", lambda E: E.scalar_tensor_tensor(out=scr2.ap(0, n), in0=scr1.ap(0, n),
                                                   scalar=-0.5, in1=u_tv.ap(lo, hi),
                                                   op0=ALU.mult, op1=ALU.mult),
             reads=[scr1.k(0, n), u_tv.k(lo, hi)], writes=[scr2.k(0, n)])
        b.op("v", lambda E: E.tensor_scalar(out=scr2.ap(0, n), in0=scr2.ap(0, n),
                                            scalar1=1.5, scalar2=None, op0=ALU.add),
             reads=[scr2.k(0, n)], writes=[scr2.k(0, n)])
        b.op("v", lambda E: E.tensor_tensor(out=y.ap(olo, olo + n), in0=y.ap(olo, olo + n),
                                            in1=scr2.ap(0, n), op=ALU.mult),
             reads=[y.k(olo, olo + n), scr2.k(0, n)], writes=[y.k(olo, olo + n)])


def emit_ln(b, env, x_dram, xn, keep_rows=False):
    nc = env.nc
    xk = x_dram.name
    xres = env.tv("big1", 0, F32, NDC * T)
    sq = [env.tv("big5", i * (T * 2), BF16, T) for i in range(2)]
    xbb = [env.tv("big5", 28672 + i * (T * 2), BF16, T) for i in range(2)]
    tmp = env.tv("big5", 4096, F32, T)
    psA, psB = env.psA, env.psB

    for dc in range(NDC):
        b.dma("y" if dc % 2 == 0 else "p", lambda E, dc=dc: E.dma_start(
            out=xres.ap(dc * T, (dc + 1) * T),
            in_=x_dram.ap()[dc * 128:(dc + 1) * 128, :]),
            reads=[DK(xk, dc * 128 * T * 4, (dc + 1) * 128 * T * 4)],
            writes=[xres.k(dc * T, (dc + 1) * T)])
    for dc in range(NDC):
        s = sq[dc % 2]
        xb_ = xbb[dc % 2]
        b.op("s", lambda E, dc=dc, s=s: E.activation(
            out=s.ap(), in_=xres.ap(dc * T, (dc + 1) * T), func=AF.Square),
            reads=[xres.k(dc * T, (dc + 1) * T)], writes=[s.k()])
        b.op("v", lambda E, dc=dc, xb_=xb_: E.tensor_copy(
            out=xb_.ap(), in_=xres.ap(dc * T, (dc + 1) * T)),
            reads=[xres.k(dc * T, (dc + 1) * T)], writes=[xb_.k()])
        for tt in range(2):
            b.op("t", lambda E, dc=dc, tt=tt, xb_=xb_: E.matmul(
                out=psA.ap()[0:1, tt * 512:(tt + 1) * 512],
                lhsT=env.c["ones_col_bf"].ap(),
                rhs=xb_.ap(tt * 512, (tt + 1) * 512),
                start=(dc == 0), stop=(dc == NDC - 1)),
                reads=[xb_.k(tt * 512, (tt + 1) * 512),
                       env.c["ones_col_bf"].k()],
                writes=[PK("psA", tt * 512, (tt + 1) * 512)])
            b.op("t", lambda E, dc=dc, tt=tt, s=s: E.matmul(
                out=psA.ap()[0:1, 1024 + tt * 512:1024 + (tt + 1) * 512],
                lhsT=env.c["ones_col_bf"].ap(),
                rhs=s.ap(tt * 512, (tt + 1) * 512),
                start=(dc == 0), stop=(dc == NDC - 1)),
                reads=[s.k(tt * 512, (tt + 1) * 512), env.c["ones_col_bf"].k()],
                writes=[PK("psA", 1024 + tt * 512, 1024 + (tt + 1) * 512)])
    m_row = env.prow("big5", 8192)
    v_row = env.prow("big5", 12288)
    mm_row = env.prow("big5", 16384)
    rstd_row = env.prow("big5", 20480)
    mr_row = env.prow("big5", 24576)
    b.op("v", lambda E: E.tensor_scalar(out=m_row._ap, in0=psA.ap()[0:1, 0:1024],
                                        scalar1=1.0 / D, scalar2=None, op0=ALU.mult),
         reads=[PK("psA", 0, 1024)], writes=[m_row.k()])
    b.op("v", lambda E: E.tensor_scalar(out=v_row._ap, in0=psA.ap()[0:1, 1024:2048],
                                        scalar1=1.0 / D, scalar2=None, op0=ALU.mult),
         reads=[PK("psA", 1024, 2048)], writes=[v_row.k()])
    b.op("v", lambda E: E.tensor_tensor(out=mm_row._ap, in0=m_row._ap, in1=m_row._ap,
                                        op=ALU.mult),
         reads=[m_row.k()], writes=[mm_row.k()])
    b.op("v", lambda E: E.tensor_tensor(out=v_row._ap, in0=v_row._ap, in1=mm_row._ap,
                                        op=ALU.subtract),
         reads=[v_row.k(), mm_row.k()], writes=[v_row.k()])
    b.op("s", lambda E: E.activation(out=v_row._ap, in_=v_row._ap, func=AF.Ln,
                                     bias=env.c["ccr"]._ap[0:1, 0:1], scale=1.0),
         reads=[v_row.k(), env.c["ccr"].k(0, 1)], writes=[v_row.k()])
    b.op("s", lambda E: E.activation(out=rstd_row._ap, in_=v_row._ap, func=AF.Exp,
                                     scale=-0.5),
         reads=[v_row.k()], writes=[rstd_row.k()])
    b.op("v", lambda E: E.tensor_tensor(out=mr_row._ap, in0=m_row._ap, in1=rstd_row._ap,
                                        op=ALU.mult),
         reads=[m_row.k(), rstd_row.k()], writes=[mr_row.k()])
    for tt in range(2):
        b.op("t", lambda E, tt=tt: E.matmul(
            out=psB.ap()[:, tt * 512:(tt + 1) * 512], lhsT=env.c["ones_row"]._ap,
            rhs=rstd_row._ap[:, tt * 512:(tt + 1) * 512], start=True, stop=True),
            reads=[rstd_row.k(tt * 512, (tt + 1) * 512), env.c["ones_row"].k()],
            writes=[PK("psB", tt * 512, (tt + 1) * 512)])
        b.op("t", lambda E, tt=tt: E.matmul(
            out=psB.ap()[:, 1024 + tt * 512:1024 + (tt + 1) * 512],
            lhsT=env.c["ones_row"]._ap,
            rhs=mr_row._ap[:, tt * 512:(tt + 1) * 512], start=True, stop=True),
            reads=[mr_row.k(tt * 512, (tt + 1) * 512), env.c["ones_row"].k()],
            writes=[PK("psB", 1024 + tt * 512, 1024 + (tt + 1) * 512)])
    for dc in range(NDC):
        b.op("v", lambda E, dc=dc: E.tensor_tensor(
            out=tmp.ap(), in0=xres.ap(dc * T, (dc + 1) * T), in1=psB.ap()[:, 0:1024],
            op=ALU.mult),
            reads=[xres.k(dc * T, (dc + 1) * T), PK("psB", 0, 1024)], writes=[tmp.k()])
        b.op("v", lambda E, dc=dc: E.tensor_tensor(
            out=xn.ap(dc * T, (dc + 1) * T), in0=tmp.ap(), in1=psB.ap()[:, 1024:2048],
            op=ALU.subtract),
            reads=[tmp.k(), PK("psB", 1024, 2048)],
            writes=[xn.k(dc * T, (dc + 1) * T)])
    if keep_rows:
        return rstd_row, mr_row
    return None, None


def emit_l0a(b, env, x_in, x_out):
    psA, psB = env.psA, env.psB
    xn = env.tv("big2", 0, BF16, NDC * T)
    emit_ln(b, env, x_in, xn)

    mk(b, "l0a:ln")
    krot = env.tv("big1", 0, BF16, NDC * T)
    vbf = env.tv("big1", 4 * MB, BF16, NTC * D)
    wblk = [env.tv("big3", i * (NDC * 128 * 2), BF16, NDC * 128) for i in range(2)]
    vwb = env.tv("big3", 1 * MB, BF16, NDC * 512)
    cosf = env.tv("big4", 0, F32, T)
    sinf = env.tv("big4", T * 4, F32, T)
    ks = env.tv("big4", 2 * T * 4, F32, T)
    ksb = env.tv("big4", 22528, BF16, T)
    rs1 = env.tv("big4", 3 * T * 4, F32, T)
    rs2 = env.tv("big4", 4 * T * 4, F32, T)
    qstg = env.tv("big4", 5 * T * 4, BF16, T)
    b.dma("y", lambda E: E.dma_start(out=cosf.ap(), in_=env.dr["cosf"].ap()),
          reads=[DK("cosf")], writes=[cosf.k()])
    b.dma("y", lambda E: E.dma_start(out=sinf.ap(), in_=env.dr["sinf"].ap()),
          reads=[DK("sinf")], writes=[sinf.k()])
    dk = env.tv("big5", 0, F32, D)
    Ttv = env.tv("big5", 1 * MB, F32, D)
    ktmA = env.tv("big5", 2 * MB, BF16, D)
    vdkst = env.tv("big5", 2 * MB + D * 2, BF16, D)
    lamcA = env.tv("big5", 3 * MB, F32, D)
    b.dma("y", lambda E: E.dma_start(out=dk.ap(), in_=env.dr["dk_tab"].ap()),
          reads=[DK("dk_tab")], writes=[dk.k()])
    b.dma("y", lambda E: E.dma_start(out=lamcA.ap(), in_=env.dr["lamc_tab"].ap()),
          reads=[DK("lamc_tab")], writes=[lamcA.k()])

    def proj_rot(wname, out_dram):
        for ec in range(H):
            gamma_pump(b, env, 6)
            wb = wblk[ec % 2]
            w_load(b, env, wname, ec * 128, 128, wb, NDC)
            for tt in range(2):
                for dc in range(NDC):
                    b.op("t", lambda E, ec=ec, tt=tt, dc=dc, wb=wb: E.matmul(
                        out=psA.ap()[:, tt * 512:(tt + 1) * 512],
                        lhsT=wb.ap(dc * 128, (dc + 1) * 128),
                        rhs=xn.ap(dc * T + tt * 512, dc * T + (tt + 1) * 512),
                        start=(dc == 0), stop=(dc == NDC - 1)),
                        reads=[wb.k(dc * 128, (dc + 1) * 128),
                               xn.k(dc * T + tt * 512, dc * T + (tt + 1) * 512)],
                        writes=[PK("psA", tt * 512, (tt + 1) * 512)])
                b.op("s", lambda E, tt=tt: E.activation(
                    out=ks.ap(tt * 512, (tt + 1) * 512),
                    in_=psA.ap()[:, tt * 512:(tt + 1) * 512], func=AF.Silu),
                    reads=[PK("psA", tt * 512, (tt + 1) * 512)],
                    writes=[ks.k(tt * 512, (tt + 1) * 512)])
            b.op("s", lambda E: E.activation(out=ksb.ap(), in_=ks.ap(), func=AF.Copy),
                 reads=[ks.k()], writes=[ksb.k()])
            for tt in range(2):
                b.op("t", lambda E, tt=tt: E.matmul(
                    out=psB.ap()[:, tt * 512:(tt + 1) * 512],
                    lhsT=env.c["permpm_bf"].ap(),
                    rhs=ksb.ap(tt * 512, (tt + 1) * 512), start=True, stop=True),
                    reads=[ksb.k(tt * 512, (tt + 1) * 512), env.c["permpm_bf"].k()],
                    writes=[PK("psB", tt * 512, (tt + 1) * 512)])
            b.op("v", lambda E: E.tensor_tensor(out=rs1.ap(), in0=ks.ap(),
                                                in1=cosf.ap(), op=ALU.mult),
                 reads=[ks.k(), cosf.k()], writes=[rs1.k()])
            b.op("v", lambda E: E.tensor_tensor(out=rs2.ap(), in0=psB.ap()[:, 0:1024],
                                                in1=sinf.ap(), op=ALU.mult),
                 reads=[PK("psB", 0, 1024), sinf.k()], writes=[rs2.k()])
            if out_dram is None:
                dst = krot
                b.op("v", lambda E, ec=ec: E.tensor_tensor(
                    out=dst.ap(ec * T, (ec + 1) * T), in0=rs1.ap(), in1=rs2.ap(),
                    op=ALU.add),
                    reads=[rs1.k(), rs2.k()], writes=[dst.k(ec * T, (ec + 1) * T)])
            else:
                b.op("v", lambda E: E.tensor_tensor(
                    out=qstg.ap(), in0=rs1.ap(), in1=rs2.ap(), op=ALU.add),
                    reads=[rs1.k(), rs2.k()], writes=[qstg.k()])
                b.dma("y", lambda E, ec=ec: E.dma_start(
                    out=env.dr[out_dram].ap()[ec * 128:(ec + 1) * 128, :],
                    in_=qstg.ap()),
                    reads=[qstg.k()],
                    writes=[DK(out_dram, ec * 128 * T * 2, (ec + 1) * 128 * T * 2)])

    proj_rot("wk", None)
    mk(b, "l0a:kproj")
    if L0A_STAGE == 11:
        odbg = env.dout("o_krot", (128, NDC * T), mybir.dt.bfloat16)
        b.dma("y", lambda E: E.dma_start(out=odbg.ap(), in_=krot.ap()),
              reads=[krot.k()], writes=[DK("o_krot")])
        return
    for eb in range(4):
        w_load(b, env, "wv", eb * 512, 512, vwb, NDC)
        for tc in range(NTC):
            gamma_pump(b, env, 2)
            pb = (tc % 2) * 512
            for dc in range(NDC):
                b.op("t", lambda E, eb=eb, tc=tc, dc=dc, pb=pb: E.matmul(
                    out=psA.ap()[:, pb:pb + 512],
                    lhsT=xn.ap(dc * T + tc * 128, dc * T + (tc + 1) * 128),
                    rhs=vwb.ap(dc * 512, (dc + 1) * 512),
                    start=(dc == 0), stop=(dc == NDC - 1)),
                    reads=[xn.k(dc * T + tc * 128, dc * T + (tc + 1) * 128),
                           vwb.k(dc * 512, (dc + 1) * 512)],
                    writes=[PK("psA", pb, pb + 512)])
            if L0A_STAGE == 1152:
                b.op("v", lambda E, eb=eb, tc=tc, pb=pb: E.tensor_copy(
                    out=vbf.ap(tc * D + eb * 512, tc * D + (eb + 1) * 512),
                    in_=psA.ap()[:, pb:pb + 512]),
                    reads=[PK("psA", pb, pb + 512)],
                    writes=[vbf.k(tc * D + eb * 512, tc * D + (eb + 1) * 512)])
            else:
                b.op("s", lambda E, eb=eb, tc=tc, pb=pb: E.activation(
                    out=vbf.ap(tc * D + eb * 512, tc * D + (eb + 1) * 512),
                    in_=psA.ap()[:, pb:pb + 512], func=AF.Copy),
                    reads=[PK("psA", pb, pb + 512)],
                    writes=[vbf.k(tc * D + eb * 512, tc * D + (eb + 1) * 512)])
            if L0A_STAGE not in (1151, 1152):
                b.op("v", lambda E, eb=eb, tc=tc, pb=pb: E.tensor_tensor(
                    out=vdkst.ap(eb * 512, (eb + 1) * 512),
                    in0=psA.ap()[:, pb:pb + 512],
                    in1=dk.ap(eb * 512, (eb + 1) * 512), op=ALU.mult),
                    reads=[PK("psA", pb, pb + 512), dk.k(eb * 512, (eb + 1) * 512)],
                    writes=[vdkst.k(eb * 512, (eb + 1) * 512)])
            if L0A_STAGE not in (1151, 1152, 1153):
                b.dma("y", lambda E, eb=eb, tc=tc: E.dma_start(
                    out=env.dr["vdk_d"].ap()[tc * 128:(tc + 1) * 128, eb * 512:(eb + 1) * 512],
                    in_=vdkst.ap(eb * 512, (eb + 1) * 512)),
                    reads=[vdkst.k(eb * 512, (eb + 1) * 512)],
                    writes=[DK("vdk_d", (tc * D + eb * 512) * 2, (tc * D + (eb + 1) * 512) * 2)])
    if L0A_STAGE in (115, 1151, 1152, 1153):
        odbg = env.dout("o_v", (128, NTC * D), mybir.dt.bfloat16)
        b.dma("y", lambda E: E.dma_start(out=odbg.ap(), in_=vbf.ap()),
              reads=[vbf.k()], writes=[DK("o_v")])
        return
    b.op("v", lambda E: E.memset(Ttv.ap(), 0.0), writes=[Ttv.k()])
    vdkc_bufs = [env.tv("big3", 1 * MB + i * D * 2, BF16, D) for i in range(2)]
    for tc in range(NTC):
        vc = vdkc_bufs[tc % 2]
        b.dma("y", lambda E, tc=tc, vc=vc: E.dma_start(
            out=vc.ap(), in_=env.dr["vdk_d"].ap()[tc * 128:(tc + 1) * 128, :]),
            reads=[DK("vdk_d", tc * 128 * D * 2, (tc + 1) * 128 * D * 2)],
            writes=[vc.k()])
        for h in range(H):
            b.op("t", lambda E, tc=tc, h=h: E.matmul(
                out=psB.ap()[:, h * 128:(h + 1) * 128],
                lhsT=krot.ap(h * T + tc * 128, h * T + (tc + 1) * 128),
                rhs=env.c["idbf"].ap(), start=True, stop=True),
                reads=[krot.k(h * T + tc * 128, h * T + (tc + 1) * 128),
                       env.c["idbf"].k()],
                writes=[PK("psB", h * 128, (h + 1) * 128)])
        b.op("v", lambda E: E.tensor_copy(out=ktmA.ap(), in_=psB.ap()),
             reads=[PK("psB", 0, 2048)], writes=[ktmA.k()])
        if L0A_STAGE != 117:
            for h in range(H):
                b.op("t", lambda E, tc=tc, h=h, vc=vc: E.matmul(
                    out=psA.ap()[:, h * 128:(h + 1) * 128],
                    lhsT=ktmA.ap(h * 128, (h + 1) * 128),
                    rhs=vc.ap(h * 128, (h + 1) * 128), start=True, stop=True),
                    reads=[ktmA.k(h * 128, (h + 1) * 128), vc.k(h * 128, (h + 1) * 128)],
                    writes=[PK("psA", h * 128, (h + 1) * 128)])
            b.op("v", lambda E: E.tensor_tensor(out=Ttv.ap(), in0=Ttv.ap(),
                                                in1=lamcA.ap(), op=ALU.mult),
                 reads=[Ttv.k(), lamcA.k()], writes=[Ttv.k()])
            b.op("v", lambda E: E.tensor_tensor(out=Ttv.ap(), in0=Ttv.ap(),
                                                in1=psA.ap(), op=ALU.add),
                 reads=[Ttv.k(), PK("psA", 0, 2048)], writes=[Ttv.k()])
    if L0A_STAGE == 12:
        odbg = env.dout("o_T", (128, D))
        b.dma("y", lambda E: E.dma_start(out=odbg.ap(), in_=Ttv.ap()),
              reads=[Ttv.k()], writes=[DK("o_T")])
        return
    mk(b, "l0a:v+phA")
    b.dma("p", lambda E: E.dma_start(out=env.dr["ag1_in"].ap(), in_=Ttv.ap()),
          reads=[Ttv.k()], writes=[DK("ag1_in")])
    b.op("p", lambda E: E.collective_compute(
        "AllGather", ALU.bypass, replica_groups=[list(range(NCORE))],
        ins=[env.dr["ag1_in"].ap()], outs=[env.dr["ag1_out"].ap()]),
        reads=[DK("ag1_in")], writes=[DK("ag1_out")])
    if L0A_STAGE == 13:
        odbg = env.dout("o_ag", (NCORE * 128, D))
        b.dma("y", lambda E: E.dma_start(out=odbg.ap(), in_=env.dr["ag1_out"].ap()),
              reads=[DK("ag1_out")], writes=[DK("o_ag")])
        return
    proj_rot("wq", "qrot_d")
    wg1b = env.tv("big3", 8192, BF16, NDC * 128)
    w_load(b, env, "wg1", 0, 128, wg1b, NDC)
    g1 = env.tv("big3", 16384, BF16, T)
    wg2b = env.tv("big3", 18432, BF16, D)
    w_load(b, env, "wg2", 0, D, wg2b, 1)
    for tt in range(2):
        for dc in range(NDC):
            b.op("t", lambda E, tt=tt, dc=dc: E.matmul(
                out=psB.ap()[:, 1024 + tt * 512:1024 + (tt + 1) * 512],
                lhsT=wg1b.ap(dc * 128, (dc + 1) * 128),
                rhs=xn.ap(dc * T + tt * 512, dc * T + (tt + 1) * 512),
                start=(dc == 0), stop=(dc == NDC - 1)),
                reads=[wg1b.k(dc * 128, (dc + 1) * 128),
                       xn.k(dc * T + tt * 512, dc * T + (tt + 1) * 512)],
                writes=[PK("psB", 1024 + tt * 512, 1024 + (tt + 1) * 512)])
    b.op("v", lambda E: E.tensor_copy(out=g1.ap(), in_=psB.ap()[:, 1024:2048]),
         reads=[PK("psB", 1024, 2048)], writes=[g1.k()])
    mk(b, "l0a:q+g1")
    dtt = env.tv("big2", 0, F32, D)
    dqt = env.tv("big2", 1 * MB, F32, D)
    lamc = env.tv("big2", 2 * MB, F32, D)
    S = env.tv("big2", 3 * MB, F32, D)
    tjb = [env.tv("big5", i * MB, F32, D) for i in range(2)]
    b.op("v", lambda E: E.memset(S.ap(), 0.0), writes=[S.k()])
    for j in range(NCORE):
        tj = tjb[j % 2]
        b.dma("y", lambda E, j=j, tj=tj: E.dma_start(
            out=tj.ap(), in_=env.dr["ag1_out"].ap()[j * 128:(j + 1) * 128, :]),
            reads=[DK("ag1_out")], writes=[tj.k()])
        for h in range(H):
            b.op("v", lambda E, j=j, h=h, tj=tj: E.scalar_tensor_tensor(
                out=S.ap(h * 128, (h + 1) * 128), in0=tj.ap(h * 128, (h + 1) * 128),
                scalar=env.c["coefc"].ap(j * 16 + h, j * 16 + h + 1),
                in1=S.ap(h * 128, (h + 1) * 128), op0=ALU.mult, op1=ALU.add),
                reads=[tj.k(h * 128, (h + 1) * 128), S.k(h * 128, (h + 1) * 128),
                       env.c["coefc"].k(j * 16 + h, j * 16 + h + 1)],
                writes=[S.k(h * 128, (h + 1) * 128)])
    b.dma("y", lambda E: E.dma_start(out=dtt.ap(), in_=env.dr["dt_tab"].ap()),
          reads=[DK("dt_tab")], writes=[dtt.k()])
    b.dma("y", lambda E: E.dma_start(out=dqt.ap(), in_=env.dr["dq_tab"].ap()),
          reads=[DK("dq_tab")], writes=[dqt.k()])
    b.dma("y", lambda E: E.dma_start(out=lamc.ap(), in_=env.dr["lamc_tab"].ap()),
          reads=[DK("lamc_tab")], writes=[lamc.k()])
    qcb = [env.tv("big3", i * D * 2, BF16, D) for i in range(2)]
    vdc = [env.tv("big3", 1 * MB + i * D * 2, BF16, D) for i in range(2)]

    def sweep_loads(tc):
        qc, vc = qcb[tc % 2], vdc[tc % 2]
        b.dma("y", lambda E, tc=tc, qc=qc: E.dma_start(
            out=qc.ap().rearrange("p (a c) -> p a c", a=H),
            in_=env.dr["qrot_d"].ap()[:, tc * 128:(tc + 1) * 128].rearrange(
                "(a p) c -> p a c", p=128)),
            reads=[DK("qrot_d")], writes=[qc.k()])
        b.dma("y", lambda E, tc=tc, vc=vc: E.dma_start(
            out=vc.ap(), in_=env.dr["vdk_d"].ap()[tc * 128:(tc + 1) * 128, :]),
            reads=[DK("vdk_d", tc * 128 * D * 2, (tc + 1) * 128 * D * 2)],
            writes=[vc.k()])

    sweep_loads(0)
    sweep_loads(1)
    if L0A_STAGE == 1:
        for ec in range(NDC):
            xt_ = env.tv("big5", (ec % 2) * 4096, F32, T)
            b.dma("y", lambda E, ec=ec, xt_=xt_: E.dma_start(
                out=xt_.ap(), in_=x_in.ap()[ec * 128:(ec + 1) * 128, :]),
                reads=[DK(x_in.name, ec * 128 * T * 4, (ec + 1) * 128 * T * 4)],
                writes=[xt_.k()])
            b.op("v", lambda E, ec=ec, xt_=xt_: E.tensor_tensor(
                out=xt_.ap(), in0=xt_.ap(), in1=S.ap(0, 1024) if False else xt_.ap(),
                op=ALU.add), reads=[xt_.k()], writes=[xt_.k()])
            b.dma("y", lambda E, ec=ec, xt_=xt_: E.dma_start(
                out=x_out.ap()[ec * 128:(ec + 1) * 128, :], in_=xt_.ap()),
                reads=[xt_.k()],
                writes=[DK(x_out.name, ec * 128 * T * 4, (ec + 1) * 128 * T * 4)])
        odbg = env.dout("o_S", (128, D))
        b.dma("y", lambda E: E.dma_start(out=odbg.ap(), in_=S.ap()),
              reads=[S.k()], writes=[DK("o_S")])
        return

    mk(b, "l0a:prefix")
    ktm = env.tv("big5", 0, BF16, D)
    sTD = env.tv("big5", D * 2, BF16, D)
    qdq = env.tv("big5", 2 * D * 2, BF16, D)
    Sbf = env.tv("big5", 3 * D * 2, BF16, D)
    sg = env.tv("big5", 2 * MB, F32, D)
    osg = env.tv("big5", 24576, F32, D)
    xgb = env.tv("big5", 32768, BF16, D)
    xgT = env.tv("big4", 0, BF16, NDC * T)
    msv = env.salloc(F32, 16, "l0a_ms")
    rst = env.salloc(F32, 16, "l0a_rstd")
    nsc1 = env.salloc(F32, 16, "l0a_n1")
    nsc2 = env.salloc(F32, 16, "l0a_n2")
    b.op("v", lambda E: E.tensor_copy(out=Sbf.ap(), in_=S.ap()),
         reads=[S.k()], writes=[Sbf.k()])
    def sweep_gate(tc):
        for eb in range(4):
            b.op("t", lambda E, tc=tc, eb=eb: E.matmul(
                out=psB.ap()[:, eb * 512:(eb + 1) * 512],
                lhsT=g1.ap(tc * 128, (tc + 1) * 128),
                rhs=wg2b.ap(eb * 512, (eb + 1) * 512), start=True, stop=True),
                reads=[g1.k(tc * 128, (tc + 1) * 128), wg2b.k(eb * 512, (eb + 1) * 512)],
                writes=[PK("psB", eb * 512, (eb + 1) * 512)])
        b.op("s", lambda E: E.activation(out=sg.ap(), in_=psB.ap(), func=AF.Sigmoid),
             reads=[PK("psB", 0, 2048)], writes=[sg.k()])

    sweep_gate(0)
    for tc in range(NTC):
        qc = qcb[tc % 2]
        vc = vdc[tc % 2]
        for h in range(H):
            b.op("t", lambda E, tc=tc, h=h, qc=qc: E.matmul(
                out=psA.ap()[:, h * 128:(h + 1) * 128],
                lhsT=krot.ap(h * T + tc * 128, h * T + (tc + 1) * 128),
                rhs=qc.ap(h * 128, (h + 1) * 128), start=True, stop=True),
                reads=[krot.k(h * T + tc * 128, h * T + (tc + 1) * 128),
                       qc.k(h * 128, (h + 1) * 128)],
                writes=[PK("psA", h * 128, (h + 1) * 128)])
        for h in range(H):
            b.op("v", lambda E, h=h: E.tensor_tensor(
                out=sTD.ap(h * 128, (h + 1) * 128),
                in0=psA.ap()[:, h * 128:(h + 1) * 128],
                in1=dtt.ap(h * 128, (h + 1) * 128), op=ALU.mult),
                reads=[PK("psA", h * 128, (h + 1) * 128), dtt.k(h * 128, (h + 1) * 128)],
                writes=[sTD.k(h * 128, (h + 1) * 128)])
        for h in range(H):
            b.op("t", lambda E, tc=tc, h=h: E.matmul(
                out=psB.ap()[:, h * 128:(h + 1) * 128],
                lhsT=krot.ap(h * T + tc * 128, h * T + (tc + 1) * 128),
                rhs=env.c["idbf"].ap(), start=True, stop=True),
                reads=[krot.k(h * T + tc * 128, h * T + (tc + 1) * 128),
                       env.c["idbf"].k()],
                writes=[PK("psB", h * 128, (h + 1) * 128)])
        b.op("v", lambda E: E.tensor_copy(out=ktm.ap(), in_=psB.ap()),
             reads=[PK("psB", 0, 2048)], writes=[ktm.k()])
        for h in range(H):
            b.op("t", lambda E, h=h, vc=vc: E.matmul(
                out=psA.ap()[:, h * 128:(h + 1) * 128],
                lhsT=ktm.ap(h * 128, (h + 1) * 128),
                rhs=vc.ap(h * 128, (h + 1) * 128), start=True, stop=True),
                reads=[ktm.k(h * 128, (h + 1) * 128), vc.k(h * 128, (h + 1) * 128)],
                writes=[PK("psA", h * 128, (h + 1) * 128)])
        for h in range(H):
            b.op("v", lambda E, h=h, qc=qc: E.tensor_tensor(
                out=qdq.ap(h * 128, (h + 1) * 128),
                in0=qc.ap(h * 128, (h + 1) * 128),
                in1=dqt.ap(h * 128, (h + 1) * 128), op=ALU.mult),
                reads=[qc.k(h * 128, (h + 1) * 128), dqt.k(h * 128, (h + 1) * 128)],
                writes=[qdq.k(h * 128, (h + 1) * 128)])
        for h in range(H):
            b.op("t", lambda E, tc=tc, h=h: E.matmul(
                out=psB.ap()[:, h * 128:(h + 1) * 128],
                lhsT=sTD.ap(h * 128, (h + 1) * 128),
                rhs=vbf.ap(tc * D + h * 128, tc * D + (h + 1) * 128),
                start=True, stop=False),
                reads=[sTD.k(h * 128, (h + 1) * 128),
                       vbf.k(tc * D + h * 128, tc * D + (h + 1) * 128)],
                writes=[PK("psB", h * 128, (h + 1) * 128)])
            b.op("t", lambda E, h=h: E.matmul(
                out=psB.ap()[:, h * 128:(h + 1) * 128],
                lhsT=qdq.ap(h * 128, (h + 1) * 128),
                rhs=Sbf.ap(h * 128, (h + 1) * 128), start=False, stop=True),
                reads=[qdq.k(h * 128, (h + 1) * 128), Sbf.k(h * 128, (h + 1) * 128)],
                writes=[PK("psB", h * 128, (h + 1) * 128)])
        b.op("v", lambda E: E.tensor_tensor(out=S.ap(), in0=S.ap(), in1=lamc.ap(),
                                            op=ALU.mult),
             reads=[S.k(), lamc.k()], writes=[S.k()])
        b.op("v", lambda E: E.tensor_tensor(out=S.ap(), in0=S.ap(), in1=psA.ap(),
                                            op=ALU.add),
             reads=[S.k(), PK("psA", 0, 2048)], writes=[S.k()])
        b.op("v", lambda E: E.tensor_copy(out=Sbf.ap(), in_=S.ap()),
             reads=[S.k()], writes=[Sbf.k()])
        b.op("v", lambda E: E.tensor_tensor(out=osg.ap(), in0=psB.ap(), in1=sg.ap(),
                                            op=ALU.mult),
             reads=[PK("psB", 0, 2048), sg.k()], writes=[osg.k()])
        sqv = env.tv("big5", 2 * MB, F32, D)
        b.op("s", lambda E: E.activation(out=sqv.ap(), in_=osg.ap(), func=AF.Square),
             reads=[osg.k()], writes=[sqv.k()])
        b.op("v", lambda E: E.tensor_reduce(
            out=msv.ap(), in_=sqv.ap().rearrange("p (a c) -> p a c", a=H),
            axis=AX.X, op=ALU.add),
            reads=[sqv.k()], writes=[msv.k()])
        b.op("v", lambda E: E.tensor_scalar(out=msv.ap(), in0=msv.ap(),
                                            scalar1=1.0 / HD, scalar2=1e-6,
                                            op0=ALU.mult, op1=ALU.add),
             reads=[msv.k()], writes=[msv.k()])
        newton_rsqrt(b, env, msv, 0, 16, rst, 0, nsc1, nsc2)
        for h in range(H):
            b.op("v", lambda E, h=h: E.tensor_scalar(
                out=xgb.ap(h * 128, (h + 1) * 128),
                in0=osg.ap(h * 128, (h + 1) * 128),
                scalar1=rst.ap(h, h + 1), scalar2=None, op0=ALU.mult),
                reads=[osg.k(h * 128, (h + 1) * 128), rst.k(h, h + 1)],
                writes=[xgb.k(h * 128, (h + 1) * 128)])
        if tc + 2 < NTC:
            sweep_loads(tc + 2)
        if tc + 1 < NTC:
            sweep_gate(tc + 1)
        for ec in range(NDC):
            b.op("t", lambda E, ec=ec: E.matmul(
                out=psA.ap()[:, ec * 128:(ec + 1) * 128],
                lhsT=xgb.ap(ec * 128, (ec + 1) * 128),
                rhs=env.c["idbf"].ap(), start=True, stop=True),
                reads=[xgb.k(ec * 128, (ec + 1) * 128), env.c["idbf"].k()],
                writes=[PK("psA", ec * 128, (ec + 1) * 128)])
        for ec in range(NDC):
            b.op("v", lambda E, ec=ec, tc=tc: E.tensor_scalar(
                out=xgT.ap(ec * T + tc * 128, ec * T + (tc + 1) * 128),
                in0=psA.ap()[:, ec * 128:(ec + 1) * 128],
                scalar1=env.c["nw_cols"].ap(ec, ec + 1), scalar2=None, op0=ALU.mult),
                reads=[PK("psA", ec * 128, (ec + 1) * 128), env.c["nw_cols"].k(ec, ec + 1)],
                writes=[xgT.k(ec * T + tc * 128, ec * T + (tc + 1) * 128)])
    mk(b, "l0a:sweep")
    if L0A_STAGE == 2:
        odbg = env.dout("o_xgT", (128, NDC * T), mybir.dt.bfloat16)
        b.dma("y", lambda E: E.dma_start(out=odbg.ap(), in_=xgT.ap()),
              reads=[xgT.k()], writes=[DK("o_xgT")])
        odbg2 = env.dout("o_S", (128, D))
        b.dma("y", lambda E: E.dma_start(out=odbg2.ap(), in_=S.ap()),
              reads=[S.k()], writes=[DK("o_S")])
        for ec in range(NDC):
            xt_ = env.tv("big5", (ec % 2) * 4096, F32, T)
            b.dma("y", lambda E, ec=ec, xt_=xt_: E.dma_start(
                out=xt_.ap(), in_=x_in.ap()[ec * 128:(ec + 1) * 128, :]),
                reads=[DK(x_in.name, ec * 128 * T * 4, (ec + 1) * 128 * T * 4)],
                writes=[xt_.k()])
            b.dma("y", lambda E, ec=ec, xt_=xt_: E.dma_start(
                out=x_out.ap()[ec * 128:(ec + 1) * 128, :], in_=xt_.ap()),
                reads=[xt_.k()],
                writes=[DK(x_out.name, ec * 128 * T * 4, (ec + 1) * 128 * T * 4)])
        return

    xt = [env.tv("big5", i * (T * 4), F32, T) for i in range(2)]
    ot = [env.tv("big5", 2 * MB + i * 512 * 4, F32, 512) for i in range(4)]
    for ecp in range(NDC):
        gamma_pump(b, env, 4)
        wb = wblk[ecp % 2]
        w_load(b, env, "wo", ecp * 128, 128, wb, NDC)
        x_t = xt[ecp % 2]
        b.dma("y", lambda E, ecp=ecp, x_t=x_t: E.dma_start(
            out=x_t.ap(), in_=x_in.ap()[ecp * 128:(ecp + 1) * 128, :]),
            reads=[DK(x_in.name, ecp * 128 * T * 4, (ecp + 1) * 128 * T * 4)],
            writes=[x_t.k()])
        for tt in range(2):
            pcol = ((2 * ecp + tt) % 4) * 512
            for ein in range(NDC):
                b.op("t", lambda E, ecp=ecp, tt=tt, ein=ein, wb=wb, pcol=pcol: E.matmul(
                    out=psB.ap()[:, pcol:pcol + 512],
                    lhsT=wb.ap(ein * 128, (ein + 1) * 128),
                    rhs=xgT.ap(ein * T + tt * 512, ein * T + (tt + 1) * 512),
                    start=(ein == 0), stop=(ein == NDC - 1)),
                    reads=[wb.k(ein * 128, (ein + 1) * 128),
                           xgT.k(ein * T + tt * 512, ein * T + (tt + 1) * 512)],
                    writes=[PK("psB", pcol, pcol + 512)])
            o_t = ot[(2 * ecp + tt) % 4]
            b.op("v", lambda E, tt=tt, x_t=x_t, o_t=o_t, pcol=pcol: E.tensor_tensor(
                out=o_t.ap(), in0=x_t.ap(tt * 512, (tt + 1) * 512),
                in1=psB.ap()[:, pcol:pcol + 512], op=ALU.add),
                reads=[x_t.k(tt * 512, (tt + 1) * 512), PK("psB", pcol, pcol + 512)],
                writes=[o_t.k()])
            b.dma("y", lambda E, ecp=ecp, tt=tt, o_t=o_t: E.dma_start(
                out=x_out.ap()[ecp * 128:(ecp + 1) * 128, tt * 512:(tt + 1) * 512],
                in_=o_t.ap()),
                reads=[o_t.k()],
                writes=[DK(x_out.name, ecp * 128 * T * 4, (ecp + 1) * 128 * T * 4)])


def emit_gamma_stream(b, env, mat, thname):
    R, Cc = env.dr[mat].shape
    gacc = env.salloc(F32, 1, "gacc_" + thname)
    rcol = env.salloc(F32, 1, "grc_" + thname)
    wtl = [env.tv("big3", 24576 + i * 4096, F32, 1024) for i in range(2)]
    b.op("v", lambda E, gacc=gacc: E.memset(gacc.ap(), 0.0), writes=[gacc.k()])
    cbs = [(c, min(c + 1024, Cc)) for c in range(0, Cc, 1024)]
    idx = [0]

    def tile(rc, c0, c1):
        def emitfn():
            wt = wtl[idx[0] % 2]
            idx[0] += 1
            n = c1 - c0
            b.dma("s", lambda E, wt=wt: E.dma_start(
                out=wt.ap(0, n), in_=env.dr[mat].ap()[rc * 128:(rc + 1) * 128, c0:c1]),
                reads=[DK(mat, rc * 128 * Cc * 4, (rc + 1) * 128 * Cc * 4)],
                writes=[wt.k(0, n)])
            b.op("v", lambda E, wt=wt: E.tensor_reduce(
                out=rcol.ap(), in_=wt.ap(0, n), axis=AX.X, op=ALU.add,
                apply_absolute_value=True),
                reads=[wt.k(0, n)], writes=[rcol.k()])
            b.op("v", lambda E: E.tensor_tensor(out=gacc.ap(), in0=gacc.ap(),
                                                in1=rcol.ap(), op=ALU.add),
                 reads=[gacc.k(), rcol.k()], writes=[gacc.k()])
        return emitfn

    for rc in range(R // 128):
        for (c0, c1) in cbs:
            env.gpump.append(tile(rc, c0, c1))
    env.gsize[thname] = (gacc, R * Cc)


def gamma_pump(b, env, n):
    for _ in range(n):
        if not env.gpump:
            return
        env.gpump.popleft()()


def emit_gamma_fin(b, env, thname):
    psA = env.psA
    gacc, N = env.gsize[thname]
    thcol = env.salloc(F32, 1, thname)
    throw = env.row("throw", 8)
    b.op("t", lambda E: E.matmul(out=psA.ap()[0:1, 0:1], lhsT=gacc.ap(),
                                 rhs=env.c["ones_col"].ap(), start=True, stop=True),
         reads=[gacc.k(), env.c["ones_col"].k()], writes=[PK("psA", 0, 1)])
    b.op("s", lambda E, N=N: E.activation(
        out=throw._ap[:, 0:1], in_=psA.ap()[0:1, 0:1], func=AF.Identity,
        scale=0.5 / N, bias=env.c["ccr"]._ap[0:1, 1:2]),
        reads=[PK("psA", 0, 1), env.c["ccr"].k(1, 2)], writes=[throw.k(0, 1)])
    b.op("t", lambda E: E.matmul(out=psA.ap()[:, 0:1], lhsT=env.c["ones_row"]._ap,
                                 rhs=throw._ap[:, 0:1], start=True, stop=True),
         reads=[throw.k(0, 1), env.c["ones_row"].k()], writes=[PK("psA", 0, 1)])
    b.op("v", lambda E: E.tensor_copy(out=thcol.ap(), in_=psA.ap()[:, 0:1]),
         reads=[PK("psA", 0, 1)], writes=[thcol.k()])
    return thcol


def quantize_tile(b, env, win, lo, hi, thcol, Ab, Bb, qw, qlo):
    n = hi - lo
    b.op("v", lambda E: E.tensor_scalar(out=Ab.ap(0, n), in0=win.ap(lo, hi),
                                        scalar1=thcol.ap(), scalar2=None, op0=ALU.is_gt),
         reads=[win.k(lo, hi), thcol.k()], writes=[Ab.k(0, n)])
    b.op("s", lambda E: E.activation(out=Bb.ap(0, n), in_=win.ap(lo, hi), func=AF.Sign,
                                     bias=thcol.ap()),
         reads=[win.k(lo, hi), thcol.k()], writes=[Bb.k(0, n)])
    b.op("v", lambda E: E.scalar_tensor_tensor(out=qw.ap(qlo, qlo + n), in0=Bb.ap(0, n),
                                               scalar=0.5, in1=Ab.ap(0, n),
                                               op0=ALU.mult, op1=ALU.add),
         reads=[Bb.k(0, n), Ab.k(0, n)], writes=[qw.k(qlo, qlo + n)])


def emit_l0f(b, env, x_in, x_out):
    psA, psB = env.psA, env.psB
    xn = env.tv("big2", 0, BF16, NDC * T)
    emit_ln(b, env, x_in, xn)
    rs_row = env.prow("big5", 30720)
    for tt in range(2):
        for dc in range(NDC):
            b.op("t", lambda E, tt=tt, dc=dc: E.matmul(
                out=psA.ap()[0:1, tt * 512:(tt + 1) * 512],
                lhsT=env.c["ones_col_bf"].ap(),
                rhs=xn.ap(dc * T + tt * 512, dc * T + (tt + 1) * 512),
                start=(dc == 0), stop=(dc == NDC - 1)),
                reads=[xn.k(dc * T + tt * 512, dc * T + (tt + 1) * 512),
                       env.c["ones_col_bf"].k()],
                writes=[PK("psA", tt * 512, (tt + 1) * 512)])
    b.op("v", lambda E: E.tensor_copy(out=rs_row._ap, in_=psA.ap()[0:1, 0:1024]),
         reads=[PK("psA", 0, 1024)], writes=[rs_row.k()])
    rs_hi = env.prow("big5", 34816, T, BF16)
    rs_lo = env.prow("big5", 36864, T, BF16)
    b.op("v", lambda E: E.tensor_copy(out=rs_hi._ap, in_=rs_row._ap),
         reads=[rs_row.k()], writes=[rs_hi.k()])
    b.op("v", lambda E: E.scalar_tensor_tensor(out=rs_lo._ap, in0=rs_hi._ap,
                                               scalar=-1.0, in1=rs_row._ap,
                                               op0=ALU.mult, op1=ALU.add),
         reads=[rs_hi.k(), rs_row.k()], writes=[rs_lo.k()])
    mk(b, "l0f:ln+rs")
    while env.gpump:
        gamma_pump(b, env, 8)
    th1 = emit_gamma_fin(b, env, "th1")
    th2 = emit_gamma_fin(b, env, "th2")
    w1b = [env.tv("big3", 0, F32, D), env.tv("big3", 16384, F32, D)]
    w2b = [env.tv("big3", 8192, F32, D), env.tv("big3", 24576, F32, D)]
    Ab = env.tv("big5", 0, BF16, D)
    Bb = env.tv("big5", 4096, BF16, D)
    Ab2 = env.tv("big4", 24576, BF16, D)
    Bb2 = env.tv("big4", 28672, BF16, D)
    qw1b = [env.tv("big5", 8192 + i * 4096, BF16, D) for i in range(2)]
    qw2b = [env.tv("big5", 16384 + i * 4096, BF16, D) for i in range(2)]
    sig = [env.tv("big5", 24576 + i * 2048, F32, 512) for i in range(2)]
    hidt = [env.tv("big1", hc * 2048, BF16, T) if hc < 32
            else env.tv("big4", (hc - 32) * 2048, BF16, T) for hc in range(NHC)]

    def st1_loads(hc):
        wa, wb2 = w1b[hc % 2], w2b[hc % 2]
        for hf in range(2):
            b.dma("s", lambda E, hc=hc, hf=hf, wa=wa: E.dma_start(
                out=wa.ap(hf * 1024, (hf + 1) * 1024).rearrange("p (a c) -> p a c", a=8),
                in_=env.dr["w1t"].ap()[hf * 1024:(hf + 1) * 1024,
                                       hc * 128:(hc + 1) * 128].rearrange(
                    "(a p) c -> p a c", p=128)),
                reads=[DK("w1t")], writes=[wa.k(hf * 1024, (hf + 1) * 1024)])
            b.dma("y", lambda E, hc=hc, hf=hf, wb2=wb2: E.dma_start(
                out=wb2.ap(hf * 1024, (hf + 1) * 1024).rearrange("p (a c) -> p a c", a=8),
                in_=env.dr["w2t"].ap()[hf * 1024:(hf + 1) * 1024,
                                       hc * 128:(hc + 1) * 128].rearrange(
                    "(a p) c -> p a c", p=128)),
                reads=[DK("w2t")], writes=[wb2.k(hf * 1024, (hf + 1) * 1024)])

    st1_loads(0)
    st1_loads(1)
    quantize_tile(b, env, w1b[0], 0, D, th1, Ab, Bb, qw1b[0], 0)
    quantize_tile(b, env, w2b[0], 0, D, th2, Ab2, Bb2, qw2b[0], 0)
    for hc in range(NHC):
        qw1 = qw1b[hc % 2]
        qw2 = qw2b[hc % 2]
        if hc + 1 < NHC:
            quantize_tile(b, env, w1b[(hc + 1) % 2], 0, D, th1, Ab, Bb,
                          qw1b[(hc + 1) % 2], 0)
            quantize_tile(b, env, w2b[(hc + 1) % 2], 0, D, th2, Ab2, Bb2,
                          qw2b[(hc + 1) % 2], 0)
        if hc + 2 < NHC:
            st1_loads(hc + 2)
        for tt in range(2):
            for dc in range(NDC):
                b.op("t", lambda E, tt=tt, dc=dc, qw1=qw1: E.matmul(
                    out=psA.ap()[:, tt * 512:(tt + 1) * 512],
                    lhsT=qw1.ap(dc * 128, (dc + 1) * 128),
                    rhs=xn.ap(dc * T + tt * 512, dc * T + (tt + 1) * 512),
                    start=(dc == 0), stop=False),
                    reads=[qw1.k(dc * 128, (dc + 1) * 128),
                           xn.k(dc * T + tt * 512, dc * T + (tt + 1) * 512)],
                    writes=[PK("psA", tt * 512, (tt + 1) * 512)])
            for rr in (rs_hi, rs_lo):
                b.op("t", lambda E, tt=tt, rr=rr, st=(rr is rs_lo): E.matmul(
                    out=psA.ap()[:, tt * 512:(tt + 1) * 512], lhsT=env.c["nh_row_bf"]._ap,
                    rhs=rr._ap[:, tt * 512:(tt + 1) * 512], start=False, stop=st),
                    reads=[rr.k(tt * 512, (tt + 1) * 512), env.c["nh_row_bf"].k()],
                    writes=[PK("psA", tt * 512, (tt + 1) * 512)])
            b.op("s", lambda E, tt=tt: E.activation(
                out=sig[tt].ap(), in_=psA.ap()[:, tt * 512:(tt + 1) * 512],
                func=AF.Sigmoid),
                reads=[PK("psA", tt * 512, (tt + 1) * 512)], writes=[sig[tt].k()])
            for dc in range(NDC):
                b.op("t", lambda E, tt=tt, dc=dc, qw2=qw2: E.matmul(
                    out=psB.ap()[:, tt * 512:(tt + 1) * 512],
                    lhsT=qw2.ap(dc * 128, (dc + 1) * 128),
                    rhs=xn.ap(dc * T + tt * 512, dc * T + (tt + 1) * 512),
                    start=(dc == 0), stop=False),
                    reads=[qw2.k(dc * 128, (dc + 1) * 128),
                           xn.k(dc * T + tt * 512, dc * T + (tt + 1) * 512)],
                    writes=[PK("psB", tt * 512, (tt + 1) * 512)])
            for rr in (rs_hi, rs_lo):
                b.op("t", lambda E, tt=tt, rr=rr, st=(rr is rs_lo): E.matmul(
                    out=psB.ap()[:, tt * 512:(tt + 1) * 512], lhsT=env.c["nh_row_bf"]._ap,
                    rhs=rr._ap[:, tt * 512:(tt + 1) * 512], start=False, stop=st),
                    reads=[rr.k(tt * 512, (tt + 1) * 512), env.c["nh_row_bf"].k()],
                    writes=[PK("psB", tt * 512, (tt + 1) * 512)])
            b.op("v", lambda E, hc=hc, tt=tt: E.tensor_tensor(
                out=hidt[hc].ap(tt * 512, (tt + 1) * 512),
                in0=psB.ap()[:, tt * 512:(tt + 1) * 512],
                in1=sig[tt].ap(), op=ALU.mult),
                reads=[PK("psB", tt * 512, (tt + 1) * 512), sig[tt].k()],
                writes=[hidt[hc].k(tt * 512, (tt + 1) * 512)])
            b.op("t", lambda E, hc=hc, tt=tt: E.matmul(
                out=psB.ap()[0:1, 1024 + tt * 512:1024 + (tt + 1) * 512],
                lhsT=env.c["ones_col_bf"].ap(),
                rhs=hidt[hc].ap(tt * 512, (tt + 1) * 512),
                start=(hc == 0), stop=(hc == NHC - 1)),
                reads=[hidt[hc].k(tt * 512, (tt + 1) * 512),
                       env.c["ones_col_bf"].k()],
                writes=[PK("psB", 1024 + tt * 512, 1024 + (tt + 1) * 512)])
    rsh_row = env.prow("big5", 24576)
    b.op("v", lambda E: E.tensor_copy(out=rsh_row._ap, in_=psB.ap()[0:1, 1024:2048]),
         reads=[PK("psB", 1024, 2048)], writes=[rsh_row.k()])
    rsh_hi = env.prow("big5", 28672, T, BF16)
    rsh_lo = env.prow("big5", 30720, T, BF16)
    b.op("v", lambda E: E.tensor_copy(out=rsh_hi._ap, in_=rsh_row._ap),
         reads=[rsh_row.k()], writes=[rsh_hi.k()])
    b.op("v", lambda E: E.scalar_tensor_tensor(out=rsh_lo._ap, in0=rsh_hi._ap,
                                               scalar=-1.0, in1=rsh_row._ap,
                                               op0=ALU.mult, op1=ALU.add),
         reads=[rsh_hi.k(), rsh_row.k()], writes=[rsh_lo.k()])
    mk(b, "l0f:stage1")
    mk(b, "l0f:stage2")
    th3 = emit_gamma_fin(b, env, "th3")
    w3i = [env.tv("big3", i * 2048, F32, 512) for i in range(4)]
    x1t = [env.tv("big3", 8192 + i * 2048, F32, 512) for i in range(8)]
    A3b = [env.tv("big5", 12288 + i * 1024, BF16, 512) for i in range(2)]
    B3b = [env.tv("big5", 14336 + i * 1024, BF16, 512) for i in range(2)]
    qw3b = [env.tv("big5", 2048 + i * 1024, BF16, 512) for i in range(2)]
    ostg = [env.tv("big5", 4096 + i * 2048, F32, 512) for i in range(4)]
    NIT = 4 * NHC

    def st2_load(it):
        grp, hc = it // NHC, it % NHC
        wbuf = w3i[it % 4]
        b.dma("s", lambda E, grp=grp, hc=hc, wbuf=wbuf: E.dma_start(
            out=wbuf.ap(),
            in_=env.dr["w3t"].ap()[hc * 128:(hc + 1) * 128, grp * 512:(grp + 1) * 512]),
            reads=[DK("w3t", hc * 128 * D * 4, (hc + 1) * 128 * D * 4)],
            writes=[wbuf.k()])

    def st2_quant(it):
        quantize_tile(b, env, w3i[it % 4], 0, 512, th3, A3b[it % 2], B3b[it % 2],
                      qw3b[it % 2], 0)

    for it in range(3):
        st2_load(it)
    st2_quant(0)
    for grp in range(4):
        for dcg in range(4):
            for tt in range(2):
                dco = grp * 4 + dcg
                xb_ = x1t[2 * dcg + tt]
                b.dma("y", lambda E, dco=dco, tt=tt, xb_=xb_: E.dma_start(
                    out=xb_.ap(),
                    in_=x_in.ap()[dco * 128:(dco + 1) * 128, tt * 512:(tt + 1) * 512]),
                    reads=[DK(x_in.name, dco * 128 * T * 4, (dco + 1) * 128 * T * 4)],
                    writes=[xb_.k()])
        for hc in range(NHC):
            it = grp * NHC + hc
            qw3 = qw3b[it % 2]
            if it + 1 < NIT:
                st2_quant(it + 1)
            if it + 3 < NIT:
                st2_load(it + 3)
            for dcg in range(4):
                ps = psA if dcg < 2 else psB
                pn = "psA" if dcg < 2 else "psB"
                for tt in range(2):
                    pcol = ((dcg % 2) * 2 + tt) * 512
                    b.op("t", lambda E, dcg=dcg, tt=tt, ps=ps, pcol=pcol, hc=hc, qw3=qw3: E.matmul(
                        out=ps.ap()[:, pcol:pcol + 512],
                        lhsT=qw3.ap(dcg * 128, (dcg + 1) * 128),
                        rhs=hidt[hc].ap(tt * 512, (tt + 1) * 512),
                        start=(hc == 0), stop=False),
                        reads=[qw3.k(dcg * 128, (dcg + 1) * 128),
                               hidt[hc].k(tt * 512, (tt + 1) * 512)],
                        writes=[PK(pn, pcol, pcol + 512)])
        for dcg in range(4):
            ps = psA if dcg < 2 else psB
            pn = "psA" if dcg < 2 else "psB"
            for tt in range(2):
                pcol = ((dcg % 2) * 2 + tt) * 512
                for rr in (rsh_hi, rsh_lo):
                    b.op("t", lambda E, dcg=dcg, tt=tt, ps=ps, pcol=pcol, rr=rr,
                         st=(rr is rsh_lo): E.matmul(
                        out=ps.ap()[:, pcol:pcol + 512], lhsT=env.c["nh_row_bf"]._ap,
                        rhs=rr._ap[:, tt * 512:(tt + 1) * 512], start=False, stop=st),
                        reads=[rr.k(tt * 512, (tt + 1) * 512), env.c["nh_row_bf"].k()],
                        writes=[PK(pn, pcol, pcol + 512)])
                dco = grp * 4 + dcg
                xb_ = x1t[2 * dcg + tt]
                ob_ = ostg[(2 * dcg + tt) % 4]
                b.op("v", lambda E, ps=ps, pcol=pcol, xb_=xb_, ob_=ob_: E.tensor_tensor(
                    out=ob_.ap(), in0=xb_.ap(), in1=ps.ap()[:, pcol:pcol + 512],
                    op=ALU.add),
                    reads=[xb_.k(), PK(pn, pcol, pcol + 512)], writes=[ob_.k()])
                b.dma("y", lambda E, dco=dco, tt=tt, ob_=ob_: E.dma_start(
                    out=x_out.ap()[dco * 128:(dco + 1) * 128, tt * 512:(tt + 1) * 512],
                    in_=ob_.ap()),
                    reads=[ob_.k()],
                    writes=[DK(x_out.name, dco * 128 * T * 4, (dco + 1) * 128 * T * 4)])


def emit_l1a(b, env, x_in, x_out):
    mk(b, "l1a:start")
    psA, psB = env.psA, env.psB
    xn = env.tv("big2", 0, BF16, NDC * T)
    rstd_row, mr_row = emit_ln(b, env, x_in, xn, keep_rows=True)
    x3pre = env.x3pre
    e1 = env.salloc(F32, NDC, "l1_e1")
    ccol = env.salloc(F32, NDC, "l1_ccol")
    b.op("s", lambda E: E.activation(out=e1.ap(), in_=env.c["lam_cols"].ap(),
                                     func=AF.Exp),
         reads=[env.c["lam_cols"].k()], writes=[e1.k()])
    b.op("s", lambda E: E.activation(out=e1.ap(), in_=e1.ap(), func=AF.Ln,
                                     bias=env.c["ones_col"].ap()),
         reads=[e1.k(), env.c["ones_col"].k()], writes=[e1.k()])
    b.op("v", lambda E: E.tensor_scalar(out=ccol.ap(), in0=e1.ap(), scalar1=-8.0,
                                        scalar2=None, op0=ALU.mult),
         reads=[e1.k()], writes=[ccol.k()])
    wgb = [env.tv("big3", i * 4096, BF16, D) for i in range(2)]
    wib = [env.tv("big3", 8192 + i * 4096, BF16, D) for i in range(2)]
    xt = [env.tv("big3", 16384 + i * 4096, F32, T) for i in range(2)]
    stg = [env.tv("big5", i * 4096, F32, T) for i in range(2)]
    srg = env.tv("big4", 0, F32, T)
    sigt = env.tv("big4", 4096, F32, T)
    alph = env.tv("big4", 8192, F32, T)
    ub = env.tv("big4", 12288, F32, T)
    betb = env.tv("big4", 16384, F32, T)
    xnf = env.tv("big4", 20480, F32, T)
    bx = env.tv("big4", 24576, F32, T)
    xb2 = env.tv("big4", 0, F32, T)
    hl = env.tv("big4", 12288, F32, T)
    zeros = env.tv("big4", 28672, F32, T)
    Acum = env.tv("big1", 0, BF16, NDC * T)
    A_cols = env.salloc(F32, NDC, "l1_acore")
    H_cols = env.salloc(F32, NDC, "l1_hcore")
    b.op("v", lambda E: E.memset(zeros.ap(), 0.0), writes=[zeros.k()])
    rstdb = env.tv("big5", 8192, F32, T)
    mrb = env.tv("big5", 12288, F32, T)
    for tt in range(2):
        b.op("t", lambda E, tt=tt: E.matmul(
            out=psA.ap()[:, tt * 512:(tt + 1) * 512], lhsT=env.c["ones_row"]._ap,
            rhs=rstd_row._ap[:, tt * 512:(tt + 1) * 512], start=True, stop=True),
            reads=[rstd_row.k(tt * 512, (tt + 1) * 512), env.c["ones_row"].k()],
            writes=[PK("psA", tt * 512, (tt + 1) * 512)])
        b.op("t", lambda E, tt=tt: E.matmul(
            out=psA.ap()[:, 1024 + tt * 512:1024 + (tt + 1) * 512],
            lhsT=env.c["ones_row"]._ap,
            rhs=mr_row._ap[:, tt * 512:(tt + 1) * 512], start=True, stop=True),
            reads=[mr_row.k(tt * 512, (tt + 1) * 512), env.c["ones_row"].k()],
            writes=[PK("psA", 1024 + tt * 512, 1024 + (tt + 1) * 512)])
    b.op("v", lambda E: E.tensor_copy(out=rstdb.ap(), in_=psA.ap()[:, 0:1024]),
         reads=[PK("psA", 0, 1024)], writes=[rstdb.k()])
    b.op("v", lambda E: E.tensor_copy(out=mrb.ap(), in_=psA.ap()[:, 1024:2048]),
         reads=[PK("psA", 1024, 2048)], writes=[mrb.k()])
    for ec in range(NDC):
        wg_, wi_, x_ = wgb[ec % 2], wib[ec % 2], xt[ec % 2]
        w_load(b, env, "wgat", ec * 128, 128, wg_, NDC, n_split=1)
        w_load(b, env, "wint", ec * 128, 128, wi_, NDC, n_split=1)
        b.dma("y", lambda E, ec=ec, x_=x_: E.dma_start(
            out=x_.ap(), in_=x_in.ap()[ec * 128:(ec + 1) * 128, :]),
            reads=[DK(x_in.name, ec * 128 * T * 4, (ec + 1) * 128 * T * 4)],
            writes=[x_.k()])
        for tt in range(2):
            for dc in range(NDC):
                b.op("t", lambda E, tt=tt, dc=dc, wg_=wg_: E.matmul(
                    out=psA.ap()[:, tt * 512:(tt + 1) * 512],
                    lhsT=wg_.ap(dc * 128, (dc + 1) * 128),
                    rhs=xn.ap(dc * T + tt * 512, dc * T + (tt + 1) * 512),
                    start=(dc == 0), stop=(dc == NDC - 1)),
                    reads=[wg_.k(dc * 128, (dc + 1) * 128),
                           xn.k(dc * T + tt * 512, dc * T + (tt + 1) * 512)],
                    writes=[PK("psA", tt * 512, (tt + 1) * 512)])
            for dc in range(NDC):
                b.op("t", lambda E, tt=tt, dc=dc, wi_=wi_: E.matmul(
                    out=psB.ap()[:, tt * 512:(tt + 1) * 512],
                    lhsT=wi_.ap(dc * 128, (dc + 1) * 128),
                    rhs=xn.ap(dc * T + tt * 512, dc * T + (tt + 1) * 512),
                    start=(dc == 0), stop=(dc == NDC - 1)),
                    reads=[wi_.k(dc * 128, (dc + 1) * 128),
                           xn.k(dc * T + tt * 512, dc * T + (tt + 1) * 512)],
                    writes=[PK("psB", tt * 512, (tt + 1) * 512)])
        b.op("s", lambda E, ec=ec: E.activation(
            out=srg.ap(), in_=psA.ap()[:, 0:1024], func=AF.Sigmoid,
            bias=env.c["bgat_cols"].ap(ec, ec + 1)),
            reads=[PK("psA", 0, 1024), env.c["bgat_cols"].k(ec, ec + 1)],
            writes=[srg.k()])
        b.op("s", lambda E, ec=ec: E.activation(
            out=sigt.ap(), in_=psB.ap()[:, 0:1024], func=AF.Sigmoid,
            bias=env.c["bin_cols"].ap(ec, ec + 1)),
            reads=[PK("psB", 0, 1024), env.c["bin_cols"].k(ec, ec + 1)],
            writes=[sigt.k()])
        b.op("s", lambda E, ec=ec: E.activation(
            out=alph.ap(), in_=srg.ap(), func=AF.Exp, scale=ccol.ap(ec, ec + 1)),
            reads=[srg.k(), ccol.k(ec, ec + 1)], writes=[alph.k()])
        b.op("s", lambda E: E.activation(out=srg.ap(), in_=alph.ap(), func=AF.Square),
             reads=[alph.k()], writes=[srg.k()])
        b.op("s", lambda E: E.activation(out=ub.ap(), in_=srg.ap(), func=AF.Ln,
                                         scale=-1.0, bias=env.c["eps1p_col"].ap()),
             reads=[srg.k(), env.c["eps1p_col"].k()], writes=[ub.k()])
        b.op("s", lambda E: E.activation(out=betb.ap(), in_=ub.ap(), func=AF.Exp,
                                         scale=0.5),
             reads=[ub.k()], writes=[betb.k()])
        b.op("v", lambda E, x_=x_: E.tensor_tensor(out=xnf.ap(), in0=x_.ap(),
                                                   in1=rstdb.ap(), op=ALU.mult),
             reads=[x_.k(), rstdb.k()], writes=[xnf.k()])
        b.op("v", lambda E: E.tensor_tensor(out=xnf.ap(), in0=xnf.ap(),
                                            in1=mrb.ap(), op=ALU.subtract),
             reads=[xnf.k(), mrb.k()], writes=[xnf.k()])
        b.op("v", lambda E: E.tensor_tensor(out=bx.ap(), in0=betb.ap(), in1=xnf.ap(),
                                            op=ALU.mult),
             reads=[betb.k(), xnf.k()], writes=[bx.k()])
        b.op("v", lambda E: E.tensor_tensor(out=xb2.ap(), in0=bx.ap(), in1=sigt.ap(),
                                            op=ALU.mult),
             reads=[bx.k(), sigt.k()], writes=[xb2.k()])
        b.op("v", lambda E: E.tensor_tensor_scan(out=hl.ap(), data0=alph.ap(),
                                                 data1=xb2.ap(), initial=0.0,
                                                 op0=ALU.mult, op1=ALU.add),
             reads=[alph.k(), xb2.k()], writes=[hl.k()])
        st_ = stg[ec % 2]
        b.op("v", lambda E, x_=x_, st_=st_: E.tensor_tensor(
            out=st_.ap(), in0=x_.ap(), in1=hl.ap(), op=ALU.add),
            reads=[x_.k(), hl.k()], writes=[st_.k()])
        b.dma("y", lambda E, ec=ec, st_=st_: E.dma_start(
            out=x3pre.ap()[ec * 128:(ec + 1) * 128, :], in_=st_.ap()),
            reads=[st_.k()],
            writes=[DK("x3pre", ec * 128 * T * 4, (ec + 1) * 128 * T * 4)])
        b.op("v", lambda E: E.tensor_tensor_scan(
            out=bx.ap(), data0=alph.ap(), data1=zeros.ap(),
            initial=1.0, op0=ALU.mult, op1=ALU.add),
            reads=[alph.k(), zeros.k()], writes=[bx.k()])
        b.op("v", lambda E, ec=ec: E.tensor_copy(
            out=Acum.ap(ec * T, (ec + 1) * T), in_=bx.ap()),
            reads=[bx.k()], writes=[Acum.k(ec * T, (ec + 1) * T)])
        b.op("v", lambda E, ec=ec: E.tensor_copy(
            out=A_cols.ap(ec, ec + 1), in_=bx.ap(T - 1, T)),
            reads=[bx.k(T - 1, T)], writes=[A_cols.k(ec, ec + 1)])
        b.op("v", lambda E, ec=ec: E.tensor_copy(
            out=H_cols.ap(ec, ec + 1), in_=hl.ap(T - 1, T)),
            reads=[hl.k(T - 1, T)], writes=[H_cols.k(ec, ec + 1)])
    b.dma("p", lambda E: E.dma_start(out=env.dr["ag2_in"].ap()[:, 0:16],
                                     in_=A_cols.ap()),
          reads=[A_cols.k()], writes=[DK("ag2_in", 0, 64)])
    b.dma("p", lambda E: E.dma_start(out=env.dr["ag2_in"].ap()[:, 16:32],
                                     in_=H_cols.ap()),
          reads=[H_cols.k()], writes=[DK("ag2_in", 64, 128)])
    b.op("p", lambda E: E.collective_compute(
        "AllGather", ALU.bypass, replica_groups=[list(range(NCORE))],
        ins=[env.dr["ag2_in"].ap()], outs=[env.dr["ag2_out"].ap()]),
        reads=[DK("ag2_in")], writes=[DK("ag2_out")])
    agb = env.salloc(F32, 256, "l1_agb")
    b.dma("y", lambda E: E.dma_start(
        out=agb.ap().rearrange("p (r c) -> p r c", r=NCORE),
        in_=env.dr["ag2_out"].ap().rearrange("(r p) c -> p r c", p=128)),
        reads=[DK("ag2_out")], writes=[agb.k()])
    gcol = env.salloc(F32, NDC, "l1_hinit")
    aeff = env.salloc(F32, NDC, "l1_aeff")
    heff = env.salloc(F32, NDC, "l1_heff")
    b.op("v", lambda E: E.memset(gcol.ap(), 0.0), writes=[gcol.k()])
    for j in range(NCORE):
        b.op("v", lambda E, j=j: E.tensor_scalar(
            out=aeff.ap(), in0=agb.ap(j * 32, j * 32 + 16),
            scalar1=env.c["mcol"].ap(j, j + 1), scalar2=env.c["onem"].ap(j, j + 1),
            op0=ALU.mult, op1=ALU.add),
            reads=[agb.k(j * 32, j * 32 + 16), env.c["mcol"].k(j, j + 1),
                   env.c["onem"].k(j, j + 1)],
            writes=[aeff.k()])
        b.op("v", lambda E, j=j: E.tensor_scalar(
            out=heff.ap(), in0=agb.ap(j * 32 + 16, j * 32 + 32),
            scalar1=env.c["mcol"].ap(j, j + 1), scalar2=None, op0=ALU.mult),
            reads=[agb.k(j * 32 + 16, j * 32 + 32), env.c["mcol"].k(j, j + 1)],
            writes=[heff.k()])
        for ec in range(NDC):
            b.op("v", lambda E, ec=ec: E.scalar_tensor_tensor(
                out=gcol.ap(ec, ec + 1), in0=gcol.ap(ec, ec + 1),
                scalar=aeff.ap(ec, ec + 1), in1=heff.ap(ec, ec + 1),
                op0=ALU.mult, op1=ALU.add),
                reads=[gcol.k(ec, ec + 1), aeff.k(ec, ec + 1), heff.k(ec, ec + 1)],
                writes=[gcol.k(ec, ec + 1)])
    mk(b, "l1a:corr")
    for ec in range(NDC):
        x_ = xt[ec % 2]
        st_ = stg[ec % 2]
        b.dma("y", lambda E, ec=ec, x_=x_: E.dma_start(
            out=x_.ap(), in_=x3pre.ap()[ec * 128:(ec + 1) * 128, :]),
            reads=[DK("x3pre", ec * 128 * T * 4, (ec + 1) * 128 * T * 4)],
            writes=[x_.k()])
        b.op("v", lambda E, ec=ec, x_=x_, st_=st_: E.scalar_tensor_tensor(
            out=st_.ap(), in0=Acum.ap(ec * T, (ec + 1) * T),
            scalar=gcol.ap(ec, ec + 1), in1=x_.ap(), op0=ALU.mult, op1=ALU.add),
            reads=[Acum.k(ec * T, (ec + 1) * T), gcol.k(ec, ec + 1), x_.k()],
            writes=[st_.k()])
        b.dma("y", lambda E, ec=ec, st_=st_: E.dma_start(
            out=x_out.ap()[ec * 128:(ec + 1) * 128, :], in_=st_.ap()),
            reads=[st_.k()],
            writes=[DK(x_out.name, ec * 128 * T * 4, (ec + 1) * 128 * T * 4)])


def emit_l1f(b, env, x_in, x_out):
    mk(b, "l1f:start")
    psA, psB = env.psA, env.psB
    xn = env.tv("big2", 0, BF16, NDC * T)
    emit_ln(b, env, x_in, xn)
    fuse = env.tv("big1", 0, BF16, NOC * 512)
    wgb = [env.tv("big3", i * 4096, BF16, D) for i in range(2)]
    wub = [env.tv("big3", 8192 + i * 4096, BF16, D) for i in range(2)]
    wdb = [env.tv("big3", i * 16384, BF16, NOC * 128) for i in range(2)]
    sgt = [env.tv("big5", i * 2048, F32, 512) for i in range(2)]
    ut = [env.tv("big5", 4096 + i * 2048, F32, 512) for i in range(2)]
    xt = [env.tv("big5", 8192 + i * 2048, F32, 512) for i in range(2)]
    ostg = [env.tv("big5", 12288 + i * 2048, F32, 512) for i in range(2)]
    for th in range(2):
        for oc in range(NOC):
            wg_, wu_ = wgb[oc % 2], wub[oc % 2]
            w_load(b, env, "wg8", oc * 128, 128, wg_, NDC, n_split=1)
            w_load(b, env, "wu8", oc * 128, 128, wu_, NDC, n_split=1)
            for dc in range(NDC):
                b.op("t", lambda E, th=th, dc=dc, oc=oc, wg_=wg_: E.matmul(
                    out=psA.ap()[:, (oc % 2) * 512:(oc % 2) * 512 + 512],
                    lhsT=wg_.ap(dc * 128, (dc + 1) * 128),
                    rhs=xn.ap(dc * T + th * 512, dc * T + (th + 1) * 512),
                    start=(dc == 0), stop=(dc == NDC - 1)),
                    reads=[wg_.k(dc * 128, (dc + 1) * 128),
                           xn.k(dc * T + th * 512, dc * T + (th + 1) * 512)],
                    writes=[PK("psA", (oc % 2) * 512, (oc % 2) * 512 + 512)])
            b.op("s", lambda E, oc=oc: E.activation(
                out=sgt[oc % 2].ap(), in_=psA.ap()[:, (oc % 2) * 512:(oc % 2) * 512 + 512],
                func=AF.Silu, scale=env.c["sg_cols"].ap(oc, oc + 1)),
                reads=[PK("psA", (oc % 2) * 512, (oc % 2) * 512 + 512),
                       env.c["sg_cols"].k(oc, oc + 1)],
                writes=[sgt[oc % 2].k()])
            for dc in range(NDC):
                b.op("t", lambda E, th=th, dc=dc, oc=oc, wu_=wu_: E.matmul(
                    out=psB.ap()[:, (oc % 2) * 512:(oc % 2) * 512 + 512],
                    lhsT=wu_.ap(dc * 128, (dc + 1) * 128),
                    rhs=xn.ap(dc * T + th * 512, dc * T + (th + 1) * 512),
                    start=(dc == 0), stop=(dc == NDC - 1)),
                    reads=[wu_.k(dc * 128, (dc + 1) * 128),
                           xn.k(dc * T + th * 512, dc * T + (th + 1) * 512)],
                    writes=[PK("psB", (oc % 2) * 512, (oc % 2) * 512 + 512)])
            b.op("v", lambda E, oc=oc: E.tensor_scalar(
                out=ut[oc % 2].ap(), in0=psB.ap()[:, (oc % 2) * 512:(oc % 2) * 512 + 512],
                scalar1=env.c["su_cols"].ap(oc, oc + 1), scalar2=None, op0=ALU.mult),
                reads=[PK("psB", (oc % 2) * 512, (oc % 2) * 512 + 512),
                       env.c["su_cols"].k(oc, oc + 1)],
                writes=[ut[oc % 2].k()])
            b.op("v", lambda E, oc=oc: E.tensor_tensor(
                out=fuse.ap(oc * 512, (oc + 1) * 512), in0=sgt[oc % 2].ap(),
                in1=ut[oc % 2].ap(), op=ALU.mult),
                reads=[sgt[oc % 2].k(), ut[oc % 2].k()],
                writes=[fuse.k(oc * 512, (oc + 1) * 512)])
        for dco in range(NDC):
            wd_ = wdb[dco % 2]
            w_load(b, env, "wd8", dco * 128, 128, wd_, NOC, n_split=1)
            for oc in range(NOC):
                b.op("t", lambda E, dco=dco, oc=oc, wd_=wd_: E.matmul(
                    out=psA.ap()[:, (dco % 4) * 512:(dco % 4) * 512 + 512],
                    lhsT=wd_.ap(oc * 128, (oc + 1) * 128),
                    rhs=fuse.ap(oc * 512, (oc + 1) * 512),
                    start=(oc == 0), stop=(oc == NOC - 1)),
                    reads=[wd_.k(oc * 128, (oc + 1) * 128),
                           fuse.k(oc * 512, (oc + 1) * 512)],
                    writes=[PK("psA", (dco % 4) * 512, (dco % 4) * 512 + 512)])
            x_ = xt[dco % 2]
            o_ = ostg[dco % 2]
            b.dma("y", lambda E, dco=dco, th=th, x_=x_: E.dma_start(
                out=x_.ap(),
                in_=x_in.ap()[dco * 128:(dco + 1) * 128, th * 512:(th + 1) * 512]),
                reads=[DK(x_in.name, dco * 128 * T * 4, (dco + 1) * 128 * T * 4)],
                writes=[x_.k()])
            b.op("v", lambda E, dco=dco, x_=x_, o_=o_: E.scalar_tensor_tensor(
                out=o_.ap(), in0=psA.ap()[:, (dco % 4) * 512:(dco % 4) * 512 + 512],
                scalar=env.c["sd_cols"].ap(dco, dco + 1), in1=x_.ap(),
                op0=ALU.mult, op1=ALU.add),
                reads=[PK("psA", (dco % 4) * 512, (dco % 4) * 512 + 512),
                       env.c["sd_cols"].k(dco, dco + 1), x_.k()],
                writes=[o_.k()])
            b.dma("y", lambda E, dco=dco, th=th, o_=o_: E.dma_start(
                out=x_out.ap()[dco * 128:(dco + 1) * 128, th * 512:(th + 1) * 512],
                in_=o_.ap()),
                reads=[o_.k()],
                writes=[DK(x_out.name, dco * 128 * T * 4, (dco + 1) * 128 * T * 4)])


L0A_STAGE = 3

EMIT = {"l0a": emit_l0a, "l0f": emit_l0f, "l1a": emit_l1a, "l1f": emit_l1f}
GROUPS = [["l0a", "l0f", "l1a", "l1f"]]

_prog_cache = {}


def build_program(blocks):
    key = tuple(blocks)
    if key in _prog_cache:
        return _prog_cache[key]
    nc = bass.Bass("TRN2", target_bir_lowering=False)
    env = make_env(nc, blocks, len(blocks) == 4)
    b = B(nc)
    load_small_consts(b, env)
    if "l0f" in blocks:
        from collections import deque
        env.gpump = deque()
        env.gsize = {}
        emit_gamma_stream(b, env, "w1t", "th1")
        emit_gamma_stream(b, env, "w2t", "th2")
        emit_gamma_stream(b, env, "w3t", "th3")
    else:
        from collections import deque
        env.gpump = deque()
    for i, blk in enumerate(blocks):
        EMIT[blk](b, env, env.xbuf[i], env.xbuf[i + 1])
    b.finish()
    b.build()
    _prog_cache[key] = (nc, env)
    return nc, env


NEED = {
    "common": ["permpm", "permpm_bf", "idbf", "ones_col", "ones_col_bf", "ones_row",
               "nh_row", "nh_row_bf", "ccr", "eps1p_col"],
    "l0a": ["cosf", "sinf", "dt_tab", "dq_tab", "dk_tab", "lamc_tab", "coefc",
            "wq", "wk", "wv", "wo", "wg1", "wg2", "nw_cols"],
    "l0f": ["w1t", "w2t", "w3t"],
    "l1a": ["wint", "wgat", "bin_cols", "bgat_cols", "lam_cols", "mcol", "onem"],
    "l1f": ["wg8", "wu8", "wd8", "sg_cols", "su_cols", "sd_cols"],
}
PER_CORE = {"cosf", "sinf", "coefc", "mcol", "onem"}


def _prep(inputs):
    f32 = np.float32
    g = {}
    g["wq"] = np.ascontiguousarray(np.asarray(inputs["L0_attn_Wq"], f32))
    g["wk"] = np.ascontiguousarray(np.asarray(inputs["L0_attn_Wk"], f32))
    g["wv"] = np.ascontiguousarray(np.asarray(inputs["L0_attn_Wv"], f32))
    g["wo"] = np.ascontiguousarray(np.asarray(inputs["L0_attn_Wo"], f32))
    g["wg1"] = np.ascontiguousarray(np.asarray(inputs["L0_attn_Wg1"], f32))
    g["wg2"] = np.ascontiguousarray(np.asarray(inputs["L0_attn_Wg2"], f32))
    g["nw_cols"] = _cols(np.asarray(inputs["L0_attn_norm_weight"], f32))
    g["w1t"] = np.ascontiguousarray(np.asarray(inputs["L0_ffn_w1"], f32).T)
    g["w2t"] = np.ascontiguousarray(np.asarray(inputs["L0_ffn_w2"], f32).T)
    g["w3t"] = np.ascontiguousarray(np.asarray(inputs["L0_ffn_w3"], f32).T)
    g["wint"] = np.ascontiguousarray(np.asarray(inputs["L1_attn_W_in"], f32).T)
    g["wgat"] = np.ascontiguousarray(np.asarray(inputs["L1_attn_W_gate"], f32).T)
    g["bin_cols"] = _cols(np.asarray(inputs["L1_attn_b_in"], f32))
    g["bgat_cols"] = _cols(np.asarray(inputs["L1_attn_b_gate"], f32))
    g["lam_cols"] = _cols(np.asarray(inputs["L1_attn_forget_lambda"], f32))
    g["wg8"] = np.ascontiguousarray(np.asarray(inputs["L1_ffn_w_gate"], np.int8))
    g["wu8"] = np.ascontiguousarray(np.asarray(inputs["L1_ffn_w_up"], np.int8))
    g["wd8"] = np.ascontiguousarray(np.asarray(inputs["L1_ffn_w_down"], np.int8))
    g["sg_cols"] = _cols(np.asarray(inputs["L1_ffn_s_gate"], f32))
    g["su_cols"] = _cols(np.asarray(inputs["L1_ffn_s_up"], f32))
    g["sd_cols"] = _cols(np.asarray(inputs["L1_ffn_s_down"], f32))
    g["ones_col"] = np.ones((128, 1), f32)
    g["ones_col_bf"] = np.ones((128, 1), ml_dtypes.bfloat16)
    g["ones_row"] = np.ones((1, 128), f32)
    g["nh_row"] = np.full((1, 128), -0.5, f32)
    g["nh_row_bf"] = np.full((1, 128), -0.5, ml_dtypes.bfloat16)
    g["permpm_bf"] = None
    ccr = np.zeros((1, 8), f32)
    ccr[0, 0] = 1e-5
    ccr[0, 1] = 0.5e-8
    g["ccr"] = ccr
    g["eps1p_col"] = np.full((128, 1), f32(1.0) + f32(1e-6), f32)
    x = np.asarray(inputs["L0_attn_x"], f32)[0]
    percore = []
    for c in range(NCORE):
        hc = _host_const(c)
        d = {"cosf": hc["cosf"], "sinf": hc["sinf"], "coefc": hc["coefc"],
             "mcol": hc["mcol"], "onem": hc["onem"]}
        d["x_fm"] = np.ascontiguousarray(x[c * T:(c + 1) * T, :].T)
        percore.append(d)
    g["permpm"] = hc["permpm"]
    g["permpm_bf"] = hc["permpm"].astype(ml_dtypes.bfloat16)
    g["idbf"] = hc["idbf"]
    g["dt_tab"] = hc["dt_tab"]
    g["dq_tab"] = hc["dq_tab"]
    g["dk_tab"] = hc["dk_tab"]
    g["lamc_tab"] = hc["lamc_tab"]
    return g, percore


def kernel(**inputs):
    g, percore = _prep(inputs)
    x_cur = [percore[c]["x_fm"] for c in range(NCORE)]
    for blocks in GROUPS:
        nc, env = build_program(blocks)
        names = list(NEED["common"])
        for blk in blocks:
            names += NEED[blk]
        in_maps = []
        for c in range(NCORE):
            m = {}
            for nm in names:
                m[nm] = percore[c][nm] if nm in PER_CORE else g[nm]
            m["x_fm"] = x_cur[c]
            in_maps.append(m)
        res = run_bass_kernel_spmd(nc, in_maps, core_ids=list(range(NCORE)))
        x_cur = [np.asarray(res.results[c]["y_fm"]) for c in range(NCORE)]
    out = np.empty((1, SEQ, D), np.float32)
    for c in range(NCORE):
        out[0, c * T:(c + 1) * T, :] = x_cur[c].T
    return out
